# version 20
# baseline (speedup 1.0000x reference)
import sys
try:
    import concourse.bass
except ImportError:
    for _p in ("/opt/trn_rl_repo",):
        if _p not in sys.path:
            sys.path.insert(0, _p)

from contextlib import ExitStack
import math
import numpy as np
import ml_dtypes

import concourse.bass as bass
import concourse.mybir as mybir
from concourse.bass_utils import run_bass_kernel_spmd

F32 = mybir.dt.float32
BF16 = mybir.dt.bfloat16
U8 = mybir.dt.uint8
I32 = mybir.dt.int32
AF = mybir.ActivationFunctionType
ALU = mybir.AluOpType
AX = mybir.AxisListType

D = 4096
TPC = 1024
NCH = D // 128
TB = 512
NTB = TPC // TB
INTER = 11008
NIT = INTER // 128
NCORES = 8
EPS = 1e-5
HEADS = 32
KVHEADS = 8
HDIM = 128
KVDIM = KVHEADS * HDIM
SEQ = 4096
NKC = SEQ // 128
NKB = SEQ // 512


def perm_eo(n):
    p = np.arange(n).reshape(n // 128, 128)
    return np.concatenate([p[:, 0::2], p[:, 1::2]], axis=1).reshape(-1)

PERM = perm_eo(D)
PERM2 = perm_eo(INTER)
IPERM = np.argsort(PERM)



N_LD = 10
N_ST = 4


class KB:

    def __init__(self, nc, ctx):
        self.nc = nc
        self.ctx = ctx
        self.block = ctx.enter_context(nc.Block())
        chans = ["PE", "DVE", "ACT", "POOL", "CC", "CC2"]
        chans += ["LD%d" % i for i in range(N_LD)]
        chans += ["ST%d" % i for i in range(N_ST)]
        self.sem = {c: ctx.enter_context(nc.semaphore(c)) for c in chans}
        self.cnt = {c: 0 for c in chans}
        self.uid = 0

    def uname(self, base):
        self.uid += 1
        return "%s_u%d" % (base, self.uid)

    def inc(self, inst, ch, n=1):
        inst.then_inc(self.sem[ch], n)
        self.cnt[ch] += n
        return self.cnt[ch]

    def wait(self, eng, ch, val):
        if val > 0:
            eng.wait_ge(self.sem[ch], val)

    def sw(self, eng, ch):
        self.wait(eng, ch, self.cnt[ch])

    def barrier(self):
        incs = {"LD0": "sync", "PE": "tensor", "DVE": "vector",
                "ACT": "scalar"}
        targets = {ch: v for ch, v in self.cnt.items()
                   if v > 0 and ch not in ("CC", "CC2", "POOL")}
        for ch in incs:
            targets[ch] = self.cnt[ch] + 1

        def mk(ch):
            def f(e):
                self.wait(e, ch, targets[ch] - 1)
                self.inc(e.nop(), ch, 1)
                for och in sorted(targets):
                    if och != ch:
                        self.wait(e, och, targets[och])
            return f
        b = self.block
        b.sync(mk("LD0"))
        b.tensor(mk("PE"))
        b.vector(mk("DVE"))
        b.scalar(mk("ACT"))

    def sb(self, name, shape, dtype):
        t = self.ctx.enter_context(self.nc.sbuf_tensor(name, shape, dtype))
        return t

    def psum(self, name, shape, dtype=F32):
        return self.ctx.enter_context(self.nc.psum_tensor(name, shape, dtype))



def phase_ln(kb, bufs, x_ap):
    nc, b = kb.nc, kb.block
    ln = bufs["ln"]
    ones_bf = bufs["ones_bf"]
    ones_f1 = bufs["ones_f1"]

    with ExitStack() as ps_ctx:
        pp = lambda n, s, d=F32: ps_ctx.enter_context(nc.psum_tensor(kb.uname(n), s, d))
        sbl = lambda n, s, d: ps_ctx.enter_context(nc.sbuf_tensor(kb.uname(n), s, d))
        xt = [sbl("ln_xt", [128, TPC], F32) for _ in range(2)]
        xbf = [sbl("ln_xbf", [128, TPC], BF16) for _ in range(2)]
        x2bf = [sbl("ln_x2bf", [128, TPC], BF16) for _ in range(2)]
        t1 = sbl("ln_t1", [128, TB], F32)
        mu_b = sbl("ln_mub", [128, TB], F32)
        tmp_b = sbl("ln_tmpb", [128, TB], F32)
        t2_b = sbl("ln_t2b", [128, TB], F32)
        A_t = [sbl("ln_At", [128, TB], F32) for _ in range(2)]
        B_t = [sbl("ln_Bt", [128, TB], F32) for _ in range(2)]
        ps_sx = [pp("ln_sx%d" % t, [1, 512]) for t in range(2)]
        ps_sxx = [pp("ln_sxx%d" % t, [1, 512]) for t in range(2)]
        ps_bc = [pp("ln_bc%d" % t, [128, 512]) for t in range(2)]

        base = dict(kb.cnt)
        LDS = ("LD1", "LD2")

        def sync1(s):
            for i in range(NCH):
                if i >= 2:
                    kb.wait(s, "DVE", base["DVE"] + (i - 1))
                    kb.wait(s, "ACT", base["ACT"] + (i - 1))
                kb.inc(s.dma_start(out=xt[i % 2][:, :], in_=x_ap[i * 128:(i + 1) * 128, :]), LDS[i % 2], 16)
        b.sync(sync1)

        def vec1(v):
            for i in range(NCH):
                kb.wait(v, LDS[i % 2], base[LDS[i % 2]] + 16 * (i // 2 + 1))
                if i >= 2:
                    kb.wait(v, "PE", base["PE"] + (i - 1))
                kb.inc(v.tensor_copy(out=xbf[i % 2][:, :], in_=xt[i % 2][:, :]), "DVE")
        b.vector(vec1)

        def act1(a):
            for i in range(NCH):
                kb.wait(a, LDS[i % 2], base[LDS[i % 2]] + 16 * (i // 2 + 1))
                if i >= 2:
                    kb.wait(a, "PE", base["PE"] + (i - 1))
                kb.inc(a.activation(x2bf[i % 2][:, :], xt[i % 2][:, :], AF.Square), "ACT")
        b.scalar(act1)

        def pe1(t):
            for i in range(NCH):
                kb.wait(t, "DVE", base["DVE"] + (i + 1))
                kb.wait(t, "ACT", base["ACT"] + (i + 1))
                st = (i == 0)
                sp = (i == NCH - 1)
                for tb in range(2):
                    sl = slice(tb * TB, (tb + 1) * TB)
                    t.matmul(ps_sx[tb][:, :], ones_bf[:, :], xbf[i % 2][:, sl], start=st, stop=sp)
                    inst = t.matmul(ps_sxx[tb][:, :], ones_bf[:, :], x2bf[i % 2][:, sl], start=st, stop=sp)
                kb.inc(inst, "PE")
        b.tensor(pe1)

        base2 = dict(kb.cnt)

        def vec2a(v):
            kb.wait(v, "PE", base2["PE"])
            for tb in range(2):
                kb.inc(v.tensor_copy(out=A_t[tb][0:1, :], in_=ps_sx[tb][:, :]), "DVE")
                kb.inc(v.tensor_copy(out=B_t[tb][0:1, :], in_=ps_sxx[tb][:, :]), "DVE")
        b.vector(vec2a)

        base3 = dict(kb.cnt)

        def pe2(t):
            kb.wait(t, "DVE", base3["DVE"])
            for tb in range(2):
                if tb == 1:
                    kb.wait(t, "DVE", base3["DVE"] + 2)
                t.matmul(ps_bc[0][:, :], ones_f1[:, :], A_t[tb][0:1, :], start=True, stop=True)
                inst = t.matmul(ps_bc[1][:, :], ones_f1[:, :], B_t[tb][0:1, :], start=True, stop=True)
                kb.inc(inst, "PE")
        b.tensor(pe2)

        def vec2b(v):
            for tb in range(2):
                kb.wait(v, "PE", base3["PE"] + tb + 1)
                if tb == 1:
                    kb.sw(v, "DVE")
                kb.inc(v.tensor_scalar(mu_b[:, :], ps_bc[0][:, :], 1.0 / D, None, ALU.mult), "DVE")
                kb.inc(v.tensor_scalar(tmp_b[:, :], ps_bc[1][:, :], 1.0 / D, None, ALU.mult), "DVE")
                kb.sw(v, "DVE")
                kb.inc(v.tensor_tensor(t2_b[:, :], mu_b[:, :], mu_b[:, :], ALU.mult), "DVE")
                kb.sw(v, "DVE")
                kb.inc(v.tensor_tensor(t2_b[:, :], tmp_b[:, :], t2_b[:, :], ALU.subtract), "DVE")
                kb.sw(v, "DVE")
                kb.inc(v.tensor_scalar(t2_b[:, :], t2_b[:, :], EPS, None, ALU.add), "DVE")
                kb.wait(v, "ACT", base3["ACT"] + tb + 1)
                kb.inc(v.reciprocal(A_t[tb][:, :], tmp_b[:, :]), "DVE")
                kb.sw(v, "DVE")
                kb.inc(v.tensor_tensor(B_t[tb][:, :], mu_b[:, :], A_t[tb][:, :], ALU.mult), "DVE")
                kb.sw(v, "DVE")
                kb.inc(v.tensor_scalar(B_t[tb][:, :], B_t[tb][:, :], -1.0, None, ALU.mult), "DVE")
        b.vector(vec2b)

        def act2(a):
            for tb in range(2):
                kb.wait(a, "DVE", base3["DVE"] + 8 * tb + 5)
                kb.inc(a.activation(tmp_b[:, :], t2_b[:, :], AF.Sqrt), "ACT")
        b.scalar(act2)

        base4 = dict(kb.cnt)

        def sync2(s):
            kb.wait(s, "DVE", base4["DVE"])
            for i in range(NCH):
                if i >= 2:
                    kb.wait(s, "DVE", base4["DVE"] + 4 * (i - 1))
                kb.inc(s.dma_start(out=xt[i % 2][:, :], in_=x_ap[i * 128:(i + 1) * 128, :]), LDS[i % 2], 16)
        b.sync(sync2)

        def vec3(v):
            for i in range(NCH):
                kb.wait(v, LDS[i % 2], base4[LDS[i % 2]] + 16 * (i // 2 + 1))
                for tb in range(2):
                    sl = slice(tb * TB, (tb + 1) * TB)
                    kb.sw(v, "DVE")
                    kb.inc(v.tensor_tensor(t1[:, 0:512], xt[i % 2][:, sl], A_t[tb][:, :], ALU.mult), "DVE")
                    kb.sw(v, "DVE")
                    kb.inc(v.tensor_tensor(ln[i][:, sl], t1[:, 0:512], B_t[tb][:, :], ALU.add), "DVE")
        b.vector(vec3)

        kb.barrier()


def phase_moe(kb, bufs, wt, x_prev_ap, x_next_ap):
    nc, b = kb.nc, kb.block
    ln = bufs["ln"]
    E16 = bufs["E16_f"]

    moesc = ExitStack()
    kb.ctx.enter_context(moesc)
    sbl = lambda n, s, d: moesc.enter_context(nc.sbuf_tensor(kb.uname(n), s, d))
    rw_f = sbl("moe_rwf", [128, 512], F32)
    rwbf = sbl("moe_rwbf", [128, 256], BF16)
    a_f = sbl("moe_af", [128, TPC], F32)
    a_bf = sbl("moe_abf", [128, NCH * 128], BF16)
    bw_bf = sbl("moe_bwbf", [128, NCH * 128], BF16)
    lg_sb = sbl("moe_lg", [128, 64], F32)
    msk_sb = sbl("moe_msk", [128, 8], F32)
    m_sb = sbl("moe_m", [128, 8], F32)
    rwtm = sbl("moe_rwtm", [128, 64], F32)
    rw8 = sbl("moe_rw8", [8, 1024], F32)
    rwx_sb = [sbl("moe_rwx%d" % t, [128, 512], F32) for t in range(2)]
    wh_sb = [sbl("moe_wh%d" % t, [128, 512], BF16) for t in range(2)]
    xr = [sbl("moe_xr%d" % i, [128, 512], F32) for i in range(2)]
    xn = [sbl("moe_xn%d" % i, [128, 512], F32) for i in range(2)]

    base = dict(kb.cnt)

    def syncw(s):
        kb.inc(s.dma_start(out=rw_f[:, 0:256].rearrange("p (ch e) -> p ch e", ch=NCH),
                           in_=wt["route_wT"].rearrange("(ch p) e -> p ch e", p=128)), "LD1", 16)
    b.sync(syncw)

    def vecw(v):
        kb.wait(v, "LD1", base["LD1"] + 16)
        kb.inc(v.tensor_copy(out=rwbf[:, :], in_=rw_f[:, 0:256]), "DVE")
    b.vector(vecw)

    loads = []
    for q in range(4):
        loads.append((wt["A_lhsT"][q * 1024:(q + 1) * 1024, :].rearrange("(c p) h -> p c h", p=128),
                      a_bf[:, q * 1024:(q + 1) * 1024], True))
    for q in range(4):
        loads.append((wt["Bw_lhsT"][:, q * 1024:(q + 1) * 1024], bw_bf[:, q * 1024:(q + 1) * 1024], False))
    for q, (src, dst, is3d) in enumerate(loads):
        basep = dict(kb.cnt)

        def syncq(s, src=src, is3d=is3d, basep=basep):
            kb.wait(s, "DVE", basep["DVE"])
            out_ap = a_f[:, :].rearrange("p (c h) -> p c h", c=8) if is3d else a_f[:, :]
            kb.inc(s.dma_start(out=out_ap, in_=src), "LD2", 16)
        b.sync(syncq)

        def vecq(v, dst=dst, basep=basep):
            kb.wait(v, "LD2", basep["LD2"] + 16)
            kb.inc(v.tensor_copy(out=dst, in_=a_f[:, :]), "DVE")
        b.vector(vecq)

    with ExitStack() as sc1:
        ps_l = sc1.enter_context(nc.psum_tensor(kb.uname("moe_psl"), [128, 8], F32))
        ps_rw8 = sc1.enter_context(nc.psum_tensor(kb.uname("moe_psrw8"), [8, 1024], F32))
        base3 = dict(kb.cnt)

        def pe_l(t):
            kb.wait(t, "DVE", base3["DVE"])
            for tt in range(8):
                if tt > 0:
                    kb.wait(t, "DVE", base3["DVE"] + 4 * (tt - 1) + 1)
                for ch in range(NCH):
                    inst = t.matmul(ps_l[:, :], ln[ch][:, tt * 128:(tt + 1) * 128],
                                    rwbf[:, ch * 8:(ch + 1) * 8],
                                    start=(ch == 0), stop=(ch == NCH - 1))
                kb.inc(inst, "PE")
        b.tensor(pe_l)

        def vec_l(v):
            for tt in range(8):
                kb.wait(v, "PE", base3["PE"] + tt + 1)
                kb.inc(v.tensor_copy(out=lg_sb[:, tt * 8:(tt + 1) * 8], in_=ps_l[:, :]), "DVE")
                kb.sw(v, "DVE")
                kb.inc(v.reduce_max(m_sb[:, tt:tt + 1], lg_sb[:, tt * 8:(tt + 1) * 8], axis=AX.X), "DVE")
                kb.sw(v, "DVE")
                kb.inc(v.tensor_scalar(msk_sb[:, :], lg_sb[:, tt * 8:(tt + 1) * 8],
                                       m_sb[:, tt:tt + 1], None, ALU.is_equal), "DVE")
                kb.sw(v, "DVE")
                kb.inc(v.tensor_scalar(rwtm[:, tt * 8:(tt + 1) * 8], msk_sb[:, :],
                                       m_sb[:, tt:tt + 1], None, ALU.mult), "DVE")
        b.vector(vec_l)

        base4 = dict(kb.cnt)

        def pe_t(t):
            kb.wait(t, "DVE", base4["DVE"])
            for tt in range(8):
                inst = t.transpose(ps_rw8[0:8, tt * 128:(tt + 1) * 128],
                                   rwtm[:, tt * 8:(tt + 1) * 8], bufs["ident_f"][:, :])
            kb.inc(inst, "PE")
        b.tensor(pe_t)

        def vec_t(v):
            kb.wait(v, "PE", base4["PE"] + 1)
            kb.inc(v.tensor_copy(out=rw8[:, :], in_=ps_rw8[:, :]), "DVE")
        b.vector(vec_t)

    join = dict(kb.cnt)
    with ExitStack() as sc2:
        pp = lambda n, s: sc2.enter_context(nc.psum_tensor(kb.uname(n), s, F32))
        ps_rwx = [pp("moe_rwx_ps%d" % t, [128, 512]) for t in range(2)]
        ps_h = [pp("moe_h_ps%d" % t, [128, 512]) for t in range(2)]
        ps_o = [pp("moe_o_ps%d" % t, [128, 512]) for t in range(2)]

        def pe_hx(t):
            kb.wait(t, "DVE", join["DVE"])
            for tb in range(2):
                t.matmul(ps_rwx[tb][:, :], E16[:, :], rw8[0:8, tb * 512:(tb + 1) * 512],
                         start=True, stop=True)
                for ch in range(NCH):
                    inst = t.matmul(ps_h[tb][:, :], a_bf[:, ch * 128:(ch + 1) * 128],
                                    ln[ch][:, tb * 512:(tb + 1) * 512],
                                    start=(ch == 0), stop=(ch == NCH - 1))
                kb.inc(inst, "PE")
        b.tensor(pe_hx)

        def vec_hx(v):
            for tb in range(2):
                kb.wait(v, "PE", join["PE"] + tb + 1)
                kb.inc(v.tensor_copy(out=rwx_sb[tb][:, :], in_=ps_rwx[tb][:, :]), "DVE")
                kb.sw(v, "DVE")
                kb.inc(v.tensor_tensor(wh_sb[tb][:, :], ps_h[tb][:, :], rwx_sb[tb][:, :], ALU.mult), "DVE")
        b.vector(vec_hx)

        base5 = dict(kb.cnt)
        RLD = ("LD3", "LD4")
        RST = ("ST0", "ST1")

        def sync_r(s):
            k = 0
            for ot in range(NCH):
                for tb in range(2):
                    if k >= 2:
                        kb.wait(s, "DVE", base5["DVE"] + (k - 1))
                    kb.inc(s.dma_start(out=xr[k % 2][:, :],
                                       in_=x_prev_ap[ot * 128:(ot + 1) * 128, tb * 512:(tb + 1) * 512]),
                           RLD[k % 2], 16)
                    k += 1
        b.sync(sync_r)

        def pe_o(t):
            kb.wait(t, "DVE", base5["DVE"])
            k = 0
            for ot in range(NCH):
                for tb in range(2):
                    if k >= 2:
                        kb.wait(t, "DVE", base5["DVE"] + (k - 1))
                    inst = t.matmul(ps_o[k % 2][:, :], bw_bf[:, ot * 128:(ot + 1) * 128],
                                    wh_sb[tb][:, :], start=True, stop=True)
                    kb.inc(inst, "PE")
                    k += 1
        b.tensor(pe_o)

        def vec_r(v):
            k = 0
            for ot in range(NCH):
                for tb in range(2):
                    kb.wait(v, "PE", base5["PE"] + k + 1)
                    kb.wait(v, RLD[k % 2], base5[RLD[k % 2]] + 16 * (k // 2 + 1))
                    if k >= 2:
                        kb.wait(v, RST[k % 2], base5[RST[k % 2]] + 16 * (k // 2))
                    kb.inc(v.scalar_tensor_tensor(xn[k % 2][:, :], ps_o[k % 2][:, :], 0.5,
                                                  xr[k % 2][:, :], ALU.mult, ALU.add), "DVE")
                    k += 1
        b.vector(vec_r)

        def st_r(g):
            k = 0
            for ot in range(NCH):
                for tb in range(2):
                    kb.wait(g, "DVE", base5["DVE"] + k + 1)
                    kb.inc(g.dma_start(out=x_next_ap[ot * 128:(ot + 1) * 128, tb * 512:(tb + 1) * 512],
                                       in_=xn[k % 2][:, :]), RST[k % 2], 16)
                    k += 1
        b.scalar(st_r)

        kb.barrier()
    moesc.close()


def emit_stream_proj(kb, bufs, wT_dram, NT, rhs_tiles, evac_fn, store_fn,
                     E, S, ld_par=("LD5", "LD6"), st_par=("ST2", "ST3"),
                     extra_pe_wait=None, extra_sync_fn=None):
    nc, b = kb.nc, kb.block
    base = dict(kb.cnt)

    def cast_cnt(i):
        return base["DVE"] + (1 if i == 0 else (i + 1) + (i - 1) * E)

    def evac_cnt(i):
        if i == NT - 1:
            return base["DVE"] + NT + NT * E
        return base["DVE"] + (i + 2) + (i + 1) * E

    with ExitStack() as sc:
        wstg = [sc.enter_context(nc.sbuf_tensor(kb.uname("prj_wstg"), [128, D], F32)) for _ in range(2)]
        wbf = [sc.enter_context(nc.sbuf_tensor(kb.uname("prj_wbf"), [128, D], BF16)) for _ in range(2)]
        ps = [[sc.enter_context(nc.psum_tensor(kb.uname("prj_ps%d_%d" % (j, t)), [128, 512], F32))
               for t in range(2)] for j in range(2)]

        def syncw(s):
            for i in range(NT):
                if i >= 2:
                    kb.wait(s, "DVE", cast_cnt(i - 2))
                else:
                    kb.wait(s, "DVE", base["DVE"])
                kb.inc(s.dma_start(out=wstg[i % 2][:, :], in_=wT_dram[i, :, :]),
                       ld_par[i % 2], 16)
                if extra_sync_fn is not None:
                    extra_sync_fn(s, i, evac_cnt)
        b.sync(syncw)

        def vecw(v):
            for i in range(NT):
                kb.wait(v, ld_par[i % 2], base[ld_par[i % 2]] + 16 * (i // 2 + 1))
                if i >= 2:
                    kb.wait(v, "PE", base["PE"] + (i - 1))
                inst = v.tensor_copy(out=wbf[i % 2][:, :], in_=wstg[i % 2][:, :])
                kb.inc(inst, "DVE")
                assert kb.cnt["DVE"] == cast_cnt(i), (i, kb.cnt["DVE"], cast_cnt(i))
                if i >= 1:
                    j = i - 1
                    kb.wait(v, "PE", base["PE"] + j + 1)
                    kb.wait(v, st_par[j % 2], base[st_par[j % 2]] + 16 * S * (j // 2))
                    evac_fn(v, j, ps[j % 2])
                    assert kb.cnt["DVE"] == evac_cnt(j), (j, kb.cnt["DVE"], evac_cnt(j))
            j = NT - 1
            kb.wait(v, "PE", base["PE"] + j + 1)
            kb.wait(v, st_par[j % 2], base[st_par[j % 2]] + 16 * S * (j // 2))
            evac_fn(v, j, ps[j % 2])
            assert kb.cnt["DVE"] == evac_cnt(j)
        b.vector(vecw)

        def pe(t):
            for i in range(NT):
                kb.wait(t, "DVE", cast_cnt(i))
                if i >= 2:
                    kb.wait(t, "DVE", evac_cnt(i - 2))
                if extra_pe_wait is not None and i == 0:
                    extra_pe_wait(t)
                for ch in range(NCH):
                    for tb in range(2):
                        inst = t.matmul(ps[i % 2][tb][:, :], wbf[i % 2][:, ch * 128:(ch + 1) * 128],
                                        rhs_tiles[ch][:, tb * TB:(tb + 1) * TB],
                                        start=(ch == 0), stop=(ch == NCH - 1))
                kb.inc(inst, "PE")
        b.tensor(pe)

        def store(g):
            for i in range(NT):
                kb.wait(g, "DVE", evac_cnt(i))
                store_fn(g, i)
        b.scalar(store)


def emit_rope_tb(kb, v, src, dst, swp, cos_sl, sn_sl):
    kb.inc(v.tensor_copy(out=swp[0:64, :], in_=src[64:128, :]), "DVE")
    kb.inc(v.tensor_copy(out=swp[64:128, :], in_=src[0:64, :]), "DVE")
    kb.sw(v, "DVE")
    kb.inc(v.tensor_tensor(swp, swp, sn_sl, ALU.mult), "DVE")
    kb.inc(v.tensor_tensor(dst, src, cos_sl, ALU.mult), "DVE")
    kb.sw(v, "DVE")
    kb.inc(v.tensor_tensor(dst, dst, swp, ALU.add), "DVE")


def phase_qkv(kb, bufs, wt, use_rope, q_dram, kv_in_dram, seam_fn=None):
    nc, b = kb.nc, kb.block
    ln = bufs["ln"]

    with ExitStack() as sc:
        sb = lambda n_, s_, d_: sc.enter_context(nc.sbuf_tensor(kb.uname(n_), s_, d_))
        qst = [sb("qkv_qst%d" % i, [128, TPC], BF16) for i in range(2)]
        rp = [sb("qkv_rp%d" % i, [128, TPC], BF16) for i in range(2)]
        cos_bf = sb("qkv_cos", [128, TPC], BF16)
        sn_bf = sb("qkv_sn", [128, TPC], BF16)
        v_fm = [sb("qkv_vfm%d" % i, [128, TPC], BF16) for i in range(8)]
        vst = [sb("qkv_vst%d" % i, [128, TPC], BF16) for i in range(2)]

        if use_rope:
            base0 = dict(kb.cnt)

            cstg = [sb("qkv_cstg%d" % i, [128, TPC], F32) for i in range(2)]

            def syncc(s):
                kb.inc(s.dma_start(out=cstg[0][:, :], in_=wt["cos_fm"]), "LD1", 16)
                kb.inc(s.dma_start(out=cstg[1][:, :], in_=wt["sinneg_fm"]), "LD2", 16)
            b.sync(syncc)

            def vecc(v):
                kb.wait(v, "LD1", base0["LD1"] + 16)
                kb.inc(v.tensor_copy(out=cos_bf[:, :], in_=cstg[0][:, :]), "DVE")
                kb.wait(v, "LD2", base0["LD2"] + 16)
                kb.inc(v.tensor_copy(out=sn_bf[:, :], in_=cstg[1][:, :]), "DVE")
            b.vector(vecc)
            kb.barrier()

        E = 12 if use_rope else 2

        def qk_evac(v, nt, pspair):
            for tb in range(2):
                sl = slice(tb * TB, (tb + 1) * TB)
                kb.inc(v.tensor_copy(out=(rp[0][:, sl] if use_rope else qst[nt % 2][:, sl]),
                                     in_=pspair[tb][:, :]), "DVE")
                if use_rope:
                    kb.sw(v, "DVE")
                    emit_rope_tb(kb, v, rp[0][:, sl], qst[nt % 2][:, sl], rp[1][:, sl],
                                 cos_bf[:, sl], sn_bf[:, sl])

        def q_store(g, nt):
            kb.inc(g.dma_start(out=q_dram[nt * 128:(nt + 1) * 128, :], in_=qst[nt % 2][:, :]),
                   ("ST2", "ST3")[nt % 2], 16)

        emit_stream_proj(kb, bufs, wt["wqT"], NCH, ln, qk_evac, q_store, E, 1)
        if seam_fn is not None:
            seam_fn()

        def k_store(g, nt):
            kb.inc(g.dma_start(out=kv_in_dram[nt * 128:(nt + 1) * 128, :], in_=qst[nt % 2][:, :]),
                   ("ST2", "ST3")[nt % 2], 16)

        emit_stream_proj(kb, bufs, wt["wkT"], KVDIM // 128, ln, qk_evac, k_store, E, 1)
        if seam_fn is not None:
            seam_fn()

        def v_evac(v, nt, pspair):
            for tb in range(2):
                kb.inc(v.tensor_copy(out=v_fm[nt][:, tb * TB:(tb + 1) * TB], in_=pspair[tb][:, :]), "DVE")

        def v_store(g, nt):
            pass

        emit_stream_proj(kb, bufs, wt["wvT"], KVDIM // 128, ln, v_evac, v_store, 2, 0)
        if seam_fn is not None:
            seam_fn()

        with ExitStack() as sc2:
            pst = [sc2.enter_context(nc.psum_tensor(kb.uname("vt_ps%d" % i), [128, 512], BF16)) for i in range(2)]
            baseT = dict(kb.cnt)

            def peT(t):
                for tc in range(8):
                    for half in range(2):
                        if tc * 2 + half >= 2:
                            kb.wait(t, "DVE", baseT["DVE"] + (tc * 2 + half) - 1)
                        else:
                            kb.wait(t, "DVE", baseT["DVE"])
                        for dvt in range(4):
                            inst = t.transpose(pst[half][:, dvt * 128:(dvt + 1) * 128],
                                               v_fm[half * 4 + dvt][:, tc * 128:(tc + 1) * 128],
                                               bufs["ident_bf"][:, :])
                        kb.inc(inst, "PE")
            b.tensor(peT)

            def vecT(v):
                for tc in range(8):
                    for half in range(2):
                        k = tc * 2 + half
                        kb.wait(v, "PE", baseT["PE"] + k + 1)
                        if k >= 4:
                            kb.wait(v, ("ST2", "ST3")[tc % 2], baseT[("ST2", "ST3")[tc % 2]] + 16 * (tc // 2))
                        kb.inc(v.tensor_copy(out=vst[tc % 2][:, half * 512:(half + 1) * 512],
                                             in_=pst[half][:, :]), "DVE")
            b.vector(vecT)

            kv_v = kv_in_dram[1024:2048, :].rearrange("(hk p) (tc d) -> p hk tc d", p=128, d=128)

            def stT(g):
                for tc in range(8):
                    kb.wait(g, "DVE", baseT["DVE"] + tc * 2 + 2)
                    kb.inc(g.dma_start(out=kv_v[:, :, tc, :],
                                       in_=vst[tc % 2][:, :].rearrange("p (hk d) -> p hk d", hk=8)),
                           ("ST2", "ST3")[tc % 2], 16)
            b.scalar(stT)

            kb.barrier()


SCALE = 1.0 / math.sqrt(128.0)


KVROWS = 2 * KVDIM + 8


def phase_qk_norms(kb, bufs, q_dram, kv_in_ap):
    nc, b = kb.nc, kb.block
    scal = bufs["scal"]

    with ExitStack() as sc:
        sbl = lambda n_, s_, d_: sc.enter_context(nc.sbuf_tensor(kb.uname(n_), s_, d_))
        qt = [sbl("nrm_qt", [128, TPC], BF16) for _ in range(2)]
        q2 = [sbl("nrm_q2", [128, TPC], BF16) for _ in range(2)]
        kbf = sbl("nrm_kbf", [1, 16], BF16)
        nps = [sc.enter_context(nc.psum_tensor(kb.uname("nrm_ps"), [1, 512], F32)) for _ in range(2)]
        base = dict(kb.cnt)
        LDS = ("LD1", "LD2")
        tiles = [("q", i) for i in range(NCH)] + [("k", i) for i in range(KVDIM // 128)]
        NTL = len(tiles)

        def syncn(s):
            for j, (kind, i) in enumerate(tiles):
                if j >= 2:
                    kb.wait(s, "ACT", base["ACT"] + (j - 1))
                src = q_dram if kind == "q" else kv_in_ap
                kb.inc(s.dma_start(out=qt[j % 2][:, :], in_=src[i * 128:(i + 1) * 128, :]), LDS[j % 2], 16)
        b.sync(syncn)

        def actn(a):
            for j in range(NTL):
                kb.wait(a, LDS[j % 2], base[LDS[j % 2]] + 16 * (j // 2 + 1))
                if j >= 2:
                    kb.wait(a, "PE", base["PE"] + 2 * (j - 1))
                kb.inc(a.activation(q2[j % 2][:, :], qt[j % 2][:, :], AF.Square), "ACT")
        b.scalar(actn)

        def pen(t):
            for j in range(NTL):
                kb.wait(t, "ACT", base["ACT"] + j + 1)
                if j >= 1:
                    kb.wait(t, "DVE", base["DVE"] + 4 * j + 1)
                for tb in range(2):
                    kb.inc(t.matmul(nps[tb][:, :], bufs["ones_bf"][:, :],
                                    q2[j % 2][:, tb * TB:(tb + 1) * TB], start=True, stop=True), "PE")
        b.tensor(pen)

        def vecn(v):
            kb.inc(v.memset(scal[0:1, 0:4], 0.0), "DVE")
            for j, (kind, i) in enumerate(tiles):
                slot = 0 if kind == "q" else 1
                for tb in range(2):
                    kb.wait(v, "PE", base["PE"] + 2 * j + tb + 1)
                    kb.sw(v, "DVE")
                    kb.inc(v.reduce_max(scal[0:1, 2:3], nps[tb][:, :], axis=AX.X), "DVE")
                    kb.sw(v, "DVE")
                    kb.inc(v.tensor_tensor(scal[0:1, slot:slot + 1], scal[0:1, slot:slot + 1],
                                           scal[0:1, 2:3], ALU.max), "DVE")
            kb.sw(v, "DVE")
            kb.inc(v.tensor_copy(out=kbf[0:1, 0:1], in_=scal[0:1, 1:2]), "DVE")
        b.vector(vecn)

        def stn(g):
            kb.wait(g, "DVE", kb.cnt["DVE"])
            kb.inc(g.dma_start(out=kv_in_ap[2 * KVDIM:2 * KVDIM + 1, 0:1], in_=kbf[0:1, 0:1]), "ST0", 16)
        b.scalar(stn)
        kb.barrier()


def phase_attn(kb, bufs, q_dram, kv_g, m01T_dram, causal):
    nc, b = kb.nc, kb.block
    ctx_t = bufs["ln"]
    scal = bufs["scal"]
    phist = {}

    with ExitStack() as sc:
        sb = lambda n_, s_, d_: sc.enter_context(nc.sbuf_tensor(kb.uname(n_), s_, d_))
        pp = lambda n_, s_, d_=F32: sc.enter_context(nc.psum_tensor(kb.uname(n_), s_, d_))
        k_h = [sb("at_k", [128, SEQ], BF16) for _ in range(2)]
        v_h = [sb("at_v", [128, SEQ], BF16) for _ in range(2)]
        q_h = [sb("at_q", [128, TPC], BF16) for _ in range(2)]
        PT = [sb("at_pt", [128, 1024], BF16) for _ in range(2)]
        mskT = [sb("at_mt", [128, NKC * 512], BF16) for _ in range(2)] if causal else None
        kn_bf = sb("at_knbf", [1, 8], BF16)
        negMb = sb("at_negmb", [128, 1], F32)
        rlrow = [sb("at_rlrow", [1, 512], F32) for _ in range(2)]
        rlb_sb = [sb("at_rlb", [128, 512], F32) for _ in range(2)]
        drow = nc.dram_tensor(kb.uname("at_drow"), [2, 512], F32, kind="Internal").ap()
        st_ps = [pp("at_st", [128, 1024]) for _ in range(2)]
        ctx_ps = [pp("at_ctx", [128, 512]) for _ in range(2)]
        l_ps = pp("at_l", [1, 512])
        n2_ps = pp("at_n2", [128, 512])

        base0 = dict(kb.cnt)

        def sync0(s):
            for r in range(4):
                kb.inc(s.dma_start(out=kn_bf[0:1, r:r + 1],
                                   in_=kv_g[KVROWS * r + 2 * KVDIM:KVROWS * r + 2 * KVDIM + 1, 0:1]),
                       "LD1", 16)
        b.sync(sync0)

        def vec0(v):
            kb.wait(v, "LD1", base0["LD1"] + 64)
            kb.inc(v.tensor_copy(out=scal[0:1, 8:12], in_=kn_bf[0:1, 0:4]), "DVE")
            kb.sw(v, "DVE")
            kb.inc(v.reduce_max(scal[0:1, 1:2], scal[0:1, 8:12], axis=AX.X), "DVE")
            kb.sw(v, "DVE")
            kb.inc(v.tensor_tensor(scal[0:1, 2:3], scal[0:1, 0:1], scal[0:1, 1:2], ALU.mult), "DVE")
            kb.wait(v, "ACT", base0["ACT"] + 1)
            kb.inc(v.tensor_scalar(scal[0:1, 4:5], scal[0:1, 3:4], -SCALE, None, ALU.mult), "DVE")
        b.vector(vec0)

        def act0(a):
            kb.wait(a, "DVE", base0["DVE"] + 3)
            kb.inc(a.activation(scal[0:1, 3:4], scal[0:1, 2:3], AF.Sqrt), "ACT")
        b.scalar(act0)

        def pe0(t):
            kb.wait(t, "DVE", base0["DVE"] + 4)
            kb.inc(t.matmul(n2_ps[:, 0:1], bufs["ones_f1"][:, :], scal[0:1, 4:5],
                            start=True, stop=True), "PE")
        b.tensor(pe0)

        def vec0b(v):
            kb.wait(v, "PE", base0["PE"] + 1)
            kb.inc(v.tensor_copy(out=negMb[:, :], in_=n2_ps[:, 0:1]), "DVE")
        b.vector(vec0b)
        kb.barrier()

        if causal:
            basem = dict(kb.cnt)

            def syncm(s, basem=basem):
                for sbk in range(2):
                    kb.inc(s.dma_start(
                        out=mskT[sbk][:, :].rearrange("p (c q) -> p c q", c=NKC),
                        in_=m01T_dram[:, sbk * 512:(sbk + 1) * 512]
                        .rearrange("(c p) q -> p c q", p=128)), "LD7", 16)
            b.sync(syncm)
            mask_ld = kb.cnt["LD7"]
        else:
            mask_ld = 0

        pend = {"v": None}
        blkctr = {"n": 0}

        def emit_bcast_dma(g):
            if pend["v"] is None:
                return
            cpar = pend["v"][3]
            kb.wait(g, "DVE", pend["v"][2])
            kb.inc(g.dma_start(out=drow[cpar:cpar + 1, :], in_=rlrow[cpar][0:1, :]), "LD4", 16)
            kb.wait(g, "LD4", kb.cnt["LD4"])
            kb.inc(g.dma_start(out=rlb_sb[cpar][:, :],
                               in_=drow[cpar:cpar + 1, :].broadcast_to([128, 512])), "LD4", 16)
            pend["v"] = pend["v"] + (kb.cnt["LD4"],)

        def emit_partB(v):
            ph, psb, _, cpar, ld4 = pend["v"]
            kb.wait(v, "LD4", ld4)
            kb.inc(v.tensor_tensor(ctx_t[ph][:, psb * 512:(psb + 1) * 512],
                                   ctx_ps[cpar][:, :], rlb_sb[cpar][:, :], ALU.mult), "DVE")
            pend["v"] = None

        for h in range(32):
            hk = h // 4
            hist = phist.get(h - 2, 0)
            baseh = dict(kb.cnt)
            LDP = ("LD5", "LD6")[h % 2]

            def syncl(s, h=h, hk=hk, LDP=LDP, hist=hist):
                kb.wait(s, "PE", hist)
                for r in range(4):
                    kb.inc(s.dma_start(out=k_h[h % 2][:, r * 1024:(r + 1) * 1024],
                                       in_=kv_g[KVROWS * r + hk * 128:KVROWS * r + (hk + 1) * 128, :]),
                           LDP, 16)
                    kb.inc(s.dma_start(out=v_h[h % 2][:, r * 1024:(r + 1) * 1024],
                                       in_=kv_g[KVROWS * r + 1024 + hk * 128:KVROWS * r + 1024 + (hk + 1) * 128, :]),
                           LDP, 16)
                kb.inc(s.dma_start(out=q_h[h % 2][:, :], in_=q_dram[h * 128:(h + 1) * 128, :]),
                       LDP, 16)
            b.sync(syncl)

            for sbk in range(2):
                base = dict(kb.cnt)
                cpar = blkctr["n"] % 2
                blkctr["n"] += 1
                had_pend = pend["v"] is not None
                if had_pend:
                    def poolB(g):
                        emit_bcast_dma(g)
                    b.sync(poolB)
                act_exp = lambda p, base=base: base["ACT"] + p + 1

                def pe_s_cnt(kc, base=base):
                    p, hf = divmod(kc, 2)
                    return base["PE"] + (2 * p + hf + 1 if p < 2 else 3 * p - 1 + hf)

                def pe_lp_cnt(j, base=base):
                    return base["PE"] + (3 * j + 7 if j <= 13 else 33 + j)

                def dve_m(j, base=base):
                    return base["DVE"] + j + 1

                def peA(t, h=h, sbk=sbk, base=base, LDP=LDP, baseh=baseh,
                        had_pend=had_pend, cpar=cpar):
                    def lavpair(j):
                        if causal:
                            kb.wait(t, "DVE", dve_m(j))
                        else:
                            if j == 0:
                                kb.wait(t, "DVE", base["DVE"])
                            kb.wait(t, "ACT", act_exp(j))
                        for hf in range(2):
                            gk = 2 * j + hf
                            t.matmul(l_ps[:, :], bufs["ones_bf"][:, :],
                                     PT[j % 2][:, hf * 512:(hf + 1) * 512],
                                     start=(gk == 0), stop=(gk == 31))
                            inst = t.matmul(ctx_ps[cpar][:, :], v_h[h % 2][:, gk * 128:(gk + 1) * 128],
                                            PT[j % 2][:, hf * 512:(hf + 1) * 512],
                                            start=(gk == 0), stop=(gk == 31))
                        kb.inc(inst, "PE")

                    for p in range(16):
                        for hf in range(2):
                            kc = 2 * p + hf
                            if kc == 0:
                                if sbk == 0:
                                    kb.wait(t, LDP, baseh[LDP] + 16 * 9)
                                kb.wait(t, "ACT", base["ACT"])
                            if kc >= 4 and hf == 0:
                                kb.wait(t, "ACT", act_exp(p - 2))
                            inst = t.matmul(st_ps[p % 2][:, hf * 512:(hf + 1) * 512],
                                            k_h[h % 2][:, kc * 128:(kc + 1) * 128],
                                            q_h[h % 2][:, sbk * 512:(sbk + 1) * 512],
                                            start=True, stop=True)
                            kb.inc(inst, "PE")
                        if p >= 2:
                            lavpair(p - 2)
                    lavpair(14)
                    lavpair(15)
                b.tensor(peA)

                def actA(a_, h=h, sbk=sbk, base=base):
                    for p in range(16):
                        kb.wait(a_, "PE", pe_s_cnt(2 * p + 1))
                        if p >= 2:
                            kb.wait(a_, "PE", pe_lp_cnt(p - 2))
                        kb.inc(a_.activation(PT[p % 2][:, :], st_ps[p % 2][:, :], AF.Exp,
                                             bias=negMb[:, 0:1], scale=SCALE), "ACT")
                b.scalar(actA)

                def vecB(v, h=h, sbk=sbk, base=base, had_pend=had_pend, cpar=cpar):
                    if causal:
                        for p in range(16):
                            if p == 0:
                                kb.wait(v, "LD7", mask_ld)
                            kb.wait(v, "ACT", act_exp(p))
                            kb.inc(v.tensor_tensor(PT[p % 2][:, :], PT[p % 2][:, :],
                                                   mskT[sbk][:, p * 1024:(p + 1) * 1024], ALU.mult), "DVE")
                    if had_pend:
                        emit_partB(v)
                    kb.wait(v, "PE", pe_lp_cnt(15))
                    if blkctr["n"] >= 3:
                        kb.wait(v, "LD4", kb.cnt["LD4"])
                    kb.inc(v.tensor_copy(out=rlrow[cpar][0:1, :], in_=l_ps[0:1, :]), "DVE")
                    kb.sw(v, "DVE")
                    kb.inc(v.reciprocal(rlrow[cpar][0:1, :], rlrow[cpar][0:1, :]), "DVE")
                    pend["v"] = (h, sbk, kb.cnt["DVE"], cpar)
                b.vector(vecB)

                assert kb.cnt["PE"] == base["PE"] + 48, (kb.cnt["PE"], base["PE"])
                assert kb.cnt["ACT"] == base["ACT"] + 16
            phist[h] = kb.cnt["PE"]

        def pool_tail(g):
            emit_bcast_dma(g)
        b.sync(pool_tail)

        def vec_tail(v):
            emit_partB(v)
        b.vector(vec_tail)
        kb.barrier()



GATE_SLOTS = 11
DOWN_SLOTS = 4
GATE_TILE_ROWS = 128 * NCH
DOWN_TILE_ROWS = 128 * NIT
GU_BLOCK_ROWS = 2 * GATE_SLOTS * GATE_TILE_ROWS
DN_BLOCK_ROWS = DOWN_SLOTS * DOWN_TILE_ROWS


def deq_tile_owner(m, nt):
    if m == "gate" or m == "up":
        r, j = nt // GATE_SLOTS, nt % GATE_SLOTS
        off = (0 if m == "gate" else GATE_SLOTS * GATE_TILE_ROWS) + j * GATE_TILE_ROWS
    else:
        r, j = nt // DOWN_SLOTS, nt % DOWN_SLOTS
        off = j * DOWN_TILE_ROWS
    return r, off


def core_deq_tiles(rank, m):
    if m in ("gate", "up"):
        return [min(rank * GATE_SLOTS + j, 85) for j in range(GATE_SLOTS)]
    return [rank * DOWN_SLOTS + j for j in range(DOWN_SLOTS)]




class DeqEmitter:

    def __init__(self, kb, bufs):
        self.kb, self.bufs, self.nc = kb, bufs, kb.nc
        self.maskI = self.maskH = None
        self.mats = {}
        self.g = None
        self.zscope = None

    def prep_matrix(self, m, qwT, sT, qz, out4d, K, N, glist):
        kb, nc, b = self.kb, self.nc, self.kb.block
        G = max(glist) + 1
        if self.zscope is None:
            self.zscope = ExitStack()
        ctx = self.zscope
        if self.maskI is None:
            self.maskI = kb.ctx.enter_context(nc.sbuf_tensor(kb.uname("dq_mask"), [128, 1], I32))
            self.maskH = kb.ctx.enter_context(nc.sbuf_tensor(kb.uname("dq_maskh"), [128, 1], I32))

            def v0(v):
                kb.inc(v.memset(self.maskI[:, :], 0x0F0F0F0F), "DVE")
                kb.inc(v.memset(self.maskH[:, :], -252645136), "DVE")
            b.vector(v0)
        qz_sb = ctx.enter_context(nc.sbuf_tensor(kb.uname("dq_qz"), [G, N // 8], I32))
        z1f = ctx.enter_context(nc.sbuf_tensor(kb.uname("dq_z1"), [G, N], F32))
        zl = ctx.enter_context(nc.sbuf_tensor(kb.uname("dq_zl"), [G, N // 8], I32))
        zh = ctx.enter_context(nc.sbuf_tensor(kb.uname("dq_zh"), [G, N // 8], I32))

        def s0(s):
            kb.inc(s.dma_start(out=qz_sb[:, :], in_=qz), "LD1", 16)
        b.sync(s0)
        ld1 = kb.cnt["LD1"]
        maskI, maskH = self.maskI, self.maskH

        def v1(v):
            kb.wait(v, "LD1", ld1)
            kb.sw(v, "DVE")
            kb.inc(v.tensor_scalar(zl[:, :], qz_sb[:, :], maskI[0:G, 0:1], None, ALU.bitwise_and), "DVE")
            kb.inc(v.tensor_scalar(zh[:, :], qz_sb[:, :], maskH[0:G, 0:1], None, ALU.bitwise_and), "DVE")
            kb.sw(v, "DVE")
            z1v = z1f[:, :].rearrange("g (nb tk) -> g nb tk", nb=N // 128)
            kb.inc(v.tensor_copy(out=z1v[:, :, 0:64], in_=zl[:, :].bitcast(U8)), "DVE")
            kb.inc(v.tensor_scalar(z1v[:, :, 64:128], zh[:, :].bitcast(U8), 0.0625, None, ALU.mult), "DVE")
        b.vector(v1)
        self.mats[m] = dict(qwT=qwT, sT=sT, out4d=out4d, K=K, G=G, glist=glist,
                            z1f=z1f, prep_dve=kb.cnt["DVE"])

    def group_open(self, K, Gmax, tiles):
        kb, nc = self.kb, self.nc
        sc = ExitStack()
        sbl = lambda n_, s_, d_: sc.enter_context(nc.sbuf_tensor(kb.uname(n_), s_, d_))
        self.g = dict(
            sc=sc, K=K, KC=K // 128, tiles=tiles, i=0, e=0,
            qw=[sbl("dq_qw", [128, K // 8], I32) for _ in range(2)],
            s=[sbl("dq_s", [128, Gmax], F32) for _ in range(2)],
            lo=[sbl("dq_lo", [128, K // 8], I32) for _ in range(2)],
            hi=[sbl("dq_hi", [128, K // 8], I32) for _ in range(2)],
            nz=[sbl("dq_nz", [128, Gmax], F32) for _ in range(2)],
            s16=[sbl("dq_s16", [128, Gmax], F32) for _ in range(2)],
            tmp=sbl("dq_tmp", [128, Gmax], F32),
            w4T=[sbl("dq_w4T", [128, K], BF16) for _ in range(2)],
            wt=[sbl("dq_wt", [128, 512], BF16) for _ in range(2)],
            psz=sc.enter_context(nc.psum_tensor(kb.uname("dq_psz"), [128, Gmax], F32)),
            ptp=sc.enter_context(nc.psum_tensor(kb.uname("dq_ptp"), [128, 512], BF16)),
            entry=dict(kb.cnt), dve_base=[], act_base=[], pe_base=[],
            dve_end=[], act_end=[], pe_end=[], ld=[], st=[], evac=[])

    def emit_tiles(self, n):
        kb, b = self.kb, self.kb.block
        g = self.g
        KC, NEV_all = g["KC"], None
        LDS, STS = ("LD2", "LD3"), ("ST2", "ST3")
        for _ in range(n):
            i = g["i"]
            m, nt = g["tiles"][i]
            mat = self.mats[m]
            glist, G = mat["glist"], mat["G"]
            groups = [(g0, min(4, KC - g0)) for g0 in range(0, KC, 4)]
            NEV = len(groups)
            assert KC % 2 == 0
            base = dict(kb.cnt)
            g["dve_base"].append(base["DVE"])
            g["act_base"].append(base["ACT"])
            g["pe_base"].append(base["PE"])
            dve_end = base["DVE"] + 5 + KC + NEV
            act_end = base["ACT"] + KC
            pe_end = base["PE"] + 1 + NEV
            qw, s_sb, lo, hi = g["qw"], g["s"], g["lo"], g["hi"]
            nz, s16, tmp = g["nz"], g["s16"], g["tmp"]
            w4T, wt, psz, ptp = g["w4T"], g["wt"], g["psz"], g["ptp"]
            e0 = g["e"]
            snap = {"ST2": kb.cnt["ST2"], "ST3": kb.cnt["ST3"]}
            for ev in range(NEV):
                g["evac"].append(base["DVE"] + 5 + KC + ev + 1)
                ch = STS[(e0 + ev) % 2]
                snap[ch] += 16
                g["st"].append(snap[ch])

            def syncp(s, i=i, m=m, nt=nt, mat=mat, base=base):
                if i >= 1:
                    kb.wait(s, "DVE", g["dve_end"][i - 1])
                if i >= 2:
                    kb.wait(s, "ACT", g["act_end"][i - 2])
                kb.inc(s.dma_start(out=qw[i % 2][:, :], in_=mat["qwT"][nt * 128:(nt + 1) * 128, :]), LDS[i % 2], 16)
                kb.inc(s.dma_start(out=s_sb[i % 2][:, 0:G], in_=mat["sT"][nt * 128:(nt + 1) * 128, :]), LDS[i % 2], 16)
            b.sync(syncp)
            g["ld"].append(kb.cnt[LDS[i % 2]])

            def pep(t, i=i, mat=mat, base=base, groups=groups, NEV=NEV, G=G):
                kb.wait(t, "DVE", mat["prep_dve"])
                if i >= 1:
                    kb.wait(t, "DVE", g["dve_base"][i - 1] + 1)
                kb.inc(t.transpose(psz[0:128, 0:G], mat["z1f"][0:G, (g["tiles"][i][1]) * 128:(g["tiles"][i][1] + 1) * 128],
                                   self.bufs["ident_f"][0:G, 0:G]), "PE")
                kb.wait(t, "DVE", base["DVE"] + 5 + KC)
                kb.wait(t, "ACT", base["ACT"] + KC)
                for ev, (g0, gn) in enumerate(groups):
                    e = e0 + ev
                    if e >= 1:
                        kb.wait(t, "DVE", g["evac"][e - 1])
                    for c4 in range(gn):
                        kc = g0 + c4
                        inst = t.transpose(ptp[:, c4 * 128:(c4 + 1) * 128],
                                           w4T[i % 2][:, kc * 128:(kc + 1) * 128],
                                           self.bufs["ident_bf"][:, :])
                    kb.inc(inst, "PE")
            b.tensor(pep)

            def vecp(v, i=i, base=base, groups=groups, NEV=NEV, G=G, glist=glist):
                kb.wait(v, LDS[i % 2], g["ld"][i])
                kb.wait(v, "PE", base["PE"] + 1)
                if i >= 2:
                    kb.wait(v, "ACT", g["act_end"][i - 2])
                kb.inc(v.tensor_scalar(tmp[:, 0:G], psz[0:128, 0:G], -1.0, -1.0, ALU.mult, ALU.add), "DVE")
                kb.sw(v, "DVE")
                kb.inc(v.tensor_tensor(nz[i % 2][:, 0:G], tmp[:, 0:G], s_sb[i % 2][:, 0:G], ALU.mult), "DVE")
                kb.inc(v.tensor_scalar(s16[i % 2][:, 0:G], s_sb[i % 2][:, 0:G], 0.0625, None, ALU.mult), "DVE")
                kb.inc(v.tensor_scalar(lo[i % 2][:, :], qw[i % 2][:, :], self.maskI[:, 0:1], None, ALU.bitwise_and), "DVE")
                kb.inc(v.tensor_scalar(hi[i % 2][:, :], qw[i % 2][:, :], self.maskH[:, 0:1], None, ALU.bitwise_and), "DVE")
                kb.sw(v, "DVE")
                for c in range(0, KC, 2):
                    gg = glist[c]
                    kb.inc(v.tensor_scalar(w4T[i % 2][:, c * 128:c * 128 + 64],
                                           lo[i % 2][:, :].bitcast(U8)[:, c * 64:(c + 1) * 64],
                                           s_sb[i % 2][:, gg:gg + 1], nz[i % 2][:, gg:gg + 1],
                                           ALU.mult, ALU.add), "DVE")
                    kb.inc(v.tensor_scalar(w4T[i % 2][:, c * 128 + 64:(c + 1) * 128],
                                           hi[i % 2][:, :].bitcast(U8)[:, c * 64:(c + 1) * 64],
                                           s16[i % 2][:, gg:gg + 1], nz[i % 2][:, gg:gg + 1],
                                           ALU.mult, ALU.add), "DVE")
                for ev, (g0, gn) in enumerate(groups):
                    e = e0 + ev
                    kb.wait(v, "PE", base["PE"] + 1 + ev + 1)
                    if e >= 2:
                        kb.wait(v, STS[e % 2], g["st"][e - 2])
                    kb.inc(v.tensor_copy(out=wt[e % 2][:, 0:gn * 128],
                                         in_=ptp[:, 0:gn * 128]), "DVE")
                    assert kb.cnt["DVE"] == g["evac"][e]
            b.vector(vecp)

            def actp(a, i=i, base=base, glist=glist):
                kb.wait(a, "DVE", base["DVE"] + 5)
                if i >= 1:
                    kb.wait(a, "PE", g["pe_end"][i - 1])
                for c in range(1, KC, 2):
                    gg = glist[c]
                    kb.inc(a.activation(w4T[i % 2][:, c * 128:c * 128 + 64],
                                        lo[i % 2][:, :].bitcast(U8)[:, c * 64:(c + 1) * 64],
                                        AF.Identity, bias=nz[i % 2][:, gg:gg + 1],
                                        scale=s_sb[i % 2][:, gg:gg + 1]), "ACT")
                    kb.inc(a.activation(w4T[i % 2][:, c * 128 + 64:(c + 1) * 128],
                                        hi[i % 2][:, :].bitcast(U8)[:, c * 64:(c + 1) * 64],
                                        AF.Identity, bias=nz[i % 2][:, gg:gg + 1],
                                        scale=s16[i % 2][:, gg:gg + 1]), "ACT")
            b.scalar(actp)

            def stp(gq_, i=i, nt=nt, mat=mat, base=base, groups=groups):
                for ev, (g0, gn) in enumerate(groups):
                    e = e0 + ev
                    kb.wait(gq_, "DVE", g["evac"][e])
                    kb.inc(gq_.dma_start(
                        out=mat["out4d"][nt, :, g0:g0 + gn, :],
                        in_=wt[e % 2][:, 0:gn * 128].rearrange("p (c n) -> p c n", c=gn)),
                        STS[e % 2], 16)
                    assert kb.cnt[STS[e % 2]] == g["st"][e]
            b.gpsimd(stp)

            assert kb.cnt["DVE"] == dve_end and kb.cnt["ACT"] == act_end and kb.cnt["PE"] == pe_end
            g["dve_end"].append(dve_end)
            g["act_end"].append(act_end)
            g["pe_end"].append(pe_end)
            g["e"] = e0 + NEV
            g["i"] = i + 1

    def group_close(self):
        kb, b = self.kb, self.kb.block
        g = self.g
        fin = dict(kb.cnt)
        st2, st3 = fin["ST2"], fin["ST3"]
        dve_f, act_f, pe_f = fin["DVE"], fin["ACT"], fin["PE"]

        def gsync(s):
            kb.wait(s, "ST2", st2); kb.wait(s, "ST3", st3)
            kb.wait(s, "DVE", dve_f); kb.wait(s, "ACT", act_f); kb.wait(s, "PE", pe_f)
        b.sync(gsync)

        def gvec(v):
            kb.wait(v, "ST2", st2); kb.wait(v, "ST3", st3)
            kb.wait(v, "ACT", act_f); kb.wait(v, "PE", pe_f)
        b.vector(gvec)

        def gact(a):
            kb.wait(a, "ST2", st2); kb.wait(a, "ST3", st3)
            kb.wait(a, "DVE", dve_f); kb.wait(a, "PE", pe_f)
        b.scalar(gact)

        def gpe(t):
            kb.wait(t, "DVE", dve_f); kb.wait(t, "ACT", act_f)
        b.tensor(gpe)
        g["sc"].close()
        if self.zscope is not None:
            self.zscope.close()
            self.zscope = None
        self.g = None



def phase_gptq_dequant_matrix(kb, bufs, qwT, sT, qz, out_dram, K, N, glist):
    nc, b = kb.nc, kb.block
    G = len(set(glist))
    G = max(glist) + 1
    NT = N // 128
    KC = K // 128

    with ExitStack() as sc:
        sbl = lambda n_, s_, d_: sc.enter_context(nc.sbuf_tensor(kb.uname(n_), s_, d_))
        qz_sb = sbl("gq_qz", [G, N // 8], I32)
        z1f = sbl("gq_z1", [G, N], F32)
        zl = sbl("gq_zl", [G, N // 8], I32)
        zh = sbl("gq_zh", [G, N // 8], I32)
        maskI = sbl("gq_mask", [128, 1], I32)
        maskH = sbl("gq_maskh", [128, 1], I32)
        qw_sb = [sbl("gq_qw", [128, K // 8], I32) for _ in range(2)]
        s_sb = [sbl("gq_s", [128, G], F32) for _ in range(2)]
        lo_i = [sbl("gq_lo", [128, K // 8], I32) for _ in range(2)]
        hi_i = [sbl("gq_hi", [128, K // 8], I32) for _ in range(2)]
        nz_sb = [sbl("gq_nz", [128, G], F32) for _ in range(2)]
        s16_sb = [sbl("gq_s16", [128, G], F32) for _ in range(2)]
        tmp_sb = sbl("gq_tmp", [128, G], F32)
        w4T = [sbl("gq_w4T", [128, K], BF16) for _ in range(2)]
        wt_sb = [sbl("gq_wt", [128, 512], BF16) for _ in range(2)]

        base0 = dict(kb.cnt)

        def sync0(s):
            kb.inc(s.dma_start(out=qz_sb[:, :], in_=qz), "LD1", 16)
        b.sync(sync0)

        def vec0(v):
            kb.inc(v.memset(maskI[:, :], 0x0F0F0F0F), "DVE")
            kb.inc(v.memset(maskH[:, :], -252645136), "DVE")
            kb.wait(v, "LD1", base0["LD1"] + 16)
            kb.sw(v, "DVE")
            kb.inc(v.tensor_scalar(zl[:, :], qz_sb[:, :], maskI[0:G, 0:1], None, ALU.bitwise_and), "DVE")
            kb.inc(v.tensor_scalar(zh[:, :], qz_sb[:, :], maskH[0:G, 0:1], None, ALU.bitwise_and), "DVE")
            kb.sw(v, "DVE")
            z1v = z1f[:, :].rearrange("g (nb tk) -> g nb tk", nb=N // 128)
            kb.inc(v.tensor_copy(out=z1v[:, :, 0:64], in_=zl[:, :].bitcast(U8)), "DVE")
            kb.inc(v.tensor_scalar(z1v[:, :, 64:128], zh[:, :].bitcast(U8), 0.0625, None, ALU.mult), "DVE")
        b.vector(vec0)
        kb.barrier()

        basep = dict(kb.cnt)
        LDS = ("LD2", "LD3")
        STS = ("ST2", "ST3")
        groups = [(g0, min(4, KC - g0)) for g0 in range(0, KC, 4)]
        NEV = len(groups)
        assert KC % 2 == 0
        E_TILE = 2 + 1 + 2 + KC + NEV
        A_TILE = KC
        dve_tile_end = lambda i: basep["DVE"] + (i + 1) * E_TILE
        act_tile_end = lambda i: basep["ACT"] + (i + 1) * A_TILE
        P_TILE = 1 + NEV
        pe_tile_end = lambda i: basep["PE"] + (i + 1) * P_TILE
        dve_evac_cnt = lambda i, ev: dve_tile_end(i) - NEV + ev + 1

        def syncp(s):
            for i in range(NT):
                if i >= 2:
                    kb.wait(s, "DVE", dve_tile_end(i - 1))
                    kb.wait(s, "ACT", act_tile_end(i - 2))
                kb.inc(s.dma_start(out=qw_sb[i % 2][:, :], in_=qwT[i * 128:(i + 1) * 128, :]), LDS[i % 2], 16)
                kb.inc(s.dma_start(out=s_sb[i % 2][:, :], in_=sT[i * 128:(i + 1) * 128, :]), LDS[i % 2], 16)
        b.sync(syncp)

        with ExitStack() as scp:
            psz = [scp.enter_context(nc.psum_tensor(kb.uname("gq_psz"), [128, G], F32)) for _ in range(2)]
            ptp = [scp.enter_context(nc.psum_tensor(kb.uname("gq_ptp"), [128, 512], BF16)) for _ in range(2)]

            def pep(t):
                for i in range(NT):
                    if i >= 2:
                        kb.wait(t, "DVE", dve_tile_end(i - 2) + 2)
                    inst = t.transpose(psz[i % 2][:, :],
                                       z1f[0:G, i * 128:(i + 1) * 128],
                                       bufs["ident_f"][0:G, 0:G])
                    kb.inc(inst, "PE")
                    kb.wait(t, "DVE", dve_tile_end(i) - NEV)
                    kb.wait(t, "ACT", act_tile_end(i))
                    for ev, (g0, gn) in enumerate(groups):
                        e = i * NEV + ev
                        if e >= 2:
                            pi, pev = divmod(e - 2, NEV)
                            kb.wait(t, "DVE", dve_evac_cnt(pi, pev))
                        for c4 in range(gn):
                            kc = g0 + c4
                            inst = t.transpose(ptp[e % 2][:, c4 * 128:(c4 + 1) * 128],
                                               w4T[i % 2][:, kc * 128:(kc + 1) * 128],
                                               bufs["ident_bf"][:, :])
                        kb.inc(inst, "PE")
            b.tensor(pep)

            def vecp(v):
                for i in range(NT):
                    kb.wait(v, LDS[i % 2], basep[LDS[i % 2]] + 32 * (i // 2 + 1))
                    kb.wait(v, "PE", pe_tile_end(i) - NEV)
                    if i >= 2:
                        kb.wait(v, "ACT", act_tile_end(i - 2))
                    kb.inc(v.tensor_scalar(tmp_sb[:, :], psz[i % 2][:, :], -1.0, -1.0, ALU.mult, ALU.add), "DVE")
                    kb.sw(v, "DVE")
                    kb.inc(v.tensor_tensor(nz_sb[i % 2][:, :], tmp_sb[:, :], s_sb[i % 2][:, :], ALU.mult), "DVE")
                    kb.inc(v.tensor_scalar(s16_sb[i % 2][:, :], s_sb[i % 2][:, :], 0.0625, None, ALU.mult), "DVE")
                    kb.inc(v.tensor_scalar(lo_i[i % 2][:, :], qw_sb[i % 2][:, :], maskI[:, 0:1], None, ALU.bitwise_and), "DVE")
                    kb.inc(v.tensor_scalar(hi_i[i % 2][:, :], qw_sb[i % 2][:, :], maskH[:, 0:1], None, ALU.bitwise_and), "DVE")
                    kb.sw(v, "DVE")
                    for c in range(0, KC, 2):
                        g = glist[c]
                        kb.inc(v.tensor_scalar(w4T[i % 2][:, c * 128:c * 128 + 64],
                                               lo_i[i % 2][:, :].bitcast(U8)[:, c * 64:(c + 1) * 64],
                                               s_sb[i % 2][:, g:g + 1], nz_sb[i % 2][:, g:g + 1],
                                               ALU.mult, ALU.add), "DVE")
                        kb.inc(v.tensor_scalar(w4T[i % 2][:, c * 128 + 64:(c + 1) * 128],
                                               hi_i[i % 2][:, :].bitcast(U8)[:, c * 64:(c + 1) * 64],
                                               s16_sb[i % 2][:, g:g + 1], nz_sb[i % 2][:, g:g + 1],
                                               ALU.mult, ALU.add), "DVE")
                    for ev, (g0, gn) in enumerate(groups):
                        e = i * NEV + ev
                        kb.wait(v, "PE", pe_tile_end(i) - NEV + ev + 1)
                        if e >= 2:
                            kb.wait(v, STS[e % 2], basep[STS[e % 2]] + 16 * (e // 2))
                        kb.inc(v.tensor_copy(out=wt_sb[e % 2][:, 0:gn * 128],
                                             in_=ptp[e % 2][:, 0:gn * 128]), "DVE")
                    assert kb.cnt["DVE"] == dve_tile_end(i), (i, kb.cnt["DVE"], dve_tile_end(i))
            b.vector(vecp)

            def actp(a):
                for i in range(NT):
                    kb.wait(a, "DVE", dve_tile_end(i) - NEV - KC)
                    if i >= 1:
                        kb.wait(a, "PE", pe_tile_end(i - 1))
                    for c in range(1, KC, 2):
                        g = glist[c]
                        kb.inc(a.activation(w4T[i % 2][:, c * 128:c * 128 + 64],
                                            lo_i[i % 2][:, :].bitcast(U8)[:, c * 64:(c + 1) * 64],
                                            AF.Identity, bias=nz_sb[i % 2][:, g:g + 1],
                                            scale=s_sb[i % 2][:, g:g + 1]), "ACT")
                        kb.inc(a.activation(w4T[i % 2][:, c * 128 + 64:(c + 1) * 128],
                                            hi_i[i % 2][:, :].bitcast(U8)[:, c * 64:(c + 1) * 64],
                                            AF.Identity, bias=nz_sb[i % 2][:, g:g + 1],
                                            scale=s16_sb[i % 2][:, g:g + 1]), "ACT")
                    assert kb.cnt["ACT"] == act_tile_end(i)
            b.scalar(actp)

            def stp(g):
                for i in range(NT):
                    for ev, (g0, gn) in enumerate(groups):
                        e = i * NEV + ev
                        kb.wait(g, "DVE", dve_evac_cnt(i, ev))
                        kb.inc(g.dma_start(
                            out=out_dram[i, :, g0:g0 + gn, :],
                            in_=wt_sb[e % 2][:, 0:gn * 128].rearrange("p (c n) -> p c n", c=gn)),
                            STS[e % 2], 16)
            b.gpsimd(stp)

        kb.barrier()


def phase_gptq_ffn(kb, bufs, Wg_bf, Wu_bf, Wd_bf, x_prev_ap, x_next_ap, hh1_dram,
                   cc2_wait=0):
    nc, b = kb.nc, kb.block
    ln = bufs["ln"]
    if not callable(Wg_bf):
        _g, _u, _d = Wg_bf, Wu_bf, Wd_bf
        Wg_bf = lambda nt: _g[nt, :, :, :]
        Wu_bf = lambda nt: _u[nt, :, :, :]
        Wd_bf = lambda nt: _d[nt, :, :, :]

    with ExitStack() as sc:
        hh0 = sc.enter_context(nc.sbuf_tensor(kb.uname("gf_hh0"), [128, NIT * 512], BF16))

        with ExitStack() as s1:
            gstr = [s1.enter_context(nc.sbuf_tensor(kb.uname("gf_gs"), [128, D], BF16)) for _ in range(2)]
            ustr = [s1.enter_context(nc.sbuf_tensor(kb.uname("gf_us"), [128, D], BF16)) for _ in range(2)]
            hg = [[s1.enter_context(nc.sbuf_tensor(kb.uname("gf_hg"), [128, 512], BF16))
                   for _ in range(2)] for _ in range(2)]
            h1s = [s1.enter_context(nc.sbuf_tensor(kb.uname("gf_h1s"), [128, 512], BF16)) for _ in range(2)]
            ps_g = [[s1.enter_context(nc.psum_tensor(kb.uname("gf_psg"), [128, 512], F32))
                     for _ in range(2)] for _ in range(2)]
            ps_u = [[s1.enter_context(nc.psum_tensor(kb.uname("gf_psu"), [128, 512], F32))
                     for _ in range(2)] for _ in range(2)]
            base = dict(kb.cnt)
            LDS = ("LD2", "LD3")
            STS = ("ST2", "ST3")

            def syncg(s, base=base):
                for i in range(NIT):
                    if i == 0 and cc2_wait:
                        kb.wait(s, "CC2", cc2_wait)
                    if i >= 2:
                        kb.wait(s, "PE", base["PE"] + (i - 1))
                    kb.inc(s.dma_start(out=gstr[i % 2][:, :], in_=Wg_bf(i)), LDS[i % 2], 16)
                    kb.inc(s.dma_start(out=ustr[i % 2][:, :], in_=Wu_bf(i)), LDS[i % 2], 16)
            b.sync(syncg)

            def peg(t, base=base):
                for i in range(NIT):
                    p = i % 2
                    kb.wait(t, LDS[p], base[LDS[p]] + 32 * (i // 2 + 1))
                    if i >= 2:
                        kb.wait(t, "ACT", base["ACT"] + 2 * (i - 1))
                        kb.wait(t, "DVE", base["DVE"] + 2 * (i - 1))
                    for ch in range(NCH):
                        for tb in range(2):
                            t.matmul(ps_g[p][tb][:, :], gstr[p][:, ch * 128:(ch + 1) * 128],
                                     ln[ch][:, tb * TB:(tb + 1) * TB],
                                     start=(ch == 0), stop=(ch == NCH - 1))
                        for tb in range(2):
                            inst = t.matmul(ps_u[p][tb][:, :], ustr[p][:, ch * 128:(ch + 1) * 128],
                                            ln[ch][:, tb * TB:(tb + 1) * TB],
                                            start=(ch == 0), stop=(ch == NCH - 1))
                    kb.inc(inst, "PE")
            b.tensor(peg)

            def actg(a, base=base):
                for i in range(NIT):
                    p = i % 2
                    kb.wait(a, "PE", base["PE"] + i + 1)
                    if i >= 2:
                        kb.wait(a, "DVE", base["DVE"] + 2 * (i - 1))
                    kb.inc(a.activation(hg[p][0][:, :], ps_g[p][0][:, :], AF.Silu), "ACT")
                    kb.inc(a.activation(hg[p][1][:, :], ps_g[p][1][:, :], AF.Silu), "ACT")
            b.scalar(actg)

            def vecg(v, base=base):
                for i in range(NIT):
                    p = i % 2
                    kb.wait(v, "ACT", base["ACT"] + 2 * (i + 1))
                    kb.inc(v.tensor_tensor(hh0[:, i * 512:(i + 1) * 512], hg[p][0][:, :],
                                           ps_u[p][0][:, :], ALU.mult), "DVE")
                    if i >= 2:
                        kb.wait(v, STS[p], base[STS[p]] + 16 * (i // 2))
                    kb.inc(v.tensor_tensor(h1s[p][:, :], hg[p][1][:, :],
                                           ps_u[p][1][:, :], ALU.mult), "DVE")
            b.vector(vecg)

            def stg(g, base=base):
                for i in range(NIT):
                    p = i % 2
                    kb.wait(g, "DVE", base["DVE"] + 2 * (i + 1))
                    kb.inc(g.dma_start(out=hh1_dram[i * 128:(i + 1) * 128, :],
                                       in_=h1s[p][:, :]), STS[p], 16)
            b.gpsimd(stg)
            kb.barrier()

        with ExitStack() as s2:
            hh1x = s2.enter_context(nc.sbuf_tensor(kb.uname("gf_hh1x"), [128, (NIT - 64) * 512], BF16))
            dstr = [s2.enter_context(nc.sbuf_tensor(kb.uname("gf_ds"), [128, 43 * 128], BF16))
                    for _ in range(2)]
            xr = [s2.enter_context(nc.sbuf_tensor(kb.uname("gf_xr"), [128, TPC], F32)) for _ in range(2)]
            ps_d = [[s2.enter_context(nc.psum_tensor(kb.uname("gf_psd"), [128, 512], F32))
                     for _ in range(2)] for _ in range(2)]
            base = dict(kb.cnt)
            LDS = ("LD2", "LD3")
            XLD = ("LD8", "LD9")
            STS = ("ST2", "ST3")

            def hh1sb(kc):
                if kc < 64:
                    return ln[kc // 2][:, (kc % 2) * 512:(kc % 2) * 512 + 512]
                return hh1x[:, (kc - 64) * 512:(kc - 63) * 512]

            def syncd(s, base=base):
                for kc in range(NIT):
                    kb.inc(s.dma_start(out=hh1sb(kc), in_=hh1_dram[kc * 128:(kc + 1) * 128, :]),
                           "LD7", 16)
                for i in range(NCH):
                    for hf in range(2):
                        if i >= 1:
                            kb.wait(s, "PE", base["PE"] + 2 * (i - 1) + hf + 1)
                        kb.inc(s.dma_start(
                            out=dstr[hf][:, :].rearrange("p (c n) -> p c n", c=43),
                            in_=Wd_bf(i)[:, hf * 43:(hf + 1) * 43, :]), LDS[hf], 16)
                    if i >= 2:
                        kb.wait(s, STS[i % 2], base[STS[i % 2]] + 16 * (i // 2))
                    kb.inc(s.dma_start(out=xr[i % 2][:, :],
                                       in_=x_prev_ap[i * 128:(i + 1) * 128, :]), XLD[i % 2], 16)
            b.sync(syncd)

            def ped(t, base=base):
                for i in range(NCH):
                    p = i % 2
                    if i == 0:
                        kb.wait(t, "LD7", base["LD7"] + 16 * NIT)
                    if i >= 2:
                        kb.wait(t, "DVE", base["DVE"] + 2 * (i - 1))
                    for hf in range(2):
                        kb.wait(t, LDS[hf], base[LDS[hf]] + 16 * (i + 1))
                        for kcl in range(43):
                            kc = hf * 43 + kcl
                            t.matmul(ps_d[p][0][:, :], dstr[hf][:, kcl * 128:(kcl + 1) * 128],
                                     hh0[:, kc * 512:(kc + 1) * 512],
                                     start=(kc == 0), stop=(kc == NIT - 1))
                            inst = t.matmul(ps_d[p][1][:, :], dstr[hf][:, kcl * 128:(kcl + 1) * 128],
                                            hh1sb(kc), start=(kc == 0), stop=(kc == NIT - 1))
                        kb.inc(inst, "PE")
            b.tensor(ped)

            def vecd(v, base=base):
                for i in range(NCH):
                    p = i % 2
                    kb.wait(v, "PE", base["PE"] + 2 * (i + 1))
                    kb.wait(v, XLD[p], base[XLD[p]] + 16 * (i // 2 + 1))
                    for tb in range(2):
                        sl = slice(tb * TB, (tb + 1) * TB)
                        kb.inc(v.scalar_tensor_tensor(xr[p][:, sl], ps_d[p][tb][:, :], 0.5,
                                                      xr[p][:, sl], ALU.mult, ALU.add), "DVE")
            b.vector(vecd)

            def std(g, base=base):
                for i in range(NCH):
                    kb.wait(g, "DVE", base["DVE"] + 2 * (i + 1))
                    kb.inc(g.dma_start(out=x_next_ap[i * 128:(i + 1) * 128, :],
                                       in_=xr[i % 2][:, :]), STS[i % 2], 16)
            b.gpsimd(std)
            kb.barrier()


def phase_oproj(kb, bufs, woT_dram, x_prev_ap, x_next_ap):
    nc, b = kb.nc, kb.block
    with ExitStack() as sc:
        xr = [sc.enter_context(nc.sbuf_tensor(kb.uname("op_xr"), [128, TPC], F32)) for _ in range(2)]
        xn = [sc.enter_context(nc.sbuf_tensor(kb.uname("op_xn"), [128, TPC], F32)) for _ in range(2)]
        base = dict(kb.cnt)
        XLD = ("LD8", "LD9")

        def extra_sync(s, i, evac_cnt):
            if i >= 2:
                kb.wait(s, "DVE", evac_cnt(i - 2))
            kb.inc(s.dma_start(out=xr[i % 2][:, :], in_=x_prev_ap[i * 128:(i + 1) * 128, :]),
                   XLD[i % 2], 16)

        def evac(v, nt, pspair):
            kb.wait(v, XLD[nt % 2], base[XLD[nt % 2]] + 16 * (nt // 2 + 1))
            for tb in range(2):
                sl = slice(tb * TB, (tb + 1) * TB)
                kb.inc(v.scalar_tensor_tensor(xn[nt % 2][:, sl], pspair[tb][:, :], 1.0,
                                              xr[nt % 2][:, sl], ALU.mult, ALU.add), "DVE")

        def store(g, nt):
            kb.inc(g.dma_start(out=x_next_ap[nt * 128:(nt + 1) * 128, :], in_=xn[nt % 2][:, :]),
                   ("ST2", "ST3")[nt % 2], 16)

        emit_stream_proj(kb, bufs, woT_dram, NCH, bufs["ln"], evac, store, 2, 1,
                         extra_sync_fn=extra_sync)
        kb.barrier()


def common_bufs(kb):
    bufs = {}
    bufs["ln"] = [kb.sb("ln%d" % i, [128, TPC], BF16) for i in range(NCH)]
    bufs["scal"] = kb.sb("scal", [1, 64], F32)
    bufs["ones_bf"] = kb.sb("ones_bf", [128, 1], BF16)
    bufs["ones_f1"] = kb.sb("ones_f1", [1, 128], F32)
    bufs["ident_bf"] = kb.sb("ident_bf", [128, 128], BF16)
    bufs["ident_f"] = kb.sb("ident_f", [128, 128], F32)
    bufs["E16_f"] = kb.sb("E16_f", [8, 128], F32)
    return bufs


def emit_const_init(kb, bufs, dram_consts):
    b = kb.block

    def vecc(v):
        v.memset(bufs["ones_bf"][:, :], 1.0)
        kb.inc(v.memset(bufs["ones_f1"][:, :], 1.0), "DVE")

    def syncc(s):
        kb.inc(s.dma_start(out=bufs["ident_bf"][:, :], in_=dram_consts["ident_bf"]), "LD1", 16)
        kb.inc(s.dma_start(out=bufs["ident_f"][:, :], in_=dram_consts["ident_f"]), "LD1", 16)
        kb.inc(s.dma_start(out=bufs["E16_f"][:, :], in_=dram_consts["E16_f"]), "LD1", 16)
    b.vector(vecc)
    b.sync(syncc)
    kb.barrier()


def const_arrays():
    e16 = np.zeros((8, 128), np.float32)
    for e in range(8):
        e16[e, e * 16:(e + 1) * 16] = 1.0
    return {
        "ident_bf": np.eye(128).astype(ml_dtypes.bfloat16),
        "ident_f": np.eye(128, dtype=np.float32),
        "E16_f": e16,
    }


def declare_consts(nc):
    arrs = const_arrays()
    return {k: nc.dram_tensor("const_" + k, list(v.shape),
                              BF16 if v.dtype == ml_dtypes.bfloat16 else F32,
                              kind="ExternalInput").ap() for k, v in arrs.items()}


def build_test_attn(causal):
    nc = bass.Bass("TRN2", target_bir_lowering=False)
    q = nc.dram_tensor("q", [D, TPC], BF16, kind="ExternalInput").ap()
    kv_g = nc.dram_tensor("kv_g", [4 * KVROWS, TPC], BF16, kind="ExternalInput").ap()
    qn = nc.dram_tensor("qn", [1, 1], F32, kind="ExternalInput").ap()
    m01 = nc.dram_tensor("m01", [SEQ, TPC], BF16, kind="ExternalInput").ap() if causal else None
    ctx_out = nc.dram_tensor("ctx_out", [D, TPC], BF16, kind="ExternalOutput").ap()
    consts = declare_consts(nc)
    with ExitStack() as ctx:
        kb = KB(nc, ctx)
        bufs = common_bufs(kb)
        emit_const_init(kb, bufs, consts)

        def ldqn(s):
            kb.inc(s.dma_start(out=bufs["scal"][0:1, 0:1], in_=qn), "LD1", 16)
        kb.block.sync(ldqn)
        kb.barrier()
        phase_attn(kb, bufs, q, kv_g, m01, causal)

        def store(g):
            kb.wait(g, "DVE", kb.cnt["DVE"])
            for i in range(NCH):
                kb.inc(g.dma_start(out=ctx_out[i * 128:(i + 1) * 128, :], in_=bufs["ln"][i][:, :]), "ST0", 16)
            kb.wait(g, "ST0", kb.cnt["ST0"])
        kb.block.gpsimd(store)
    return nc


def build_test_qkv(use_rope):
    nc = bass.Bass("TRN2", target_bir_lowering=False)
    x = nc.dram_tensor("x", [D, TPC], F32, kind="ExternalInput").ap()
    qout = nc.dram_tensor("qout", [D, TPC], BF16, kind="ExternalOutput").ap()
    kvout = nc.dram_tensor("kvout", [2 * KVDIM, TPC], BF16, kind="ExternalOutput").ap()
    wt = {k: nc.dram_tensor(k, shp, F32, kind="ExternalInput").ap()
          for k, shp in (("wqT", [NCH, 128, NCH, 128]), ("wkT", [8, 128, NCH, 128]),
                         ("wvT", [8, 128, NCH, 128]),
                         ("cos_fm", [128, TPC]), ("sinneg_fm", [128, TPC]))}
    consts = declare_consts(nc)
    with ExitStack() as ctx:
        kb = KB(nc, ctx)
        bufs = common_bufs(kb)
        emit_const_init(kb, bufs, consts)
        phase_ln(kb, bufs, x)
        phase_qkv(kb, bufs, wt, use_rope, qout, kvout)
    return nc


def ref_rope_tables(tpos):
    DH = 128
    inv_freq = 1.0 / (500000.0 ** (np.arange(0, DH, 2, dtype=np.float32) / DH))
    wavelen = 2.0 * np.pi / inv_freq
    cut_lo = 8192 / 4.0
    cut_hi = 8192 / 1.0
    inv_freq = np.where(wavelen > cut_hi, inv_freq / 32.0, inv_freq)
    smooth = (8192 / wavelen - 1.0) / (4.0 - 1.0)
    smoothed = (1.0 - smooth) * inv_freq / 32.0 + smooth * inv_freq
    is_medium = ~(wavelen < cut_hi) & ~(wavelen > cut_lo)
    inv_freq = np.where(is_medium, smoothed.astype(np.float32), inv_freq).astype(np.float32)
    freqs = tpos[:, None].astype(np.float32) * inv_freq[None, :]
    return np.cos(freqs), np.sin(freqs)


def host_cos_fm(tpos):
    cos, sin = ref_rope_tables(tpos)
    cos_fm = np.concatenate([cos.T, cos.T], axis=0).astype(np.float32)
    sinneg_fm = np.concatenate([-sin.T, sin.T], axis=0).astype(np.float32)
    return np.ascontiguousarray(cos_fm), np.ascontiguousarray(sinneg_fm)



def _decl_in(nc, name, shape, dt):
    return nc.dram_tensor(name, list(shape), dt, kind="ExternalInput").ap()


def _decl_out(nc, name, shape, dt):
    return nc.dram_tensor(name, list(shape), dt, kind="ExternalOutput").ap()


def _decl_moe_wt(nc):
    return {"route_wT": _decl_in(nc, "route_wT", [D, 8], F32),
            "A_lhsT": _decl_in(nc, "A_lhsT", [D, 128], F32),
            "Bw_lhsT": _decl_in(nc, "Bw_lhsT", [128, D], F32)}


def _decl_gptq(nc):
    d = {}
    for m, N_, G_ in (("gate", INTER, 32), ("up", INTER, 32), ("down", D, 86)):
        K_ = D if m != "down" else INTER
        d[m + "_qwT"] = _decl_in(nc, m + "_qwT", [N_, K_ // 8], I32)
        d[m + "_sT"] = _decl_in(nc, m + "_sT", [N_, G_], F32)
        d[m + "_qz"] = _decl_in(nc, m + "_qz", [G_, N_ // 8], I32)
        d[m + "_bf"] = nc.dram_tensor(m + "_bf", [N_ // 128, 128, K_ // 128, 128], BF16, kind="Internal").ap()
    return d


def _emit_gptq_dequant_all(kb, bufs, gq, glists):
    for m, K_, N_ in (("gate", D, INTER), ("up", D, INTER), ("down", INTER, D)):
        phase_gptq_dequant_matrix(kb, bufs, gq[m + "_qwT"], gq[m + "_sT"],
                                  gq[m + "_qz"], gq[m + "_bf"], K_, N_, glists[m])


def build_part0():
    nc = bass.Bass("TRN2", target_bir_lowering=False)
    x0 = _decl_in(nc, "x0", [D, TPC], F32)
    x1 = _decl_out(nc, "x1", [D, TPC], F32)
    q0 = _decl_out(nc, "q0", [D, TPC], BF16)
    kvs0 = _decl_out(nc, "kvs0", [KVROWS, TPC], BF16)
    moe_wt = _decl_moe_wt(nc)
    qkv_wt = {"wqT": _decl_in(nc, "wqT", [NCH, 128, NCH, 128], F32),
              "wkT": _decl_in(nc, "wkT", [KVDIM // 128, 128, NCH, 128], F32),
              "wvT": _decl_in(nc, "wvT", [KVDIM // 128, 128, NCH, 128], F32)}
    consts = declare_consts(nc)
    with ExitStack() as ctx:
        kb = KB(nc, ctx)
        bufs = common_bufs(kb)
        emit_const_init(kb, bufs, consts)
        phase_ln(kb, bufs, x0)
        phase_moe(kb, bufs, moe_wt, x0, x1)
        phase_ln(kb, bufs, x1)
        phase_qkv(kb, bufs, qkv_wt, False, q0, kvs0)
        phase_qk_norms(kb, bufs, q0, kvs0)
    return nc


def build_part1(glists):
    nc = bass.Bass("TRN2", target_bir_lowering=False)
    x1 = _decl_in(nc, "x1", [D, TPC], F32)
    q0 = _decl_in(nc, "q0", [D, TPC], BF16)
    kv_g0 = _decl_in(nc, "kv_g0", [4 * KVROWS, TPC], BF16)
    x4 = _decl_out(nc, "x4", [D, TPC], F32)
    q1 = _decl_out(nc, "q1", [D, TPC], BF16)
    kvs1 = _decl_out(nc, "kvs1", [KVROWS, TPC], BF16)
    x2 = nc.dram_tensor("x2", [D, TPC], F32, kind="Internal").ap()
    x3 = nc.dram_tensor("x3", [D, TPC], F32, kind="Internal").ap()
    woT0 = _decl_in(nc, "woT0", [NCH, 128, NCH, 128], F32)
    moe_wt = _decl_moe_wt(nc)
    qkv_wt = {"wqT": _decl_in(nc, "wqT", [NCH, 128, NCH, 128], F32),
              "wkT": _decl_in(nc, "wkT", [KVDIM // 128, 128, NCH, 128], F32),
              "wvT": _decl_in(nc, "wvT", [KVDIM // 128, 128, NCH, 128], F32),
              "cos_fm": _decl_in(nc, "cos_fm", [128, TPC], F32),
              "sinneg_fm": _decl_in(nc, "sinneg_fm", [128, TPC], F32)}
    gq = _decl_gptq(nc)
    consts = declare_consts(nc)
    with ExitStack() as ctx:
        kb = KB(nc, ctx)
        bufs = common_bufs(kb)
        emit_const_init(kb, bufs, consts)
        phase_attn(kb, bufs, q0, kv_g0, None, False)
        phase_oproj(kb, bufs, woT0, x1, x2)
        phase_ln(kb, bufs, x2)
        phase_moe(kb, bufs, moe_wt, x2, x3)
        phase_ln(kb, bufs, x3)
        phase_gptq_ffn(kb, bufs, wg_fn, wu_fn, wd_fn, x3, x4, hh1_scr, cc2_wait=6)
        phase_ln(kb, bufs, x4)
        phase_qkv(kb, bufs, qkv_wt, True, q1, kvs1)
        phase_qk_norms(kb, bufs, q1, kvs1)
    return nc


def build_part2(glists):
    nc = bass.Bass("TRN2", target_bir_lowering=False)
    x4 = _decl_in(nc, "x4", [D, TPC], F32)
    q1 = _decl_in(nc, "q1", [D, TPC], BF16)
    kv_g1 = _decl_in(nc, "kv_g1", [4 * KVROWS, TPC], BF16)
    m01 = _decl_in(nc, "m01", [SEQ, TPC], BF16)
    xout = _decl_out(nc, "xout", [D, TPC], F32)
    x5 = nc.dram_tensor("x5", [D, TPC], F32, kind="Internal").ap()
    woT1 = _decl_in(nc, "woT1", [NCH, 128, NCH, 128], F32)
    gq = {}
    for m, G_ in (("gate", 32), ("up", 32), ("down", 86)):
        T = GATE_SLOTS if m != "down" else DOWN_SLOTS
        K_ = D if m != "down" else INTER
        N_ = INTER if m != "down" else D
        gq[m + "_qwT"] = _decl_in(nc, m + "_qwT", [T * 128, K_ // 8], I32)
        gq[m + "_sT"] = _decl_in(nc, m + "_sT", [T * 128, G_], F32)
        gq[m + "_qz"] = _decl_in(nc, m + "_qz", [G_, T * 16], I32)
    hh1_scr = nc.dram_tensor("hh1_scr", [NIT * 128, 512], BF16, kind="Internal").ap()
    deq_in_gu = nc.dram_tensor("deq_in_gu", [GU_BLOCK_ROWS, 128], BF16, kind="Internal")
    deq_out_gu = nc.dram_tensor("deq_out_gu", [8 * GU_BLOCK_ROWS, 128], BF16, kind="Internal", addr_space="Shared")
    deq_in_dn = nc.dram_tensor("deq_in_dn", [DN_BLOCK_ROWS, 128], BF16, kind="Internal")
    deq_out_dn = nc.dram_tensor("deq_out_dn", [8 * DN_BLOCK_ROWS, 128], BF16, kind="Internal", addr_space="Shared")
    consts = declare_consts(nc)
    with ExitStack() as ctx:
        kb = KB(nc, ctx)
        bufs = common_bufs(kb)
        emit_const_init(kb, bufs, consts)
        _emit_gptq_dequant_all(kb, bufs, gq, glists)
        phase_attn(kb, bufs, q1, kv_g1, m01, True)
        phase_oproj(kb, bufs, woT1, x4, x5)
        phase_ln(kb, bufs, x5)
        hh1_p = nc.dram_tensor("hh1_p", [NIT * 128, 512], BF16, kind="Internal").ap()
        phase_gptq_ffn(kb, bufs, gq["gate_bf"], gq["up_bf"], gq["down_bf"], x5, xout, hh1_p)
    return nc


def emit_kv_allgather(kb, kv_in, kv_g):
    groups = [[0, 1, 2, 3], [4, 5, 6, 7]]

    st_n = {ch: kb.cnt[ch] for ch in ("ST0", "ST2", "ST3")}

    def g_cc(g):
        for ch in sorted(st_n):
            kb.wait(g, ch, st_n[ch])
        kb.inc(g.collective_compute(
            "AllGather", ALU.bypass, replica_groups=groups,
            ins=[kv_in.ap().opt()], outs=[kv_g.ap().opt()]), "CC")
    kb.block.gpsimd(g_cc)
    cc_n = kb.cnt["CC"]

    def s_cc(s):
        kb.wait(s, "CC", cc_n)
    kb.block.sync(s_cc)
    kb.barrier()


def build_full(glists):
    nc = bass.Bass("TRN2", target_bir_lowering=False)
    x0 = _decl_in(nc, "x0", [D, TPC], F32)
    xout = _decl_out(nc, "xout", [D, TPC], F32)
    x1 = nc.dram_tensor("x1", [D, TPC], F32, kind="Internal").ap()
    x2 = nc.dram_tensor("x2", [D, TPC], F32, kind="Internal").ap()
    x3 = nc.dram_tensor("x3", [D, TPC], F32, kind="Internal").ap()
    x4 = nc.dram_tensor("x4", [D, TPC], F32, kind="Internal").ap()
    x5 = nc.dram_tensor("x5", [D, TPC], F32, kind="Internal").ap()
    q0 = nc.dram_tensor("q0", [D, TPC], BF16, kind="Internal").ap()
    q1 = nc.dram_tensor("q1", [D, TPC], BF16, kind="Internal").ap()
    kv_in0 = nc.dram_tensor("kv_in0", [KVROWS, TPC], BF16, kind="Internal")
    kv_g0 = nc.dram_tensor("kv_g0", [4 * KVROWS, TPC], BF16, kind="Internal")
    kv_in1 = nc.dram_tensor("kv_in1", [KVROWS, TPC], BF16, kind="Internal")
    kv_g1 = nc.dram_tensor("kv_g1", [4 * KVROWS, TPC], BF16, kind="Internal")
    m01 = _decl_in(nc, "m01", [SEQ, TPC], BF16)
    moe_wt = _decl_moe_wt(nc)
    a0 = {"wqT": _decl_in(nc, "a0_wqT", [NCH, 128, NCH, 128], F32),
          "wkT": _decl_in(nc, "a0_wkT", [KVDIM // 128, 128, NCH, 128], F32),
          "wvT": _decl_in(nc, "a0_wvT", [KVDIM // 128, 128, NCH, 128], F32)}
    woT0 = _decl_in(nc, "woT0", [NCH, 128, NCH, 128], F32)
    a1 = {"wqT": _decl_in(nc, "a1_wqT", [NCH, 128, NCH, 128], F32),
          "wkT": _decl_in(nc, "a1_wkT", [KVDIM // 128, 128, NCH, 128], F32),
          "wvT": _decl_in(nc, "a1_wvT", [KVDIM // 128, 128, NCH, 128], F32),
          "cos_fm": _decl_in(nc, "cos_fm", [128, TPC], F32),
          "sinneg_fm": _decl_in(nc, "sinneg_fm", [128, TPC], F32)}
    woT1 = _decl_in(nc, "woT1", [NCH, 128, NCH, 128], F32)
    gq = {}
    for m, G_ in (("gate", 32), ("up", 32), ("down", 86)):
        T = GATE_SLOTS if m != "down" else DOWN_SLOTS
        K_ = D if m != "down" else INTER
        N_ = INTER if m != "down" else D
        gq[m + "_qwT"] = _decl_in(nc, m + "_qwT", [T * 128, K_ // 8], I32)
        gq[m + "_sT"] = _decl_in(nc, m + "_sT", [T * 128, G_], F32)
        gq[m + "_qz"] = _decl_in(nc, m + "_qz", [G_, T * 16], I32)
    hh1_scr = nc.dram_tensor("hh1_scr", [NIT * 128, 512], BF16, kind="Internal").ap()
    deq_in_gu = nc.dram_tensor("deq_in_gu", [GU_BLOCK_ROWS, 128], BF16, kind="Internal")
    deq_out_gu = nc.dram_tensor("deq_out_gu", [8 * GU_BLOCK_ROWS, 128], BF16, kind="Internal", addr_space="Shared")
    deq_in_dn = nc.dram_tensor("deq_in_dn", [DN_BLOCK_ROWS, 128], BF16, kind="Internal")
    deq_out_dn = nc.dram_tensor("deq_out_dn", [8 * DN_BLOCK_ROWS, 128], BF16, kind="Internal", addr_space="Shared")
    consts = declare_consts(nc)
    with ExitStack() as ctx:
        kb = KB(nc, ctx)
        bufs = common_bufs(kb)
        emit_const_init(kb, bufs, consts)
        deq = DeqEmitter(kb, bufs)

        def deq_out4d(m):
            K_ = D if m != "down" else INTER
            T = GATE_SLOTS if m != "down" else DOWN_SLOTS
            trows = GATE_TILE_ROWS if m != "down" else DOWN_TILE_ROWS
            din = deq_in_dn.ap() if m == "down" else deq_in_gu.ap()
            sect = GATE_SLOTS * GATE_TILE_ROWS if m == "up" else 0
            return din[sect:sect + T * trows, :].rearrange(
                "(nt p c) n -> nt p c n", p=128, c=K_ // 128)
        for m in ("gate", "up"):
            deq.prep_matrix(m, gq[m + "_qwT"], gq[m + "_sT"], gq[m + "_qz"],
                            deq_out4d(m), D, GATE_SLOTS * 128, glists[m])
        deq.group_open(D, 32, [("gate", j) for j in range(GATE_SLOTS)]
                       + [("up", j) for j in range(GATE_SLOTS)])

        AG_CHUNKS = [("gate", 0, 6), ("gate", 6, 11), ("up", 0, 6), ("up", 6, 11),
                     ("down", 0, 2), ("down", 2, 4)]
        chunk_base = {}
        cb_gu = cb_dn = 0
        for ci, (m_, s0_, s1_) in enumerate(AG_CHUNKS):
            trows = DOWN_TILE_ROWS if m_ == "down" else GATE_TILE_ROWS
            rows = (s1_ - s0_) * trows
            if m_ == "down":
                chunk_base[ci] = cb_dn; cb_dn += 8 * rows
            else:
                chunk_base[ci] = cb_gu; cb_gu += 8 * rows
        assert cb_gu == 8 * GU_BLOCK_ROWS and cb_dn == 8 * DN_BLOCK_ROWS

        def emit_deq_ag(ci):
            m_, s0_, s1_ = AG_CHUNKS[ci]
            trows = DOWN_TILE_ROWS if m_ == "down" else GATE_TILE_ROWS
            rows = (s1_ - s0_) * trows
            if m_ == "down":
                in_sl = deq_in_dn.ap()[s0_ * trows:s1_ * trows, :]
                out_sl = deq_out_dn.ap()[chunk_base[ci]:chunk_base[ci] + 8 * rows, :]
            else:
                sect = GATE_SLOTS * GATE_TILE_ROWS if m_ == "up" else 0
                in_sl = deq_in_gu.ap()[sect + s0_ * trows:sect + s1_ * trows, :]
                out_sl = deq_out_gu.ap()[chunk_base[ci]:chunk_base[ci] + 8 * rows, :]

            st2_n, st3_n = kb.cnt["ST2"], kb.cnt["ST3"]

            def g_ag(g):
                kb.wait(g, "ST2", st2_n)
                kb.wait(g, "ST3", st3_n)
                kb.inc(g.collective_compute(
                    "AllGather", ALU.bypass, replica_groups=[list(range(8))],
                    ins=[in_sl.opt()], outs=[out_sl.opt()]), "CC2")
            kb.block.gpsimd(g_ag)
        dout_gu = deq_out_gu.ap()
        dout_dn = deq_out_dn.ap()

        def mk_wfn(m, K_):
            T = DOWN_SLOTS if m == "down" else GATE_SLOTS
            trows = DOWN_TILE_ROWS if m == "down" else GATE_TILE_ROWS

            def f(nt):
                r, j = nt // T, nt % T
                for ci, (m_, s0_, s1_) in enumerate(AG_CHUNKS):
                    if m_ == m and s0_ <= j < s1_:
                        base_row = chunk_base[ci] + r * (s1_ - s0_) * trows + (j - s0_) * trows
                        break
                dout = dout_dn if m == "down" else dout_gu
                return dout[base_row:base_row + 128 * (K_ // 128), :].rearrange(
                    "(p c) n -> p c n", p=128)
            return f
        wg_fn = mk_wfn("gate", D)
        wu_fn = mk_wfn("up", D)
        wd_fn = mk_wfn("down", INTER)

        phase_ln(kb, bufs, x0)
        deq.emit_tiles(4)
        phase_moe(kb, bufs, moe_wt, x0, x1)
        deq.emit_tiles(4)
        emit_deq_ag(0)
        phase_ln(kb, bufs, x1)
        deq.emit_tiles(3)
        emit_deq_ag(1)
        qkv_seams = [4, 4, 3]

        def qkv_seam():
            deq.emit_tiles(qkv_seams.pop(0))
        phase_qkv(kb, bufs, a0, False, q0, kv_in0.ap(), seam_fn=qkv_seam)
        emit_deq_ag(2)
        emit_deq_ag(3)
        deq.group_close()
        deq.prep_matrix("down", gq["down_qwT"], gq["down_sT"], gq["down_qz"],
                        deq_out4d("down"), INTER, DOWN_SLOTS * 128, glists["down"])
        deq.group_open(INTER, 86, [("down", j) for j in range(DOWN_SLOTS)])
        deq.emit_tiles(2)
        phase_qk_norms(kb, bufs, q0, kv_in0.ap())
        deq.emit_tiles(2)
        emit_kv_allgather(kb, kv_in0, kv_g0)
        deq.group_close()
        emit_deq_ag(4)
        emit_deq_ag(5)
        phase_attn(kb, bufs, q0, kv_g0.ap(), None, False)
        phase_oproj(kb, bufs, woT0, x1, x2)
        phase_ln(kb, bufs, x2)
        phase_moe(kb, bufs, moe_wt, x2, x3)
        phase_ln(kb, bufs, x3)
        phase_gptq_ffn(kb, bufs, wg_fn, wu_fn, wd_fn, x3, x4, hh1_scr, cc2_wait=6)
        phase_ln(kb, bufs, x4)
        phase_qkv(kb, bufs, a1, True, q1, kv_in1.ap())
        phase_qk_norms(kb, bufs, q1, kv_in1.ap())
        emit_kv_allgather(kb, kv_in1, kv_g1)
        phase_attn(kb, bufs, q1, kv_g1.ap(), m01, True)
        phase_oproj(kb, bufs, woT1, x4, x5)
        phase_ln(kb, bufs, x5)
        phase_gptq_ffn(kb, bufs, wg_fn, wu_fn, wd_fn, x5, xout, hh1_scr)
    return nc


def kernel_fused(inputs):
    shared, a0, a1, moe, gq, glists, per_core = prep_host(inputs)
    gkey = tuple(tuple(glists[m]) for m in ("gate", "up", "down"))
    nc = _get_nc(("full", gkey), build_full, glists)
    maps = []
    for c in range(NCORES):
        m = {"x0": per_core[c]["x0"], "m01": per_core[c]["m01"],
             "cos_fm": per_core[c]["cos_fm"], "sinneg_fm": per_core[c]["sinneg_fm"],
             "woT0": a0["woT0"], "woT1": a1["woT1"], **moe, **shared}
        for mm in ("gate", "up", "down"):
            m[mm + "_qwT"] = per_core[c][mm + "_qwT"]
            m[mm + "_sT"] = per_core[c][mm + "_sT"]
            m[mm + "_qz"] = per_core[c][mm + "_qz"]
        for k in ("wqT", "wkT", "wvT"):
            m["a0_" + k] = a0[k]
            m["a1_" + k] = a1[k]
        maps.append(m)
    rr = run_bass_kernel_spmd(nc, maps, core_ids=list(range(NCORES))).results
    out = np.empty((NCORES * TPC, D), np.float32)
    for c in range(NCORES):
        nat = np.empty((D, TPC), np.float32)
        nat[PERM] = rr[c]["xout"]
        out[c * TPC:(c + 1) * TPC] = nat.T
    return out.reshape(2, SEQ, D)


def _glist_from(g_idx):
    gi = np.asarray(g_idx).reshape(-1, 128)
    assert (gi == gi[:, 0:1]).all(), "g_idx must be constant within 128-blocks"
    return [int(v) for v in gi[:, 0]]


def tile_wT(wT):
    Dd, N = wT.shape
    return np.ascontiguousarray(
        wT.reshape(Dd // 128, 128, N // 128, 128).transpose(2, 1, 0, 3))


def prep_host(inputs):
    asc = np.ascontiguousarray
    shared = dict(const_arrays())
    shared = {"const_" + k: v for k, v in shared.items()}

    def attn_w(L):
        return {"wqT": tile_wT(np.asarray(inputs[f"L{L}_attn_Wq"]).T[PERM]),
                "wkT": tile_wT(np.asarray(inputs[f"L{L}_attn_Wk"]).T[PERM]),
                "wvT": tile_wT(np.asarray(inputs[f"L{L}_attn_Wv"]).T[PERM]),
                "woT" + str(L): tile_wT(np.asarray(inputs[f"L{L}_attn_Wo"]).T[:, PERM])}

    a0 = attn_w(0)
    a1 = attn_w(1)
    moe = {"route_wT": asc(inputs["L0_ffn_route_w"].T[PERM]),
           "A_lhsT": asc(np.asarray(inputs["L0_ffn_A"]).reshape(128, D).T[PERM]),
           "Bw_lhsT": asc(np.asarray(inputs["L0_ffn_Bw"]).transpose(0, 2, 1).reshape(128, D)[:, PERM])}
    gq = {}
    glists = {}
    for m in ("gate", "up", "down"):
        P_N = PERM2 if m != "down" else PERM
        gq[m + "_qwT"] = asc(np.asarray(inputs[f"L1_ffn_{m}_qweight"]).T[P_N])
        gq[m + "_sT"] = asc(np.asarray(inputs[f"L1_ffn_{m}_scales"]).T[P_N])
        gq[m + "_qz"] = np.asarray(inputs[f"L1_ffn_{m}_qzeros"])
        glists[m] = _glist_from(inputs[f"L1_ffn_{m}_g_idx"])

    xf = np.asarray(inputs["L0_attn_x"]).reshape(NCORES * TPC, D)
    per_core = []
    for c in range(NCORES):
        p0 = (c % 4) * TPC
        cos_fm, sinneg_fm = host_cos_fm(np.arange(TPC) + p0)
        m01 = np.ascontiguousarray((np.arange(SEQ)[:, None] <= (p0 + np.arange(TPC))[None, :]).astype(ml_dtypes.bfloat16))
        pc = {
            "x0": asc(xf[c * TPC:(c + 1) * TPC].T[PERM]),
            "cos_fm": cos_fm, "sinneg_fm": sinneg_fm, "m01": m01,
        }
        for m in ("gate", "up", "down"):
            tiles = core_deq_tiles(c, m)
            for sfx in ("_qwT", "_sT"):
                full = gq[m + sfx]
                pc[m + sfx] = asc(full.reshape(full.shape[0] // 128, 128, -1)[tiles]
                                  .reshape(len(tiles) * 128, -1))
            qzf = gq[m + "_qz"]
            G_ = qzf.shape[0]
            pc[m + "_qz"] = asc(qzf.reshape(G_, -1, 16)[:, tiles].reshape(G_, -1))
        per_core.append(pc)
    return shared, a0, a1, moe, gq, glists, per_core


_BUILD_CACHE = {}


def _get_nc(key, builder, *args):
    if key not in _BUILD_CACHE:
        _BUILD_CACHE[key] = builder(*args)
    return _BUILD_CACHE[key]


def kernel(**inputs):
    return kernel_fused(inputs)


def kernel_unfused(**inputs):
    shared, a0, a1, moe, gq, glists, per_core = prep_host(inputs)
    gkey = tuple(tuple(glists[m]) for m in ("gate", "up", "down"))

    nc0 = _get_nc("p0", build_part0)
    maps0 = []
    for c in range(NCORES):
        m = {"x0": per_core[c]["x0"], **moe,
             "wqT": a0["wqT"], "wkT": a0["wkT"], "wvT": a0["wvT"], **shared}
        maps0.append(m)
    r0 = run_bass_kernel_spmd(nc0, maps0, core_ids=list(range(NCORES))).results

    def gather(rr, key):
        out = []
        for c in range(NCORES):
            grp = range(0, 4) if c < 4 else range(4, 8)
            out.append(np.concatenate([rr[g][key] for g in grp], axis=0))
        return out

    kvg0 = gather(r0, "kvs0")

    nc1 = _get_nc(("p1", gkey), build_part1, glists)
    maps1 = []
    for c in range(NCORES):
        m = {"x1": r0[c]["x1"], "q0": r0[c]["q0"], "kv_g0": kvg0[c],
             "woT0": a0["woT0"], **moe,
             "wqT": a1["wqT"], "wkT": a1["wkT"], "wvT": a1["wvT"],
             "cos_fm": per_core[c]["cos_fm"], "sinneg_fm": per_core[c]["sinneg_fm"],
             **gq, **shared}
        maps1.append(m)
    r1 = run_bass_kernel_spmd(nc1, maps1, core_ids=list(range(NCORES))).results
    kvg1 = gather(r1, "kvs1")

    nc2 = _get_nc(("p2", gkey), build_part2, glists)
    maps2 = []
    for c in range(NCORES):
        m = {"x4": r1[c]["x4"], "q1": r1[c]["q1"], "kv_g1": kvg1[c],
             "m01": per_core[c]["m01"], "woT1": a1["woT1"], **gq, **shared}
        maps2.append(m)
    r2 = run_bass_kernel_spmd(nc2, maps2, core_ids=list(range(NCORES))).results

    out = np.empty((NCORES * TPC, D), np.float32)
    for c in range(NCORES):
        nat = np.empty((D, TPC), np.float32)
        nat[PERM] = r2[c]["xout"]
        out[c * TPC:(c + 1) * TPC] = nat.T
    return out.reshape(2, SEQ, D)


def build_test_moe():
    nc = bass.Bass("TRN2", target_bir_lowering=False)
    x = nc.dram_tensor("x", [D, TPC], F32, kind="ExternalInput").ap()
    x1 = nc.dram_tensor("x1", [D, TPC], F32, kind="ExternalOutput").ap()
    wt = {k: nc.dram_tensor(k, shp, F32, kind="ExternalInput").ap()
          for k, shp in (("route_wT", [D, 8]), ("A_lhsT", [D, 128]), ("Bw_lhsT", [128, D]))}
    consts = declare_consts(nc)
    with ExitStack() as ctx:
        kb = KB(nc, ctx)
        bufs = common_bufs(kb)
        emit_const_init(kb, bufs, consts)
        phase_ln(kb, bufs, x)
        phase_moe(kb, bufs, wt, x, x1)
    return nc


if __name__ == "__main__":
    rng = np.random.default_rng(0)
    x = rng.standard_normal((D, TPC)).astype(np.float32)
    route_w = (rng.standard_normal((8, D)) / math.sqrt(D)).astype(np.float32)
    A = (rng.standard_normal((8, 16, D)) / math.sqrt(D)).astype(np.float32)
    Bw = (rng.standard_normal((8, D, 16)) * 0.02).astype(np.float32)

    wt_np = {
        "route_wT": np.ascontiguousarray(route_w.T),
        "A_lhsT": np.ascontiguousarray(A.reshape(128, D).T),
        "Bw_lhsT": np.ascontiguousarray(Bw.transpose(0, 2, 1).reshape(128, D)),
    }
    in_map = {"x": x, **wt_np, **const_arrays()}

    nc = build_test_moe()
    import os
    if os.environ.get("SIM"):
        from concourse import bass_interp
        sim = bass_interp.CoreSim(nc)
        for k, v in in_map.items():
            sim.tensor(k if k in ("x",) else ("const_" + k if k in const_arrays() else k))[:] = v
        sim.simulate()
        out = sim.tensor("x1").astype(np.float32)
    else:
        res = run_bass_kernel_spmd(nc, [{("const_" + k if k in const_arrays() else k): v
                                         for k, v in in_map.items()}], core_ids=[0])
        out = res.results[0]["x1"].astype(np.float32)

    xt = x.T
    mu = xt.mean(1, keepdims=True)
    ln = (xt - mu) / np.sqrt(xt.var(1, keepdims=True) + EPS)
    logits = ln @ route_w.T
    mx = logits.max(1, keepdims=True)
    rw = (logits == mx) * mx
    h = ln @ A.reshape(128, D).T
    wh = h * np.repeat(rw, 16, axis=1)
    o = wh @ Bw.transpose(0, 2, 1).reshape(128, D)
    ref = x + 0.5 * o.T
    rel = np.linalg.norm(out - ref) / np.linalg.norm(ref)
    print("MoE rel l2 err:", rel, "max abs:", np.abs(out - ref).max())


# version 21
# speedup vs baseline: 1.0566x; 1.0566x over previous
import sys
try:
    import concourse.bass
except ImportError:
    for _p in ("/opt/trn_rl_repo",):
        if _p not in sys.path:
            sys.path.insert(0, _p)

from contextlib import ExitStack
import math
import numpy as np
import ml_dtypes

import concourse.bass as bass
import concourse.mybir as mybir
from concourse.bass_utils import run_bass_kernel_spmd

F32 = mybir.dt.float32
BF16 = mybir.dt.bfloat16
U8 = mybir.dt.uint8
I32 = mybir.dt.int32
AF = mybir.ActivationFunctionType
ALU = mybir.AluOpType
AX = mybir.AxisListType

D = 4096
TPC = 1024
NCH = D // 128
TB = 512
NTB = TPC // TB
INTER = 11008
NIT = INTER // 128
NCORES = 8
EPS = 1e-5
HEADS = 32
KVHEADS = 8
HDIM = 128
KVDIM = KVHEADS * HDIM
SEQ = 4096
NKC = SEQ // 128
NKB = SEQ // 512


def perm_eo(n):
    p = np.arange(n).reshape(n // 128, 128)
    return np.concatenate([p[:, 0::2], p[:, 1::2]], axis=1).reshape(-1)

PERM = perm_eo(D)
PERM2 = perm_eo(INTER)
IPERM = np.argsort(PERM)



N_LD = 10
N_ST = 4


class KB:

    def __init__(self, nc, ctx):
        self.nc = nc
        self.ctx = ctx
        self.block = ctx.enter_context(nc.Block())
        chans = ["PE", "DVE", "ACT", "POOL", "CC", "CC2"]
        chans += ["LD%d" % i for i in range(N_LD)]
        chans += ["ST%d" % i for i in range(N_ST)]
        self.sem = {c: ctx.enter_context(nc.semaphore(c)) for c in chans}
        self.cnt = {c: 0 for c in chans}
        self.uid = 0

    def uname(self, base):
        self.uid += 1
        return "%s_u%d" % (base, self.uid)

    def inc(self, inst, ch, n=1):
        inst.then_inc(self.sem[ch], n)
        self.cnt[ch] += n
        return self.cnt[ch]

    def wait(self, eng, ch, val):
        if val > 0:
            eng.wait_ge(self.sem[ch], val)

    def sw(self, eng, ch):
        self.wait(eng, ch, self.cnt[ch])

    def barrier(self):
        incs = {"LD0": "sync", "PE": "tensor", "DVE": "vector",
                "ACT": "scalar"}
        targets = {ch: v for ch, v in self.cnt.items()
                   if v > 0 and ch not in ("CC", "CC2", "POOL")}
        for ch in incs:
            targets[ch] = self.cnt[ch] + 1

        def mk(ch):
            def f(e):
                self.wait(e, ch, targets[ch] - 1)
                self.inc(e.nop(), ch, 1)
                for och in sorted(targets):
                    if och != ch:
                        self.wait(e, och, targets[och])
            return f
        b = self.block
        b.sync(mk("LD0"))
        b.tensor(mk("PE"))
        b.vector(mk("DVE"))
        b.scalar(mk("ACT"))

    def sb(self, name, shape, dtype):
        t = self.ctx.enter_context(self.nc.sbuf_tensor(name, shape, dtype))
        return t

    def psum(self, name, shape, dtype=F32):
        return self.ctx.enter_context(self.nc.psum_tensor(name, shape, dtype))



def phase_ln(kb, bufs, x_ap):
    nc, b = kb.nc, kb.block
    ln = bufs["ln"]
    ones_bf = bufs["ones_bf"]
    ones_f1 = bufs["ones_f1"]

    with ExitStack() as ps_ctx:
        pp = lambda n, s, d=F32: ps_ctx.enter_context(nc.psum_tensor(kb.uname(n), s, d))
        sbl = lambda n, s, d: ps_ctx.enter_context(nc.sbuf_tensor(kb.uname(n), s, d))
        xt = [sbl("ln_xt", [128, TPC], F32) for _ in range(2)]
        xbf = [sbl("ln_xbf", [128, TPC], BF16) for _ in range(2)]
        x2bf = [sbl("ln_x2bf", [128, TPC], BF16) for _ in range(2)]
        t1 = sbl("ln_t1", [128, TB], F32)
        mu_b = sbl("ln_mub", [128, TB], F32)
        tmp_b = sbl("ln_tmpb", [128, TB], F32)
        t2_b = sbl("ln_t2b", [128, TB], F32)
        A_t = [sbl("ln_At", [128, TB], F32) for _ in range(2)]
        B_t = [sbl("ln_Bt", [128, TB], F32) for _ in range(2)]
        ps_sx = [pp("ln_sx%d" % t, [1, 512]) for t in range(2)]
        ps_sxx = [pp("ln_sxx%d" % t, [1, 512]) for t in range(2)]
        ps_bc = [pp("ln_bc%d" % t, [128, 512]) for t in range(2)]

        base = dict(kb.cnt)
        LDS = ("LD1", "LD2")

        def sync1(s):
            for i in range(NCH):
                if i >= 2:
                    kb.wait(s, "DVE", base["DVE"] + (i - 1))
                    kb.wait(s, "ACT", base["ACT"] + (i - 1))
                kb.inc(s.dma_start(out=xt[i % 2][:, :], in_=x_ap[i * 128:(i + 1) * 128, :]), LDS[i % 2], 16)
        b.sync(sync1)

        def vec1(v):
            for i in range(NCH):
                kb.wait(v, LDS[i % 2], base[LDS[i % 2]] + 16 * (i // 2 + 1))
                if i >= 2:
                    kb.wait(v, "PE", base["PE"] + (i - 1))
                kb.inc(v.tensor_copy(out=xbf[i % 2][:, :], in_=xt[i % 2][:, :]), "DVE")
        b.vector(vec1)

        def act1(a):
            for i in range(NCH):
                kb.wait(a, LDS[i % 2], base[LDS[i % 2]] + 16 * (i // 2 + 1))
                if i >= 2:
                    kb.wait(a, "PE", base["PE"] + (i - 1))
                kb.inc(a.activation(x2bf[i % 2][:, :], xt[i % 2][:, :], AF.Square), "ACT")
        b.scalar(act1)

        def pe1(t):
            for i in range(NCH):
                kb.wait(t, "DVE", base["DVE"] + (i + 1))
                kb.wait(t, "ACT", base["ACT"] + (i + 1))
                st = (i == 0)
                sp = (i == NCH - 1)
                for tb in range(2):
                    sl = slice(tb * TB, (tb + 1) * TB)
                    t.matmul(ps_sx[tb][:, :], ones_bf[:, :], xbf[i % 2][:, sl], start=st, stop=sp)
                    inst = t.matmul(ps_sxx[tb][:, :], ones_bf[:, :], x2bf[i % 2][:, sl], start=st, stop=sp)
                kb.inc(inst, "PE")
        b.tensor(pe1)

        base2 = dict(kb.cnt)

        def vec2a(v):
            kb.wait(v, "PE", base2["PE"])
            for tb in range(2):
                kb.inc(v.tensor_copy(out=A_t[tb][0:1, :], in_=ps_sx[tb][:, :]), "DVE")
                kb.inc(v.tensor_copy(out=B_t[tb][0:1, :], in_=ps_sxx[tb][:, :]), "DVE")
        b.vector(vec2a)

        base3 = dict(kb.cnt)

        def pe2(t):
            kb.wait(t, "DVE", base3["DVE"])
            for tb in range(2):
                if tb == 1:
                    kb.wait(t, "DVE", base3["DVE"] + 2)
                t.matmul(ps_bc[0][:, :], ones_f1[:, :], A_t[tb][0:1, :], start=True, stop=True)
                inst = t.matmul(ps_bc[1][:, :], ones_f1[:, :], B_t[tb][0:1, :], start=True, stop=True)
                kb.inc(inst, "PE")
        b.tensor(pe2)

        def vec2b(v):
            for tb in range(2):
                kb.wait(v, "PE", base3["PE"] + tb + 1)
                if tb == 1:
                    kb.sw(v, "DVE")
                kb.inc(v.tensor_scalar(mu_b[:, :], ps_bc[0][:, :], 1.0 / D, None, ALU.mult), "DVE")
                kb.inc(v.tensor_scalar(tmp_b[:, :], ps_bc[1][:, :], 1.0 / D, None, ALU.mult), "DVE")
                kb.sw(v, "DVE")
                kb.inc(v.tensor_tensor(t2_b[:, :], mu_b[:, :], mu_b[:, :], ALU.mult), "DVE")
                kb.sw(v, "DVE")
                kb.inc(v.tensor_tensor(t2_b[:, :], tmp_b[:, :], t2_b[:, :], ALU.subtract), "DVE")
                kb.sw(v, "DVE")
                kb.inc(v.tensor_scalar(t2_b[:, :], t2_b[:, :], EPS, None, ALU.add), "DVE")
                kb.wait(v, "ACT", base3["ACT"] + tb + 1)
                kb.inc(v.reciprocal(A_t[tb][:, :], tmp_b[:, :]), "DVE")
                kb.sw(v, "DVE")
                kb.inc(v.tensor_tensor(B_t[tb][:, :], mu_b[:, :], A_t[tb][:, :], ALU.mult), "DVE")
                kb.sw(v, "DVE")
                kb.inc(v.tensor_scalar(B_t[tb][:, :], B_t[tb][:, :], -1.0, None, ALU.mult), "DVE")
        b.vector(vec2b)

        def act2(a):
            for tb in range(2):
                kb.wait(a, "DVE", base3["DVE"] + 8 * tb + 5)
                kb.inc(a.activation(tmp_b[:, :], t2_b[:, :], AF.Sqrt), "ACT")
        b.scalar(act2)

        base4 = dict(kb.cnt)

        def sync2(s):
            kb.wait(s, "DVE", base4["DVE"])
            for i in range(NCH):
                if i >= 2:
                    kb.wait(s, "DVE", base4["DVE"] + 4 * (i - 1))
                kb.inc(s.dma_start(out=xt[i % 2][:, :], in_=x_ap[i * 128:(i + 1) * 128, :]), LDS[i % 2], 16)
        b.sync(sync2)

        def vec3(v):
            for i in range(NCH):
                kb.wait(v, LDS[i % 2], base4[LDS[i % 2]] + 16 * (i // 2 + 1))
                for tb in range(2):
                    sl = slice(tb * TB, (tb + 1) * TB)
                    kb.sw(v, "DVE")
                    kb.inc(v.tensor_tensor(t1[:, 0:512], xt[i % 2][:, sl], A_t[tb][:, :], ALU.mult), "DVE")
                    kb.sw(v, "DVE")
                    kb.inc(v.tensor_tensor(ln[i][:, sl], t1[:, 0:512], B_t[tb][:, :], ALU.add), "DVE")
        b.vector(vec3)

        kb.barrier()


def phase_moe(kb, bufs, wt, x_prev_ap, x_next_ap):
    nc, b = kb.nc, kb.block
    ln = bufs["ln"]
    E16 = bufs["E16_f"]

    moesc = ExitStack()
    kb.ctx.enter_context(moesc)
    sbl = lambda n, s, d: moesc.enter_context(nc.sbuf_tensor(kb.uname(n), s, d))
    rw_f = sbl("moe_rwf", [128, 512], F32)
    rwbf = sbl("moe_rwbf", [128, 256], BF16)
    a_f = sbl("moe_af", [128, TPC], F32)
    a_bf = sbl("moe_abf", [128, NCH * 128], BF16)
    bw_bf = sbl("moe_bwbf", [128, NCH * 128], BF16)
    lg_sb = sbl("moe_lg", [128, 64], F32)
    msk_sb = sbl("moe_msk", [128, 8], F32)
    m_sb = sbl("moe_m", [128, 8], F32)
    rwtm = sbl("moe_rwtm", [128, 64], F32)
    rw8 = sbl("moe_rw8", [8, 1024], F32)
    rwx_sb = [sbl("moe_rwx%d" % t, [128, 512], F32) for t in range(2)]
    wh_sb = [sbl("moe_wh%d" % t, [128, 512], BF16) for t in range(2)]
    xr = [sbl("moe_xr%d" % i, [128, 512], F32) for i in range(2)]
    xn = [sbl("moe_xn%d" % i, [128, 512], F32) for i in range(2)]

    base = dict(kb.cnt)

    def syncw(s):
        kb.inc(s.dma_start(out=rw_f[:, 0:256].rearrange("p (ch e) -> p ch e", ch=NCH),
                           in_=wt["route_wT"].rearrange("(ch p) e -> p ch e", p=128)), "LD1", 16)
    b.sync(syncw)

    def vecw(v):
        kb.wait(v, "LD1", base["LD1"] + 16)
        kb.inc(v.tensor_copy(out=rwbf[:, :], in_=rw_f[:, 0:256]), "DVE")
    b.vector(vecw)

    loads = []
    for q in range(4):
        loads.append((wt["A_lhsT"][q * 1024:(q + 1) * 1024, :].rearrange("(c p) h -> p c h", p=128),
                      a_bf[:, q * 1024:(q + 1) * 1024], True))
    for q in range(4):
        loads.append((wt["Bw_lhsT"][:, q * 1024:(q + 1) * 1024], bw_bf[:, q * 1024:(q + 1) * 1024], False))
    for q, (src, dst, is3d) in enumerate(loads):
        basep = dict(kb.cnt)

        def syncq(s, src=src, is3d=is3d, basep=basep):
            kb.wait(s, "DVE", basep["DVE"])
            out_ap = a_f[:, :].rearrange("p (c h) -> p c h", c=8) if is3d else a_f[:, :]
            kb.inc(s.dma_start(out=out_ap, in_=src), "LD2", 16)
        b.sync(syncq)

        def vecq(v, dst=dst, basep=basep):
            kb.wait(v, "LD2", basep["LD2"] + 16)
            kb.inc(v.tensor_copy(out=dst, in_=a_f[:, :]), "DVE")
        b.vector(vecq)

    with ExitStack() as sc1:
        ps_l = sc1.enter_context(nc.psum_tensor(kb.uname("moe_psl"), [128, 8], F32))
        ps_rw8 = sc1.enter_context(nc.psum_tensor(kb.uname("moe_psrw8"), [8, 1024], F32))
        base3 = dict(kb.cnt)

        def pe_l(t):
            kb.wait(t, "DVE", base3["DVE"])
            for tt in range(8):
                if tt > 0:
                    kb.wait(t, "DVE", base3["DVE"] + 4 * (tt - 1) + 1)
                for ch in range(NCH):
                    inst = t.matmul(ps_l[:, :], ln[ch][:, tt * 128:(tt + 1) * 128],
                                    rwbf[:, ch * 8:(ch + 1) * 8],
                                    start=(ch == 0), stop=(ch == NCH - 1))
                kb.inc(inst, "PE")
        b.tensor(pe_l)

        def vec_l(v):
            for tt in range(8):
                kb.wait(v, "PE", base3["PE"] + tt + 1)
                kb.inc(v.tensor_copy(out=lg_sb[:, tt * 8:(tt + 1) * 8], in_=ps_l[:, :]), "DVE")
                kb.sw(v, "DVE")
                kb.inc(v.reduce_max(m_sb[:, tt:tt + 1], lg_sb[:, tt * 8:(tt + 1) * 8], axis=AX.X), "DVE")
                kb.sw(v, "DVE")
                kb.inc(v.tensor_scalar(msk_sb[:, :], lg_sb[:, tt * 8:(tt + 1) * 8],
                                       m_sb[:, tt:tt + 1], None, ALU.is_equal), "DVE")
                kb.sw(v, "DVE")
                kb.inc(v.tensor_scalar(rwtm[:, tt * 8:(tt + 1) * 8], msk_sb[:, :],
                                       m_sb[:, tt:tt + 1], None, ALU.mult), "DVE")
        b.vector(vec_l)

        base4 = dict(kb.cnt)

        def pe_t(t):
            kb.wait(t, "DVE", base4["DVE"])
            for tt in range(8):
                inst = t.transpose(ps_rw8[0:8, tt * 128:(tt + 1) * 128],
                                   rwtm[:, tt * 8:(tt + 1) * 8], bufs["ident_f"][:, :])
            kb.inc(inst, "PE")
        b.tensor(pe_t)

        def vec_t(v):
            kb.wait(v, "PE", base4["PE"] + 1)
            kb.inc(v.tensor_copy(out=rw8[:, :], in_=ps_rw8[:, :]), "DVE")
        b.vector(vec_t)

    join = dict(kb.cnt)
    with ExitStack() as sc2:
        pp = lambda n, s: sc2.enter_context(nc.psum_tensor(kb.uname(n), s, F32))
        ps_rwx = [pp("moe_rwx_ps%d" % t, [128, 512]) for t in range(2)]
        ps_h = [pp("moe_h_ps%d" % t, [128, 512]) for t in range(2)]
        ps_o = [pp("moe_o_ps%d" % t, [128, 512]) for t in range(2)]

        def pe_hx(t):
            kb.wait(t, "DVE", join["DVE"])
            for tb in range(2):
                t.matmul(ps_rwx[tb][:, :], E16[:, :], rw8[0:8, tb * 512:(tb + 1) * 512],
                         start=True, stop=True)
                for ch in range(NCH):
                    inst = t.matmul(ps_h[tb][:, :], a_bf[:, ch * 128:(ch + 1) * 128],
                                    ln[ch][:, tb * 512:(tb + 1) * 512],
                                    start=(ch == 0), stop=(ch == NCH - 1))
                kb.inc(inst, "PE")
        b.tensor(pe_hx)

        def vec_hx(v):
            for tb in range(2):
                kb.wait(v, "PE", join["PE"] + tb + 1)
                kb.inc(v.tensor_copy(out=rwx_sb[tb][:, :], in_=ps_rwx[tb][:, :]), "DVE")
                kb.sw(v, "DVE")
                kb.inc(v.tensor_tensor(wh_sb[tb][:, :], ps_h[tb][:, :], rwx_sb[tb][:, :], ALU.mult), "DVE")
        b.vector(vec_hx)

        base5 = dict(kb.cnt)
        RLD = ("LD3", "LD4")
        RST = ("ST0", "ST1")

        def sync_r(s):
            k = 0
            for ot in range(NCH):
                for tb in range(2):
                    if k >= 2:
                        kb.wait(s, "DVE", base5["DVE"] + (k - 1))
                    kb.inc(s.dma_start(out=xr[k % 2][:, :],
                                       in_=x_prev_ap[ot * 128:(ot + 1) * 128, tb * 512:(tb + 1) * 512]),
                           RLD[k % 2], 16)
                    k += 1
        b.sync(sync_r)

        def pe_o(t):
            kb.wait(t, "DVE", base5["DVE"])
            k = 0
            for ot in range(NCH):
                for tb in range(2):
                    if k >= 2:
                        kb.wait(t, "DVE", base5["DVE"] + (k - 1))
                    inst = t.matmul(ps_o[k % 2][:, :], bw_bf[:, ot * 128:(ot + 1) * 128],
                                    wh_sb[tb][:, :], start=True, stop=True)
                    kb.inc(inst, "PE")
                    k += 1
        b.tensor(pe_o)

        def vec_r(v):
            k = 0
            for ot in range(NCH):
                for tb in range(2):
                    kb.wait(v, "PE", base5["PE"] + k + 1)
                    kb.wait(v, RLD[k % 2], base5[RLD[k % 2]] + 16 * (k // 2 + 1))
                    if k >= 2:
                        kb.wait(v, RST[k % 2], base5[RST[k % 2]] + 16 * (k // 2))
                    kb.inc(v.scalar_tensor_tensor(xn[k % 2][:, :], ps_o[k % 2][:, :], 0.5,
                                                  xr[k % 2][:, :], ALU.mult, ALU.add), "DVE")
                    k += 1
        b.vector(vec_r)

        def st_r(g):
            k = 0
            for ot in range(NCH):
                for tb in range(2):
                    kb.wait(g, "DVE", base5["DVE"] + k + 1)
                    kb.inc(g.dma_start(out=x_next_ap[ot * 128:(ot + 1) * 128, tb * 512:(tb + 1) * 512],
                                       in_=xn[k % 2][:, :]), RST[k % 2], 16)
                    k += 1
        b.scalar(st_r)

        kb.barrier()
    moesc.close()


def emit_stream_proj(kb, bufs, wT_dram, NT, rhs_tiles, evac_fn, store_fn,
                     E, S, ld_par=("LD5", "LD6"), st_par=("ST2", "ST3"),
                     extra_pe_wait=None, extra_sync_fn=None):
    nc, b = kb.nc, kb.block
    base = dict(kb.cnt)

    def cast_cnt(i):
        return base["DVE"] + (1 if i == 0 else (i + 1) + (i - 1) * E)

    def evac_cnt(i):
        if i == NT - 1:
            return base["DVE"] + NT + NT * E
        return base["DVE"] + (i + 2) + (i + 1) * E

    with ExitStack() as sc:
        wstg = [sc.enter_context(nc.sbuf_tensor(kb.uname("prj_wstg"), [128, D], F32)) for _ in range(2)]
        wbf = [sc.enter_context(nc.sbuf_tensor(kb.uname("prj_wbf"), [128, D], BF16)) for _ in range(2)]
        ps = [[sc.enter_context(nc.psum_tensor(kb.uname("prj_ps%d_%d" % (j, t)), [128, 512], F32))
               for t in range(2)] for j in range(2)]

        def syncw(s):
            for i in range(NT):
                if i >= 2:
                    kb.wait(s, "DVE", cast_cnt(i - 2))
                else:
                    kb.wait(s, "DVE", base["DVE"])
                kb.inc(s.dma_start(out=wstg[i % 2][:, :], in_=wT_dram[i, :, :]),
                       ld_par[i % 2], 16)
                if extra_sync_fn is not None:
                    extra_sync_fn(s, i, evac_cnt)
        b.sync(syncw)

        def vecw(v):
            for i in range(NT):
                kb.wait(v, ld_par[i % 2], base[ld_par[i % 2]] + 16 * (i // 2 + 1))
                if i >= 2:
                    kb.wait(v, "PE", base["PE"] + (i - 1))
                inst = v.tensor_copy(out=wbf[i % 2][:, :], in_=wstg[i % 2][:, :])
                kb.inc(inst, "DVE")
                assert kb.cnt["DVE"] == cast_cnt(i), (i, kb.cnt["DVE"], cast_cnt(i))
                if i >= 1:
                    j = i - 1
                    kb.wait(v, "PE", base["PE"] + j + 1)
                    kb.wait(v, st_par[j % 2], base[st_par[j % 2]] + 16 * S * (j // 2))
                    evac_fn(v, j, ps[j % 2])
                    assert kb.cnt["DVE"] == evac_cnt(j), (j, kb.cnt["DVE"], evac_cnt(j))
            j = NT - 1
            kb.wait(v, "PE", base["PE"] + j + 1)
            kb.wait(v, st_par[j % 2], base[st_par[j % 2]] + 16 * S * (j // 2))
            evac_fn(v, j, ps[j % 2])
            assert kb.cnt["DVE"] == evac_cnt(j)
        b.vector(vecw)

        def pe(t):
            for i in range(NT):
                kb.wait(t, "DVE", cast_cnt(i))
                if i >= 2:
                    kb.wait(t, "DVE", evac_cnt(i - 2))
                if extra_pe_wait is not None and i == 0:
                    extra_pe_wait(t)
                for ch in range(NCH):
                    for tb in range(2):
                        inst = t.matmul(ps[i % 2][tb][:, :], wbf[i % 2][:, ch * 128:(ch + 1) * 128],
                                        rhs_tiles[ch][:, tb * TB:(tb + 1) * TB],
                                        start=(ch == 0), stop=(ch == NCH - 1))
                kb.inc(inst, "PE")
        b.tensor(pe)

        def store(g):
            for i in range(NT):
                kb.wait(g, "DVE", evac_cnt(i))
                store_fn(g, i)
        b.scalar(store)


def emit_rope_tb(kb, v, src, dst, swp, cos_sl, sn_sl):
    kb.inc(v.tensor_copy(out=swp[0:64, :], in_=src[64:128, :]), "DVE")
    kb.inc(v.tensor_copy(out=swp[64:128, :], in_=src[0:64, :]), "DVE")
    kb.sw(v, "DVE")
    kb.inc(v.tensor_tensor(swp, swp, sn_sl, ALU.mult), "DVE")
    kb.inc(v.tensor_tensor(dst, src, cos_sl, ALU.mult), "DVE")
    kb.sw(v, "DVE")
    kb.inc(v.tensor_tensor(dst, dst, swp, ALU.add), "DVE")


def phase_qkv(kb, bufs, wt, use_rope, q_dram, kv_in_dram, seam_fn=None, st_rec=None,
              tail_fn=None):
    nc, b = kb.nc, kb.block
    ln = bufs["ln"]

    with ExitStack() as sc:
        sb = lambda n_, s_, d_: sc.enter_context(nc.sbuf_tensor(kb.uname(n_), s_, d_))
        qst = [sb("qkv_qst%d" % i, [128, TPC], BF16) for i in range(2)]
        rp = [sb("qkv_rp%d" % i, [128, TPC], BF16) for i in range(2)]
        cos_bf = sb("qkv_cos", [128, TPC], BF16)
        sn_bf = sb("qkv_sn", [128, TPC], BF16)
        v_fm = [sb("qkv_vfm%d" % i, [128, TPC], BF16) for i in range(8)]
        vst = [sb("qkv_vst%d" % i, [128, TPC], BF16) for i in range(2)]

        if use_rope:
            base0 = dict(kb.cnt)

            cstg = [sb("qkv_cstg%d" % i, [128, TPC], F32) for i in range(2)]

            def syncc(s):
                kb.inc(s.dma_start(out=cstg[0][:, :], in_=wt["cos_fm"]), "LD1", 16)
                kb.inc(s.dma_start(out=cstg[1][:, :], in_=wt["sinneg_fm"]), "LD2", 16)
            b.sync(syncc)

            def vecc(v):
                kb.wait(v, "LD1", base0["LD1"] + 16)
                kb.inc(v.tensor_copy(out=cos_bf[:, :], in_=cstg[0][:, :]), "DVE")
                kb.wait(v, "LD2", base0["LD2"] + 16)
                kb.inc(v.tensor_copy(out=sn_bf[:, :], in_=cstg[1][:, :]), "DVE")
            b.vector(vecc)
            kb.barrier()

        E = 12 if use_rope else 2

        def qk_evac(v, nt, pspair):
            for tb in range(2):
                sl = slice(tb * TB, (tb + 1) * TB)
                kb.inc(v.tensor_copy(out=(rp[0][:, sl] if use_rope else qst[nt % 2][:, sl]),
                                     in_=pspair[tb][:, :]), "DVE")
                if use_rope:
                    kb.sw(v, "DVE")
                    emit_rope_tb(kb, v, rp[0][:, sl], qst[nt % 2][:, sl], rp[1][:, sl],
                                 cos_bf[:, sl], sn_bf[:, sl])

        def q_store(g, nt):
            ch = ("ST2", "ST3")[nt % 2]
            kb.inc(g.dma_start(out=q_dram[nt * 128:(nt + 1) * 128, :], in_=qst[nt % 2][:, :]),
                   ch, 16)
            if st_rec is not None:
                st_rec.append((ch, kb.cnt[ch]))

        emit_stream_proj(kb, bufs, wt["wqT"], NCH, ln, qk_evac, q_store, E, 1)
        if seam_fn is not None:
            seam_fn()

        def k_store(g, nt):
            ch = ("ST2", "ST3")[nt % 2]
            kb.inc(g.dma_start(out=kv_in_dram[nt * 128:(nt + 1) * 128, :], in_=qst[nt % 2][:, :]),
                   ch, 16)
            if st_rec is not None:
                st_rec.append((ch, kb.cnt[ch]))

        emit_stream_proj(kb, bufs, wt["wkT"], KVDIM // 128, ln, qk_evac, k_store, E, 1)
        if seam_fn is not None:
            seam_fn()

        def v_evac(v, nt, pspair):
            for tb in range(2):
                kb.inc(v.tensor_copy(out=v_fm[nt][:, tb * TB:(tb + 1) * TB], in_=pspair[tb][:, :]), "DVE")

        def v_store(g, nt):
            pass

        emit_stream_proj(kb, bufs, wt["wvT"], KVDIM // 128, ln, v_evac, v_store, 2, 0)
        if seam_fn is not None:
            seam_fn()

        with ExitStack() as sc2:
            pst = [sc2.enter_context(nc.psum_tensor(kb.uname("vt_ps%d" % i), [128, 512], BF16)) for i in range(2)]
            baseT = dict(kb.cnt)

            def peT(t):
                for tc in range(8):
                    for half in range(2):
                        if tc * 2 + half >= 2:
                            kb.wait(t, "DVE", baseT["DVE"] + (tc * 2 + half) - 1)
                        else:
                            kb.wait(t, "DVE", baseT["DVE"])
                        for dvt in range(4):
                            inst = t.transpose(pst[half][:, dvt * 128:(dvt + 1) * 128],
                                               v_fm[half * 4 + dvt][:, tc * 128:(tc + 1) * 128],
                                               bufs["ident_bf"][:, :])
                        kb.inc(inst, "PE")
            b.tensor(peT)

            def vecT(v):
                for tc in range(8):
                    for half in range(2):
                        k = tc * 2 + half
                        kb.wait(v, "PE", baseT["PE"] + k + 1)
                        if k >= 4:
                            kb.wait(v, ("ST2", "ST3")[tc % 2], baseT[("ST2", "ST3")[tc % 2]] + 16 * (tc // 2))
                        kb.inc(v.tensor_copy(out=vst[tc % 2][:, half * 512:(half + 1) * 512],
                                             in_=pst[half][:, :]), "DVE")
            b.vector(vecT)

            kv_v = kv_in_dram[1024:2048, :].rearrange("(hk p) (tc d) -> p hk tc d", p=128, d=128)

            def stT(g):
                for tc in range(8):
                    kb.wait(g, "DVE", baseT["DVE"] + tc * 2 + 2)
                    kb.inc(g.dma_start(out=kv_v[:, :, tc, :],
                                       in_=vst[tc % 2][:, :].rearrange("p (hk d) -> p hk d", hk=8)),
                           ("ST2", "ST3")[tc % 2], 16)
            b.scalar(stT)

            if tail_fn is not None:
                tail_fn()
            else:
                kb.barrier()


SCALE = 1.0 / math.sqrt(128.0)


KVROWS = 2 * KVDIM + 8


def alloc_norm_bufs(kb, sc):
    nc = kb.nc
    sbl = lambda n_, s_, d_: sc.enter_context(nc.sbuf_tensor(kb.uname(n_), s_, d_))
    return dict(qt=[sbl("nrm_qt", [128, TPC], BF16) for _ in range(2)],
                q2=[sbl("nrm_q2", [128, TPC], BF16) for _ in range(2)],
                kbf=sbl("nrm_kbf", [1, 16], BF16),
                nps=[sc.enter_context(nc.psum_tensor(kb.uname("nrm_ps"), [1, 512], F32))
                     for _ in range(2)])


def phase_qk_norms(kb, bufs, q_dram, kv_in_ap, st_sched=None, pre=None):
    nc, b = kb.nc, kb.block
    scal = bufs["scal"]

    with ExitStack() as sc:
        if pre is None:
            pre = alloc_norm_bufs(kb, sc)
        qt, q2, kbf, nps = pre["qt"], pre["q2"], pre["kbf"], pre["nps"]
        base = dict(kb.cnt)
        LDS = ("LD1", "LD2")
        tiles = [("q", i) for i in range(NCH)] + [("k", i) for i in range(KVDIM // 128)]
        NTL = len(tiles)

        def syncn(s):
            for j, (kind, i) in enumerate(tiles):
                if j >= 2:
                    kb.wait(s, "ACT", base["ACT"] + (j - 1))
                if st_sched is not None:
                    kb.wait(s, *st_sched[j])
                src = q_dram if kind == "q" else kv_in_ap
                kb.inc(s.dma_start(out=qt[j % 2][:, :], in_=src[i * 128:(i + 1) * 128, :]), LDS[j % 2], 16)
        b.sync(syncn)

        def actn(a):
            for j in range(NTL):
                kb.wait(a, LDS[j % 2], base[LDS[j % 2]] + 16 * (j // 2 + 1))
                if j >= 2:
                    kb.wait(a, "PE", base["PE"] + 2 * (j - 1))
                kb.inc(a.activation(q2[j % 2][:, :], qt[j % 2][:, :], AF.Square), "ACT")
        b.scalar(actn)

        def pen(t):
            for j in range(NTL):
                kb.wait(t, "ACT", base["ACT"] + j + 1)
                if j >= 1:
                    kb.wait(t, "DVE", base["DVE"] + 4 * j + 1)
                for tb in range(2):
                    kb.inc(t.matmul(nps[tb][:, :], bufs["ones_bf"][:, :],
                                    q2[j % 2][:, tb * TB:(tb + 1) * TB], start=True, stop=True), "PE")
        b.tensor(pen)

        def vecn(v):
            kb.inc(v.memset(scal[0:1, 0:4], 0.0), "DVE")
            for j, (kind, i) in enumerate(tiles):
                slot = 0 if kind == "q" else 1
                for tb in range(2):
                    kb.wait(v, "PE", base["PE"] + 2 * j + tb + 1)
                    kb.sw(v, "DVE")
                    kb.inc(v.reduce_max(scal[0:1, 2:3], nps[tb][:, :], axis=AX.X), "DVE")
                    kb.sw(v, "DVE")
                    kb.inc(v.tensor_tensor(scal[0:1, slot:slot + 1], scal[0:1, slot:slot + 1],
                                           scal[0:1, 2:3], ALU.max), "DVE")
            kb.sw(v, "DVE")
            kb.inc(v.tensor_copy(out=kbf[0:1, 0:1], in_=scal[0:1, 1:2]), "DVE")
        b.vector(vecn)

        def stn(g):
            kb.wait(g, "DVE", kb.cnt["DVE"])
            kb.inc(g.dma_start(out=kv_in_ap[2 * KVDIM:2 * KVDIM + 1, 0:1], in_=kbf[0:1, 0:1]), "ST0", 16)
        b.scalar(stn)
        kb.barrier()


def phase_attn(kb, bufs, q_dram, kv_g, m01T_dram, causal):
    nc, b = kb.nc, kb.block
    ctx_t = bufs["ln"]
    scal = bufs["scal"]
    phist = {}

    with ExitStack() as sc:
        sb = lambda n_, s_, d_: sc.enter_context(nc.sbuf_tensor(kb.uname(n_), s_, d_))
        pp = lambda n_, s_, d_=F32: sc.enter_context(nc.psum_tensor(kb.uname(n_), s_, d_))
        k_h = [sb("at_k", [128, SEQ], BF16) for _ in range(2)]
        v_h = [sb("at_v", [128, SEQ], BF16) for _ in range(2)]
        q_h = [sb("at_q", [128, TPC], BF16) for _ in range(2)]
        PT = [sb("at_pt", [128, 1024], BF16) for _ in range(2)]
        mskT = [sb("at_mt", [128, NKC * 512], BF16) for _ in range(2)] if causal else None
        kn_bf = sb("at_knbf", [1, 8], BF16)
        negMb = sb("at_negmb", [128, 1], F32)
        rlrow = [sb("at_rlrow", [1, 512], F32) for _ in range(2)]
        rlb_sb = [sb("at_rlb", [128, 512], F32) for _ in range(2)]
        drow = nc.dram_tensor(kb.uname("at_drow"), [2, 512], F32, kind="Internal").ap()
        st_ps = [pp("at_st", [128, 1024]) for _ in range(2)]
        ctx_ps = [pp("at_ctx", [128, 512]) for _ in range(2)]
        l_ps = pp("at_l", [1, 512])
        n2_ps = pp("at_n2", [128, 512])

        base0 = dict(kb.cnt)

        def sync0(s):
            for r in range(4):
                kb.inc(s.dma_start(out=kn_bf[0:1, r:r + 1],
                                   in_=kv_g[KVROWS * r + 2 * KVDIM:KVROWS * r + 2 * KVDIM + 1, 0:1]),
                       "LD1", 16)
        b.sync(sync0)

        def vec0(v):
            kb.wait(v, "LD1", base0["LD1"] + 64)
            kb.inc(v.tensor_copy(out=scal[0:1, 8:12], in_=kn_bf[0:1, 0:4]), "DVE")
            kb.sw(v, "DVE")
            kb.inc(v.reduce_max(scal[0:1, 1:2], scal[0:1, 8:12], axis=AX.X), "DVE")
            kb.sw(v, "DVE")
            kb.inc(v.tensor_tensor(scal[0:1, 2:3], scal[0:1, 0:1], scal[0:1, 1:2], ALU.mult), "DVE")
            kb.wait(v, "ACT", base0["ACT"] + 1)
            kb.inc(v.tensor_scalar(scal[0:1, 4:5], scal[0:1, 3:4], -SCALE, None, ALU.mult), "DVE")
        b.vector(vec0)

        def act0(a):
            kb.wait(a, "DVE", base0["DVE"] + 3)
            kb.inc(a.activation(scal[0:1, 3:4], scal[0:1, 2:3], AF.Sqrt), "ACT")
        b.scalar(act0)

        def pe0(t):
            kb.wait(t, "DVE", base0["DVE"] + 4)
            kb.inc(t.matmul(n2_ps[:, 0:1], bufs["ones_f1"][:, :], scal[0:1, 4:5],
                            start=True, stop=True), "PE")
        b.tensor(pe0)

        def vec0b(v):
            kb.wait(v, "PE", base0["PE"] + 1)
            kb.inc(v.tensor_copy(out=negMb[:, :], in_=n2_ps[:, 0:1]), "DVE")
        b.vector(vec0b)
        kb.barrier()

        if causal:
            basem = dict(kb.cnt)

            def syncm(s, basem=basem):
                for sbk in range(2):
                    kb.inc(s.dma_start(
                        out=mskT[sbk][:, :].rearrange("p (c q) -> p c q", c=NKC),
                        in_=m01T_dram[:, sbk * 512:(sbk + 1) * 512]
                        .rearrange("(c p) q -> p c q", p=128)), "LD7", 16)
            b.sync(syncm)
            mask_ld = kb.cnt["LD7"]
        else:
            mask_ld = 0

        pend = {"v": None}
        blkctr = {"n": 0}

        def emit_bcast_dma(g):
            if pend["v"] is None:
                return
            cpar = pend["v"][3]
            kb.wait(g, "DVE", pend["v"][2])
            kb.inc(g.dma_start(out=drow[cpar:cpar + 1, :], in_=rlrow[cpar][0:1, :]), "LD4", 16)
            kb.wait(g, "LD4", kb.cnt["LD4"])
            kb.inc(g.dma_start(out=rlb_sb[cpar][:, :],
                               in_=drow[cpar:cpar + 1, :].broadcast_to([128, 512])), "LD4", 16)
            pend["v"] = pend["v"] + (kb.cnt["LD4"],)

        def emit_partB(v):
            ph, psb, _, cpar, ld4 = pend["v"]
            kb.wait(v, "LD4", ld4)
            kb.inc(v.tensor_tensor(ctx_t[ph][:, psb * 512:(psb + 1) * 512],
                                   ctx_ps[cpar][:, :], rlb_sb[cpar][:, :], ALU.mult), "DVE")
            pend["v"] = None

        for h in range(32):
            hk = h // 4
            hist = phist.get(h - 2, 0)
            baseh = dict(kb.cnt)
            LDP = ("LD5", "LD6")[h % 2]

            def syncl(s, h=h, hk=hk, LDP=LDP, hist=hist):
                kb.wait(s, "PE", hist)
                for r in range(4):
                    kb.inc(s.dma_start(out=k_h[h % 2][:, r * 1024:(r + 1) * 1024],
                                       in_=kv_g[KVROWS * r + hk * 128:KVROWS * r + (hk + 1) * 128, :]),
                           LDP, 16)
                    kb.inc(s.dma_start(out=v_h[h % 2][:, r * 1024:(r + 1) * 1024],
                                       in_=kv_g[KVROWS * r + 1024 + hk * 128:KVROWS * r + 1024 + (hk + 1) * 128, :]),
                           LDP, 16)
                kb.inc(s.dma_start(out=q_h[h % 2][:, :], in_=q_dram[h * 128:(h + 1) * 128, :]),
                       LDP, 16)
            b.sync(syncl)

            for sbk in range(2):
                base = dict(kb.cnt)
                cpar = blkctr["n"] % 2
                blkctr["n"] += 1
                had_pend = pend["v"] is not None
                if had_pend:
                    def poolB(g):
                        emit_bcast_dma(g)
                    b.sync(poolB)
                act_exp = lambda p, base=base: base["ACT"] + p + 1

                def pe_s_cnt(kc, base=base):
                    p, hf = divmod(kc, 2)
                    return base["PE"] + (2 * p + hf + 1 if p < 2 else 3 * p - 1 + hf)

                def pe_lp_cnt(j, base=base):
                    return base["PE"] + (3 * j + 7 if j <= 13 else 33 + j)

                def dve_m(j, base=base):
                    return base["DVE"] + j + 1

                def peA(t, h=h, sbk=sbk, base=base, LDP=LDP, baseh=baseh,
                        had_pend=had_pend, cpar=cpar):
                    def lavpair(j):
                        if causal:
                            kb.wait(t, "DVE", dve_m(j))
                        else:
                            if j == 0:
                                kb.wait(t, "DVE", base["DVE"])
                            kb.wait(t, "ACT", act_exp(j))
                        for hf in range(2):
                            gk = 2 * j + hf
                            t.matmul(l_ps[:, :], bufs["ones_bf"][:, :],
                                     PT[j % 2][:, hf * 512:(hf + 1) * 512],
                                     start=(gk == 0), stop=(gk == 31))
                            inst = t.matmul(ctx_ps[cpar][:, :], v_h[h % 2][:, gk * 128:(gk + 1) * 128],
                                            PT[j % 2][:, hf * 512:(hf + 1) * 512],
                                            start=(gk == 0), stop=(gk == 31))
                        kb.inc(inst, "PE")

                    for p in range(16):
                        for hf in range(2):
                            kc = 2 * p + hf
                            if kc == 0:
                                if sbk == 0:
                                    kb.wait(t, LDP, baseh[LDP] + 16 * 9)
                                kb.wait(t, "ACT", base["ACT"])
                            if kc >= 4 and hf == 0:
                                kb.wait(t, "ACT", act_exp(p - 2))
                            inst = t.matmul(st_ps[p % 2][:, hf * 512:(hf + 1) * 512],
                                            k_h[h % 2][:, kc * 128:(kc + 1) * 128],
                                            q_h[h % 2][:, sbk * 512:(sbk + 1) * 512],
                                            start=True, stop=True)
                            kb.inc(inst, "PE")
                        if p >= 2:
                            lavpair(p - 2)
                    lavpair(14)
                    lavpair(15)
                b.tensor(peA)

                def actA(a_, h=h, sbk=sbk, base=base):
                    for p in range(16):
                        kb.wait(a_, "PE", pe_s_cnt(2 * p + 1))
                        if p >= 2:
                            kb.wait(a_, "PE", pe_lp_cnt(p - 2))
                        kb.inc(a_.activation(PT[p % 2][:, :], st_ps[p % 2][:, :], AF.Exp,
                                             bias=negMb[:, 0:1], scale=SCALE), "ACT")
                b.scalar(actA)

                def vecB(v, h=h, sbk=sbk, base=base, had_pend=had_pend, cpar=cpar):
                    if causal:
                        for p in range(16):
                            if p == 0:
                                kb.wait(v, "LD7", mask_ld)
                            kb.wait(v, "ACT", act_exp(p))
                            kb.inc(v.tensor_tensor(PT[p % 2][:, :], PT[p % 2][:, :],
                                                   mskT[sbk][:, p * 1024:(p + 1) * 1024], ALU.mult), "DVE")
                    if had_pend:
                        emit_partB(v)
                    kb.wait(v, "PE", pe_lp_cnt(15))
                    if blkctr["n"] >= 3:
                        kb.wait(v, "LD4", kb.cnt["LD4"])
                    kb.inc(v.tensor_copy(out=rlrow[cpar][0:1, :], in_=l_ps[0:1, :]), "DVE")
                    kb.sw(v, "DVE")
                    kb.inc(v.reciprocal(rlrow[cpar][0:1, :], rlrow[cpar][0:1, :]), "DVE")
                    pend["v"] = (h, sbk, kb.cnt["DVE"], cpar)
                b.vector(vecB)

                assert kb.cnt["PE"] == base["PE"] + 48, (kb.cnt["PE"], base["PE"])
                assert kb.cnt["ACT"] == base["ACT"] + 16
            phist[h] = kb.cnt["PE"]

        def pool_tail(g):
            emit_bcast_dma(g)
        b.sync(pool_tail)

        def vec_tail(v):
            emit_partB(v)
        b.vector(vec_tail)
        kb.barrier()



GATE_SLOTS = 11
DOWN_SLOTS = 4
GATE_TILE_ROWS = 128 * NCH
DOWN_TILE_ROWS = 128 * NIT
GU_BLOCK_ROWS = 2 * GATE_SLOTS * GATE_TILE_ROWS
DN_BLOCK_ROWS = DOWN_SLOTS * DOWN_TILE_ROWS


def deq_tile_owner(m, nt):
    if m == "gate" or m == "up":
        r, j = nt // GATE_SLOTS, nt % GATE_SLOTS
        off = (0 if m == "gate" else GATE_SLOTS * GATE_TILE_ROWS) + j * GATE_TILE_ROWS
    else:
        r, j = nt // DOWN_SLOTS, nt % DOWN_SLOTS
        off = j * DOWN_TILE_ROWS
    return r, off


def core_deq_tiles(rank, m):
    if m in ("gate", "up"):
        return [min(rank * GATE_SLOTS + j, 85) for j in range(GATE_SLOTS)]
    return [rank * DOWN_SLOTS + j for j in range(DOWN_SLOTS)]




class DeqEmitter:

    def __init__(self, kb, bufs):
        self.kb, self.bufs, self.nc = kb, bufs, kb.nc
        self.maskI = self.maskH = None
        self.mats = {}
        self.g = None
        self.zscope = None

    def prep_matrix(self, m, qwT, sT, qz, out4d, K, N, glist):
        kb, nc, b = self.kb, self.nc, self.kb.block
        G = max(glist) + 1
        if self.zscope is None:
            self.zscope = ExitStack()
        ctx = self.zscope
        if self.maskI is None:
            self.maskI = kb.ctx.enter_context(nc.sbuf_tensor(kb.uname("dq_mask"), [128, 1], I32))
            self.maskH = kb.ctx.enter_context(nc.sbuf_tensor(kb.uname("dq_maskh"), [128, 1], I32))

            def v0(v):
                kb.inc(v.memset(self.maskI[:, :], 0x0F0F0F0F), "DVE")
                kb.inc(v.memset(self.maskH[:, :], -252645136), "DVE")
            b.vector(v0)
        qz_sb = ctx.enter_context(nc.sbuf_tensor(kb.uname("dq_qz"), [G, N // 8], I32))
        z1f = ctx.enter_context(nc.sbuf_tensor(kb.uname("dq_z1"), [G, N], F32))
        zl = ctx.enter_context(nc.sbuf_tensor(kb.uname("dq_zl"), [G, N // 8], I32))
        zh = ctx.enter_context(nc.sbuf_tensor(kb.uname("dq_zh"), [G, N // 8], I32))

        def s0(s):
            kb.inc(s.dma_start(out=qz_sb[:, :], in_=qz), "LD1", 16)
        b.sync(s0)
        ld1 = kb.cnt["LD1"]
        maskI, maskH = self.maskI, self.maskH

        def v1(v):
            kb.wait(v, "LD1", ld1)
            kb.sw(v, "DVE")
            kb.inc(v.tensor_scalar(zl[:, :], qz_sb[:, :], maskI[0:G, 0:1], None, ALU.bitwise_and), "DVE")
            kb.inc(v.tensor_scalar(zh[:, :], qz_sb[:, :], maskH[0:G, 0:1], None, ALU.bitwise_and), "DVE")
            kb.sw(v, "DVE")
            z1v = z1f[:, :].rearrange("g (nb tk) -> g nb tk", nb=N // 128)
            kb.inc(v.tensor_copy(out=z1v[:, :, 0:64], in_=zl[:, :].bitcast(U8)), "DVE")
            kb.inc(v.tensor_scalar(z1v[:, :, 64:128], zh[:, :].bitcast(U8), 0.0625, None, ALU.mult), "DVE")
        b.vector(v1)
        self.mats[m] = dict(qwT=qwT, sT=sT, out4d=out4d, K=K, G=G, glist=glist,
                            z1f=z1f, prep_dve=kb.cnt["DVE"])

    def group_open(self, K, Gmax, tiles, nbuf=2):
        kb, nc = self.kb, self.nc
        sc = ExitStack()
        sbl = lambda n_, s_, d_: sc.enter_context(nc.sbuf_tensor(kb.uname(n_), s_, d_))
        mk2 = lambda n_, s_, d_: [sbl(n_, s_, d_) for _ in range(nbuf)] * (3 - nbuf)
        self.g = dict(
            sc=sc, K=K, KC=K // 128, tiles=tiles, i=0, e=0, nbuf=nbuf,
            qw=[sbl("dq_qw", [128, K // 8], I32) for _ in range(2)],
            s=[sbl("dq_s", [128, Gmax], F32) for _ in range(2)],
            lo=mk2("dq_lo", [128, K // 8], I32),
            hi=mk2("dq_hi", [128, K // 8], I32),
            nz=mk2("dq_nz", [128, Gmax], F32),
            s16=mk2("dq_s16", [128, Gmax], F32),
            tmp=sbl("dq_tmp", [128, Gmax], F32),
            w4T=mk2("dq_w4T", [128, K], BF16),
            wt=[sbl("dq_wt", [128, 512], BF16) for _ in range(2)],
            psz=sc.enter_context(nc.psum_tensor(kb.uname("dq_psz"), [128, Gmax], F32)),
            ptp=sc.enter_context(nc.psum_tensor(kb.uname("dq_ptp"), [128, 512], BF16)),
            entry=dict(kb.cnt), dve_base=[], act_base=[], pe_base=[],
            dve_end=[], act_end=[], pe_end=[], ld=[], st=[], evac=[])

    def emit_tiles(self, n):
        kb, b = self.kb, self.kb.block
        g = self.g
        KC, NEV_all = g["KC"], None
        LDS, STS = ("LD2", "LD3"), ("ST2", "ST3")
        for _ in range(n):
            i = g["i"]
            m, nt = g["tiles"][i]
            mat = self.mats[m]
            glist, G = mat["glist"], mat["G"]
            groups = [(g0, min(4, KC - g0)) for g0 in range(0, KC, 4)]
            NEV = len(groups)
            assert KC % 2 == 0
            base = dict(kb.cnt)
            g["dve_base"].append(base["DVE"])
            g["act_base"].append(base["ACT"])
            g["pe_base"].append(base["PE"])
            dve_end = base["DVE"] + 5 + KC + NEV
            act_end = base["ACT"] + KC
            pe_end = base["PE"] + 1 + NEV
            qw, s_sb, lo, hi = g["qw"], g["s"], g["lo"], g["hi"]
            nz, s16, tmp = g["nz"], g["s16"], g["tmp"]
            w4T, wt, psz, ptp = g["w4T"], g["wt"], g["psz"], g["ptp"]
            e0 = g["e"]
            snap = {"ST2": kb.cnt["ST2"], "ST3": kb.cnt["ST3"]}
            for ev in range(NEV):
                g["evac"].append(base["DVE"] + 5 + KC + ev + 1)
                ch = STS[(e0 + ev) % 2]
                snap[ch] += 16
                g["st"].append(snap[ch])

            def syncp(s, i=i, m=m, nt=nt, mat=mat, base=base):
                if i >= 1:
                    kb.wait(s, "DVE", g["dve_end"][i - 1])
                if i >= 2:
                    kb.wait(s, "ACT", g["act_end"][i - 2])
                kb.inc(s.dma_start(out=qw[i % 2][:, :], in_=mat["qwT"][nt * 128:(nt + 1) * 128, :]), LDS[i % 2], 16)
                kb.inc(s.dma_start(out=s_sb[i % 2][:, 0:G], in_=mat["sT"][nt * 128:(nt + 1) * 128, :]), LDS[i % 2], 16)
            b.sync(syncp)
            g["ld"].append(kb.cnt[LDS[i % 2]])

            def pep(t, i=i, mat=mat, base=base, groups=groups, NEV=NEV, G=G):
                kb.wait(t, "DVE", mat["prep_dve"])
                if i >= 1:
                    kb.wait(t, "DVE", g["dve_base"][i - 1] + 1)
                kb.inc(t.transpose(psz[0:128, 0:G], mat["z1f"][0:G, (g["tiles"][i][1]) * 128:(g["tiles"][i][1] + 1) * 128],
                                   self.bufs["ident_f"][0:G, 0:G]), "PE")
                kb.wait(t, "DVE", base["DVE"] + 5 + KC)
                kb.wait(t, "ACT", base["ACT"] + KC)
                for ev, (g0, gn) in enumerate(groups):
                    e = e0 + ev
                    if e >= 1:
                        kb.wait(t, "DVE", g["evac"][e - 1])
                    for c4 in range(gn):
                        kc = g0 + c4
                        inst = t.transpose(ptp[:, c4 * 128:(c4 + 1) * 128],
                                           w4T[i % 2][:, kc * 128:(kc + 1) * 128],
                                           self.bufs["ident_bf"][:, :])
                    kb.inc(inst, "PE")
            b.tensor(pep)

            def vecp(v, i=i, base=base, groups=groups, NEV=NEV, G=G, glist=glist):
                kb.wait(v, LDS[i % 2], g["ld"][i])
                kb.wait(v, "PE", base["PE"] + 1)
                if g["nbuf"] == 1 and i >= 1:
                    kb.wait(v, "ACT", g["act_end"][i - 1])
                    kb.wait(v, "PE", g["pe_end"][i - 1])
                elif i >= 2:
                    kb.wait(v, "ACT", g["act_end"][i - 2])
                kb.inc(v.tensor_scalar(tmp[:, 0:G], psz[0:128, 0:G], -1.0, -1.0, ALU.mult, ALU.add), "DVE")
                kb.sw(v, "DVE")
                kb.inc(v.tensor_tensor(nz[i % 2][:, 0:G], tmp[:, 0:G], s_sb[i % 2][:, 0:G], ALU.mult), "DVE")
                kb.inc(v.tensor_scalar(s16[i % 2][:, 0:G], s_sb[i % 2][:, 0:G], 0.0625, None, ALU.mult), "DVE")
                kb.inc(v.tensor_scalar(lo[i % 2][:, :], qw[i % 2][:, :], self.maskI[:, 0:1], None, ALU.bitwise_and), "DVE")
                kb.inc(v.tensor_scalar(hi[i % 2][:, :], qw[i % 2][:, :], self.maskH[:, 0:1], None, ALU.bitwise_and), "DVE")
                kb.sw(v, "DVE")
                for c in range(0, KC, 2):
                    gg = glist[c]
                    kb.inc(v.tensor_scalar(w4T[i % 2][:, c * 128:c * 128 + 64],
                                           lo[i % 2][:, :].bitcast(U8)[:, c * 64:(c + 1) * 64],
                                           s_sb[i % 2][:, gg:gg + 1], nz[i % 2][:, gg:gg + 1],
                                           ALU.mult, ALU.add), "DVE")
                    kb.inc(v.tensor_scalar(w4T[i % 2][:, c * 128 + 64:(c + 1) * 128],
                                           hi[i % 2][:, :].bitcast(U8)[:, c * 64:(c + 1) * 64],
                                           s16[i % 2][:, gg:gg + 1], nz[i % 2][:, gg:gg + 1],
                                           ALU.mult, ALU.add), "DVE")
                for ev, (g0, gn) in enumerate(groups):
                    e = e0 + ev
                    kb.wait(v, "PE", base["PE"] + 1 + ev + 1)
                    if e >= 2:
                        kb.wait(v, STS[e % 2], g["st"][e - 2])
                    kb.inc(v.tensor_copy(out=wt[e % 2][:, 0:gn * 128],
                                         in_=ptp[:, 0:gn * 128]), "DVE")
                    assert kb.cnt["DVE"] == g["evac"][e]
            b.vector(vecp)

            def actp(a, i=i, base=base, glist=glist):
                kb.wait(a, "DVE", base["DVE"] + 5)
                if i >= 1:
                    kb.wait(a, "PE", g["pe_end"][i - 1])
                for c in range(1, KC, 2):
                    gg = glist[c]
                    kb.inc(a.activation(w4T[i % 2][:, c * 128:c * 128 + 64],
                                        lo[i % 2][:, :].bitcast(U8)[:, c * 64:(c + 1) * 64],
                                        AF.Identity, bias=nz[i % 2][:, gg:gg + 1],
                                        scale=s_sb[i % 2][:, gg:gg + 1]), "ACT")
                    kb.inc(a.activation(w4T[i % 2][:, c * 128 + 64:(c + 1) * 128],
                                        hi[i % 2][:, :].bitcast(U8)[:, c * 64:(c + 1) * 64],
                                        AF.Identity, bias=nz[i % 2][:, gg:gg + 1],
                                        scale=s16[i % 2][:, gg:gg + 1]), "ACT")
            b.scalar(actp)

            def stp(gq_, i=i, nt=nt, mat=mat, base=base, groups=groups):
                for ev, (g0, gn) in enumerate(groups):
                    e = e0 + ev
                    kb.wait(gq_, "DVE", g["evac"][e])
                    kb.inc(gq_.dma_start(
                        out=mat["out4d"][nt, :, g0:g0 + gn, :],
                        in_=wt[e % 2][:, 0:gn * 128].rearrange("p (c n) -> p c n", c=gn)),
                        STS[e % 2], 16)
                    assert kb.cnt[STS[e % 2]] == g["st"][e]
            b.gpsimd(stp)

            assert kb.cnt["DVE"] == dve_end and kb.cnt["ACT"] == act_end and kb.cnt["PE"] == pe_end
            g["dve_end"].append(dve_end)
            g["act_end"].append(act_end)
            g["pe_end"].append(pe_end)
            g["e"] = e0 + NEV
            g["i"] = i + 1

    def group_close(self, close_z=True):
        kb, b = self.kb, self.kb.block
        g = self.g
        fin = dict(kb.cnt)
        st2, st3 = fin["ST2"], fin["ST3"]
        dve_f, act_f, pe_f = fin["DVE"], fin["ACT"], fin["PE"]

        def gsync(s):
            kb.wait(s, "ST2", st2); kb.wait(s, "ST3", st3)
            kb.wait(s, "DVE", dve_f); kb.wait(s, "ACT", act_f); kb.wait(s, "PE", pe_f)
        b.sync(gsync)

        def gvec(v):
            kb.wait(v, "ST2", st2); kb.wait(v, "ST3", st3)
            kb.wait(v, "ACT", act_f); kb.wait(v, "PE", pe_f)
        b.vector(gvec)

        def gact(a):
            kb.wait(a, "ST2", st2); kb.wait(a, "ST3", st3)
            kb.wait(a, "DVE", dve_f); kb.wait(a, "PE", pe_f)
        b.scalar(gact)

        def gpe(t):
            kb.wait(t, "DVE", dve_f); kb.wait(t, "ACT", act_f)
        b.tensor(gpe)
        g["sc"].close()
        if close_z and self.zscope is not None:
            self.zscope.close()
            self.zscope = None
        self.g = None



def phase_gptq_dequant_matrix(kb, bufs, qwT, sT, qz, out_dram, K, N, glist):
    nc, b = kb.nc, kb.block
    G = len(set(glist))
    G = max(glist) + 1
    NT = N // 128
    KC = K // 128

    with ExitStack() as sc:
        sbl = lambda n_, s_, d_: sc.enter_context(nc.sbuf_tensor(kb.uname(n_), s_, d_))
        qz_sb = sbl("gq_qz", [G, N // 8], I32)
        z1f = sbl("gq_z1", [G, N], F32)
        zl = sbl("gq_zl", [G, N // 8], I32)
        zh = sbl("gq_zh", [G, N // 8], I32)
        maskI = sbl("gq_mask", [128, 1], I32)
        maskH = sbl("gq_maskh", [128, 1], I32)
        qw_sb = [sbl("gq_qw", [128, K // 8], I32) for _ in range(2)]
        s_sb = [sbl("gq_s", [128, G], F32) for _ in range(2)]
        lo_i = [sbl("gq_lo", [128, K // 8], I32) for _ in range(2)]
        hi_i = [sbl("gq_hi", [128, K // 8], I32) for _ in range(2)]
        nz_sb = [sbl("gq_nz", [128, G], F32) for _ in range(2)]
        s16_sb = [sbl("gq_s16", [128, G], F32) for _ in range(2)]
        tmp_sb = sbl("gq_tmp", [128, G], F32)
        w4T = [sbl("gq_w4T", [128, K], BF16) for _ in range(2)]
        wt_sb = [sbl("gq_wt", [128, 512], BF16) for _ in range(2)]

        base0 = dict(kb.cnt)

        def sync0(s):
            kb.inc(s.dma_start(out=qz_sb[:, :], in_=qz), "LD1", 16)
        b.sync(sync0)

        def vec0(v):
            kb.inc(v.memset(maskI[:, :], 0x0F0F0F0F), "DVE")
            kb.inc(v.memset(maskH[:, :], -252645136), "DVE")
            kb.wait(v, "LD1", base0["LD1"] + 16)
            kb.sw(v, "DVE")
            kb.inc(v.tensor_scalar(zl[:, :], qz_sb[:, :], maskI[0:G, 0:1], None, ALU.bitwise_and), "DVE")
            kb.inc(v.tensor_scalar(zh[:, :], qz_sb[:, :], maskH[0:G, 0:1], None, ALU.bitwise_and), "DVE")
            kb.sw(v, "DVE")
            z1v = z1f[:, :].rearrange("g (nb tk) -> g nb tk", nb=N // 128)
            kb.inc(v.tensor_copy(out=z1v[:, :, 0:64], in_=zl[:, :].bitcast(U8)), "DVE")
            kb.inc(v.tensor_scalar(z1v[:, :, 64:128], zh[:, :].bitcast(U8), 0.0625, None, ALU.mult), "DVE")
        b.vector(vec0)
        kb.barrier()

        basep = dict(kb.cnt)
        LDS = ("LD2", "LD3")
        STS = ("ST2", "ST3")
        groups = [(g0, min(4, KC - g0)) for g0 in range(0, KC, 4)]
        NEV = len(groups)
        assert KC % 2 == 0
        E_TILE = 2 + 1 + 2 + KC + NEV
        A_TILE = KC
        dve_tile_end = lambda i: basep["DVE"] + (i + 1) * E_TILE
        act_tile_end = lambda i: basep["ACT"] + (i + 1) * A_TILE
        P_TILE = 1 + NEV
        pe_tile_end = lambda i: basep["PE"] + (i + 1) * P_TILE
        dve_evac_cnt = lambda i, ev: dve_tile_end(i) - NEV + ev + 1

        def syncp(s):
            for i in range(NT):
                if i >= 2:
                    kb.wait(s, "DVE", dve_tile_end(i - 1))
                    kb.wait(s, "ACT", act_tile_end(i - 2))
                kb.inc(s.dma_start(out=qw_sb[i % 2][:, :], in_=qwT[i * 128:(i + 1) * 128, :]), LDS[i % 2], 16)
                kb.inc(s.dma_start(out=s_sb[i % 2][:, :], in_=sT[i * 128:(i + 1) * 128, :]), LDS[i % 2], 16)
        b.sync(syncp)

        with ExitStack() as scp:
            psz = [scp.enter_context(nc.psum_tensor(kb.uname("gq_psz"), [128, G], F32)) for _ in range(2)]
            ptp = [scp.enter_context(nc.psum_tensor(kb.uname("gq_ptp"), [128, 512], BF16)) for _ in range(2)]

            def pep(t):
                for i in range(NT):
                    if i >= 2:
                        kb.wait(t, "DVE", dve_tile_end(i - 2) + 2)
                    inst = t.transpose(psz[i % 2][:, :],
                                       z1f[0:G, i * 128:(i + 1) * 128],
                                       bufs["ident_f"][0:G, 0:G])
                    kb.inc(inst, "PE")
                    kb.wait(t, "DVE", dve_tile_end(i) - NEV)
                    kb.wait(t, "ACT", act_tile_end(i))
                    for ev, (g0, gn) in enumerate(groups):
                        e = i * NEV + ev
                        if e >= 2:
                            pi, pev = divmod(e - 2, NEV)
                            kb.wait(t, "DVE", dve_evac_cnt(pi, pev))
                        for c4 in range(gn):
                            kc = g0 + c4
                            inst = t.transpose(ptp[e % 2][:, c4 * 128:(c4 + 1) * 128],
                                               w4T[i % 2][:, kc * 128:(kc + 1) * 128],
                                               bufs["ident_bf"][:, :])
                        kb.inc(inst, "PE")
            b.tensor(pep)

            def vecp(v):
                for i in range(NT):
                    kb.wait(v, LDS[i % 2], basep[LDS[i % 2]] + 32 * (i // 2 + 1))
                    kb.wait(v, "PE", pe_tile_end(i) - NEV)
                    if i >= 2:
                        kb.wait(v, "ACT", act_tile_end(i - 2))
                    kb.inc(v.tensor_scalar(tmp_sb[:, :], psz[i % 2][:, :], -1.0, -1.0, ALU.mult, ALU.add), "DVE")
                    kb.sw(v, "DVE")
                    kb.inc(v.tensor_tensor(nz_sb[i % 2][:, :], tmp_sb[:, :], s_sb[i % 2][:, :], ALU.mult), "DVE")
                    kb.inc(v.tensor_scalar(s16_sb[i % 2][:, :], s_sb[i % 2][:, :], 0.0625, None, ALU.mult), "DVE")
                    kb.inc(v.tensor_scalar(lo_i[i % 2][:, :], qw_sb[i % 2][:, :], maskI[:, 0:1], None, ALU.bitwise_and), "DVE")
                    kb.inc(v.tensor_scalar(hi_i[i % 2][:, :], qw_sb[i % 2][:, :], maskH[:, 0:1], None, ALU.bitwise_and), "DVE")
                    kb.sw(v, "DVE")
                    for c in range(0, KC, 2):
                        g = glist[c]
                        kb.inc(v.tensor_scalar(w4T[i % 2][:, c * 128:c * 128 + 64],
                                               lo_i[i % 2][:, :].bitcast(U8)[:, c * 64:(c + 1) * 64],
                                               s_sb[i % 2][:, g:g + 1], nz_sb[i % 2][:, g:g + 1],
                                               ALU.mult, ALU.add), "DVE")
                        kb.inc(v.tensor_scalar(w4T[i % 2][:, c * 128 + 64:(c + 1) * 128],
                                               hi_i[i % 2][:, :].bitcast(U8)[:, c * 64:(c + 1) * 64],
                                               s16_sb[i % 2][:, g:g + 1], nz_sb[i % 2][:, g:g + 1],
                                               ALU.mult, ALU.add), "DVE")
                    for ev, (g0, gn) in enumerate(groups):
                        e = i * NEV + ev
                        kb.wait(v, "PE", pe_tile_end(i) - NEV + ev + 1)
                        if e >= 2:
                            kb.wait(v, STS[e % 2], basep[STS[e % 2]] + 16 * (e // 2))
                        kb.inc(v.tensor_copy(out=wt_sb[e % 2][:, 0:gn * 128],
                                             in_=ptp[e % 2][:, 0:gn * 128]), "DVE")
                    assert kb.cnt["DVE"] == dve_tile_end(i), (i, kb.cnt["DVE"], dve_tile_end(i))
            b.vector(vecp)

            def actp(a):
                for i in range(NT):
                    kb.wait(a, "DVE", dve_tile_end(i) - NEV - KC)
                    if i >= 1:
                        kb.wait(a, "PE", pe_tile_end(i - 1))
                    for c in range(1, KC, 2):
                        g = glist[c]
                        kb.inc(a.activation(w4T[i % 2][:, c * 128:c * 128 + 64],
                                            lo_i[i % 2][:, :].bitcast(U8)[:, c * 64:(c + 1) * 64],
                                            AF.Identity, bias=nz_sb[i % 2][:, g:g + 1],
                                            scale=s_sb[i % 2][:, g:g + 1]), "ACT")
                        kb.inc(a.activation(w4T[i % 2][:, c * 128 + 64:(c + 1) * 128],
                                            hi_i[i % 2][:, :].bitcast(U8)[:, c * 64:(c + 1) * 64],
                                            AF.Identity, bias=nz_sb[i % 2][:, g:g + 1],
                                            scale=s16_sb[i % 2][:, g:g + 1]), "ACT")
                    assert kb.cnt["ACT"] == act_tile_end(i)
            b.scalar(actp)

            def stp(g):
                for i in range(NT):
                    for ev, (g0, gn) in enumerate(groups):
                        e = i * NEV + ev
                        kb.wait(g, "DVE", dve_evac_cnt(i, ev))
                        kb.inc(g.dma_start(
                            out=out_dram[i, :, g0:g0 + gn, :],
                            in_=wt_sb[e % 2][:, 0:gn * 128].rearrange("p (c n) -> p c n", c=gn)),
                            STS[e % 2], 16)
            b.gpsimd(stp)

        kb.barrier()


def phase_gptq_ffn(kb, bufs, Wg_bf, Wu_bf, Wd_bf, x_prev_ap, x_next_ap, hh1_dram,
                   cc2_wait=0):
    nc, b = kb.nc, kb.block
    ln = bufs["ln"]
    if not callable(Wg_bf):
        _g, _u, _d = Wg_bf, Wu_bf, Wd_bf
        Wg_bf = lambda nt: _g[nt, :, :, :]
        Wu_bf = lambda nt: _u[nt, :, :, :]
        Wd_bf = lambda nt: _d[nt, :, :, :]

    with ExitStack() as sc:
        hh0 = sc.enter_context(nc.sbuf_tensor(kb.uname("gf_hh0"), [128, NIT * 512], BF16))

        with ExitStack() as s1:
            gstr = [s1.enter_context(nc.sbuf_tensor(kb.uname("gf_gs"), [128, D], BF16)) for _ in range(2)]
            ustr = [s1.enter_context(nc.sbuf_tensor(kb.uname("gf_us"), [128, D], BF16)) for _ in range(2)]
            hg = [[s1.enter_context(nc.sbuf_tensor(kb.uname("gf_hg"), [128, 512], BF16))
                   for _ in range(2)] for _ in range(2)]
            h1s = [s1.enter_context(nc.sbuf_tensor(kb.uname("gf_h1s"), [128, 512], BF16)) for _ in range(2)]
            ps_g = [[s1.enter_context(nc.psum_tensor(kb.uname("gf_psg"), [128, 512], F32))
                     for _ in range(2)] for _ in range(2)]
            ps_u = [[s1.enter_context(nc.psum_tensor(kb.uname("gf_psu"), [128, 512], F32))
                     for _ in range(2)] for _ in range(2)]
            base = dict(kb.cnt)
            LDS = ("LD2", "LD3")
            STS = ("ST2", "ST3")

            def syncg(s, base=base):
                for i in range(NIT):
                    if i == 0 and cc2_wait:
                        kb.wait(s, "CC2", cc2_wait)
                    if i >= 2:
                        kb.wait(s, "PE", base["PE"] + (i - 1))
                    kb.inc(s.dma_start(out=gstr[i % 2][:, :], in_=Wg_bf(i)), LDS[i % 2], 16)
                    kb.inc(s.dma_start(out=ustr[i % 2][:, :], in_=Wu_bf(i)), LDS[i % 2], 16)
            b.sync(syncg)

            def peg(t, base=base):
                for i in range(NIT):
                    p = i % 2
                    kb.wait(t, LDS[p], base[LDS[p]] + 32 * (i // 2 + 1))
                    if i >= 2:
                        kb.wait(t, "ACT", base["ACT"] + 2 * (i - 1))
                        kb.wait(t, "DVE", base["DVE"] + 2 * (i - 1))
                    for ch in range(NCH):
                        for tb in range(2):
                            t.matmul(ps_g[p][tb][:, :], gstr[p][:, ch * 128:(ch + 1) * 128],
                                     ln[ch][:, tb * TB:(tb + 1) * TB],
                                     start=(ch == 0), stop=(ch == NCH - 1))
                        for tb in range(2):
                            inst = t.matmul(ps_u[p][tb][:, :], ustr[p][:, ch * 128:(ch + 1) * 128],
                                            ln[ch][:, tb * TB:(tb + 1) * TB],
                                            start=(ch == 0), stop=(ch == NCH - 1))
                    kb.inc(inst, "PE")
            b.tensor(peg)

            def actg(a, base=base):
                for i in range(NIT):
                    p = i % 2
                    kb.wait(a, "PE", base["PE"] + i + 1)
                    if i >= 2:
                        kb.wait(a, "DVE", base["DVE"] + 2 * (i - 1))
                    kb.inc(a.activation(hg[p][0][:, :], ps_g[p][0][:, :], AF.Silu), "ACT")
                    kb.inc(a.activation(hg[p][1][:, :], ps_g[p][1][:, :], AF.Silu), "ACT")
            b.scalar(actg)

            def vecg(v, base=base):
                for i in range(NIT):
                    p = i % 2
                    kb.wait(v, "ACT", base["ACT"] + 2 * (i + 1))
                    kb.inc(v.tensor_tensor(hh0[:, i * 512:(i + 1) * 512], hg[p][0][:, :],
                                           ps_u[p][0][:, :], ALU.mult), "DVE")
                    if i >= 2:
                        kb.wait(v, STS[p], base[STS[p]] + 16 * (i // 2))
                    kb.inc(v.tensor_tensor(h1s[p][:, :], hg[p][1][:, :],
                                           ps_u[p][1][:, :], ALU.mult), "DVE")
            b.vector(vecg)

            def stg(g, base=base):
                for i in range(NIT):
                    p = i % 2
                    kb.wait(g, "DVE", base["DVE"] + 2 * (i + 1))
                    kb.inc(g.dma_start(out=hh1_dram[i * 128:(i + 1) * 128, :],
                                       in_=h1s[p][:, :]), STS[p], 16)
            b.gpsimd(stg)
            kb.barrier()

        with ExitStack() as s2:
            hh1x = s2.enter_context(nc.sbuf_tensor(kb.uname("gf_hh1x"), [128, (NIT - 64) * 512], BF16))
            dstr = [s2.enter_context(nc.sbuf_tensor(kb.uname("gf_ds"), [128, 43 * 128], BF16))
                    for _ in range(2)]
            xr = [s2.enter_context(nc.sbuf_tensor(kb.uname("gf_xr"), [128, TPC], F32)) for _ in range(2)]
            ps_d = [[s2.enter_context(nc.psum_tensor(kb.uname("gf_psd"), [128, 512], F32))
                     for _ in range(2)] for _ in range(2)]
            base = dict(kb.cnt)
            LDS = ("LD2", "LD3")
            XLD = ("LD8", "LD9")
            STS = ("ST2", "ST3")

            def hh1sb(kc):
                if kc < 64:
                    return ln[kc // 2][:, (kc % 2) * 512:(kc % 2) * 512 + 512]
                return hh1x[:, (kc - 64) * 512:(kc - 63) * 512]

            def syncd(s, base=base):
                for kc in range(NIT):
                    kb.inc(s.dma_start(out=hh1sb(kc), in_=hh1_dram[kc * 128:(kc + 1) * 128, :]),
                           "LD7", 16)
                for i in range(NCH):
                    for hf in range(2):
                        if i >= 1:
                            kb.wait(s, "PE", base["PE"] + 2 * (i - 1) + hf + 1)
                        kb.inc(s.dma_start(
                            out=dstr[hf][:, :].rearrange("p (c n) -> p c n", c=43),
                            in_=Wd_bf(i)[:, hf * 43:(hf + 1) * 43, :]), LDS[hf], 16)
                    if i >= 2:
                        kb.wait(s, STS[i % 2], base[STS[i % 2]] + 16 * (i // 2))
                    kb.inc(s.dma_start(out=xr[i % 2][:, :],
                                       in_=x_prev_ap[i * 128:(i + 1) * 128, :]), XLD[i % 2], 16)
            b.sync(syncd)

            def ped(t, base=base):
                for i in range(NCH):
                    p = i % 2
                    if i == 0:
                        kb.wait(t, "LD7", base["LD7"] + 16 * NIT)
                    if i >= 2:
                        kb.wait(t, "DVE", base["DVE"] + 2 * (i - 1))
                    for hf in range(2):
                        kb.wait(t, LDS[hf], base[LDS[hf]] + 16 * (i + 1))
                        for kcl in range(43):
                            kc = hf * 43 + kcl
                            t.matmul(ps_d[p][0][:, :], dstr[hf][:, kcl * 128:(kcl + 1) * 128],
                                     hh0[:, kc * 512:(kc + 1) * 512],
                                     start=(kc == 0), stop=(kc == NIT - 1))
                            inst = t.matmul(ps_d[p][1][:, :], dstr[hf][:, kcl * 128:(kcl + 1) * 128],
                                            hh1sb(kc), start=(kc == 0), stop=(kc == NIT - 1))
                        kb.inc(inst, "PE")
            b.tensor(ped)

            def vecd(v, base=base):
                for i in range(NCH):
                    p = i % 2
                    kb.wait(v, "PE", base["PE"] + 2 * (i + 1))
                    kb.wait(v, XLD[p], base[XLD[p]] + 16 * (i // 2 + 1))
                    for tb in range(2):
                        sl = slice(tb * TB, (tb + 1) * TB)
                        kb.inc(v.scalar_tensor_tensor(xr[p][:, sl], ps_d[p][tb][:, :], 0.5,
                                                      xr[p][:, sl], ALU.mult, ALU.add), "DVE")
            b.vector(vecd)

            def std(g, base=base):
                for i in range(NCH):
                    kb.wait(g, "DVE", base["DVE"] + 2 * (i + 1))
                    kb.inc(g.dma_start(out=x_next_ap[i * 128:(i + 1) * 128, :],
                                       in_=xr[i % 2][:, :]), STS[i % 2], 16)
            b.gpsimd(std)
            kb.barrier()


def phase_oproj(kb, bufs, woT_dram, x_prev_ap, x_next_ap):
    nc, b = kb.nc, kb.block
    with ExitStack() as sc:
        xr = [sc.enter_context(nc.sbuf_tensor(kb.uname("op_xr"), [128, TPC], F32)) for _ in range(2)]
        xn = [sc.enter_context(nc.sbuf_tensor(kb.uname("op_xn"), [128, TPC], F32)) for _ in range(2)]
        base = dict(kb.cnt)
        XLD = ("LD8", "LD9")

        def extra_sync(s, i, evac_cnt):
            if i >= 2:
                kb.wait(s, "DVE", evac_cnt(i - 2))
            kb.inc(s.dma_start(out=xr[i % 2][:, :], in_=x_prev_ap[i * 128:(i + 1) * 128, :]),
                   XLD[i % 2], 16)

        def evac(v, nt, pspair):
            kb.wait(v, XLD[nt % 2], base[XLD[nt % 2]] + 16 * (nt // 2 + 1))
            for tb in range(2):
                sl = slice(tb * TB, (tb + 1) * TB)
                kb.inc(v.scalar_tensor_tensor(xn[nt % 2][:, sl], pspair[tb][:, :], 1.0,
                                              xr[nt % 2][:, sl], ALU.mult, ALU.add), "DVE")

        def store(g, nt):
            kb.inc(g.dma_start(out=x_next_ap[nt * 128:(nt + 1) * 128, :], in_=xn[nt % 2][:, :]),
                   ("ST2", "ST3")[nt % 2], 16)

        emit_stream_proj(kb, bufs, woT_dram, NCH, bufs["ln"], evac, store, 2, 1,
                         extra_sync_fn=extra_sync)
        kb.barrier()


def common_bufs(kb):
    bufs = {}
    bufs["ln"] = [kb.sb("ln%d" % i, [128, TPC], BF16) for i in range(NCH)]
    bufs["scal"] = kb.sb("scal", [1, 64], F32)
    bufs["ones_bf"] = kb.sb("ones_bf", [128, 1], BF16)
    bufs["ones_f1"] = kb.sb("ones_f1", [1, 128], F32)
    bufs["ident_bf"] = kb.sb("ident_bf", [128, 128], BF16)
    bufs["ident_f"] = kb.sb("ident_f", [128, 128], F32)
    bufs["E16_f"] = kb.sb("E16_f", [8, 128], F32)
    return bufs


def emit_const_init(kb, bufs, dram_consts):
    b = kb.block

    def vecc(v):
        v.memset(bufs["ones_bf"][:, :], 1.0)
        kb.inc(v.memset(bufs["ones_f1"][:, :], 1.0), "DVE")

    def syncc(s):
        kb.inc(s.dma_start(out=bufs["ident_bf"][:, :], in_=dram_consts["ident_bf"]), "LD1", 16)
        kb.inc(s.dma_start(out=bufs["ident_f"][:, :], in_=dram_consts["ident_f"]), "LD1", 16)
        kb.inc(s.dma_start(out=bufs["E16_f"][:, :], in_=dram_consts["E16_f"]), "LD1", 16)
    b.vector(vecc)
    b.sync(syncc)
    kb.barrier()


def const_arrays():
    e16 = np.zeros((8, 128), np.float32)
    for e in range(8):
        e16[e, e * 16:(e + 1) * 16] = 1.0
    return {
        "ident_bf": np.eye(128).astype(ml_dtypes.bfloat16),
        "ident_f": np.eye(128, dtype=np.float32),
        "E16_f": e16,
    }


def declare_consts(nc):
    arrs = const_arrays()
    return {k: nc.dram_tensor("const_" + k, list(v.shape),
                              BF16 if v.dtype == ml_dtypes.bfloat16 else F32,
                              kind="ExternalInput").ap() for k, v in arrs.items()}


def build_test_attn(causal):
    nc = bass.Bass("TRN2", target_bir_lowering=False)
    q = nc.dram_tensor("q", [D, TPC], BF16, kind="ExternalInput").ap()
    kv_g = nc.dram_tensor("kv_g", [4 * KVROWS, TPC], BF16, kind="ExternalInput").ap()
    qn = nc.dram_tensor("qn", [1, 1], F32, kind="ExternalInput").ap()
    m01 = nc.dram_tensor("m01", [SEQ, TPC], BF16, kind="ExternalInput").ap() if causal else None
    ctx_out = nc.dram_tensor("ctx_out", [D, TPC], BF16, kind="ExternalOutput").ap()
    consts = declare_consts(nc)
    with ExitStack() as ctx:
        kb = KB(nc, ctx)
        bufs = common_bufs(kb)
        emit_const_init(kb, bufs, consts)

        def ldqn(s):
            kb.inc(s.dma_start(out=bufs["scal"][0:1, 0:1], in_=qn), "LD1", 16)
        kb.block.sync(ldqn)
        kb.barrier()
        phase_attn(kb, bufs, q, kv_g, m01, causal)

        def store(g):
            kb.wait(g, "DVE", kb.cnt["DVE"])
            for i in range(NCH):
                kb.inc(g.dma_start(out=ctx_out[i * 128:(i + 1) * 128, :], in_=bufs["ln"][i][:, :]), "ST0", 16)
            kb.wait(g, "ST0", kb.cnt["ST0"])
        kb.block.gpsimd(store)
    return nc


def build_test_qkv(use_rope, with_norms=False):
    nc = bass.Bass("TRN2", target_bir_lowering=False)
    x = nc.dram_tensor("x", [D, TPC], F32, kind="ExternalInput").ap()
    qout = nc.dram_tensor("qout", [D, TPC], BF16, kind="ExternalOutput").ap()
    kvout = nc.dram_tensor("kvout", [KVROWS, TPC], BF16, kind="ExternalOutput").ap()
    wt = {k: nc.dram_tensor(k, shp, F32, kind="ExternalInput").ap()
          for k, shp in (("wqT", [NCH, 128, NCH, 128]), ("wkT", [8, 128, NCH, 128]),
                         ("wvT", [8, 128, NCH, 128]),
                         ("cos_fm", [128, TPC]), ("sinneg_fm", [128, TPC]))}
    consts = declare_consts(nc)
    with ExitStack() as ctx:
        kb = KB(nc, ctx)
        bufs = common_bufs(kb)
        emit_const_init(kb, bufs, consts)
        phase_ln(kb, bufs, x)
        if with_norms:
            st_rec = []
            pre = alloc_norm_bufs(kb, ctx)
            phase_qkv(kb, bufs, wt, use_rope, qout, kvout, st_rec=st_rec,
                      tail_fn=lambda: phase_qk_norms(kb, bufs, qout, kvout,
                                                     st_sched=st_rec, pre=pre))
        else:
            phase_qkv(kb, bufs, wt, use_rope, qout, kvout)
    return nc


def ref_rope_tables(tpos):
    DH = 128
    inv_freq = 1.0 / (500000.0 ** (np.arange(0, DH, 2, dtype=np.float32) / DH))
    wavelen = 2.0 * np.pi / inv_freq
    cut_lo = 8192 / 4.0
    cut_hi = 8192 / 1.0
    inv_freq = np.where(wavelen > cut_hi, inv_freq / 32.0, inv_freq)
    smooth = (8192 / wavelen - 1.0) / (4.0 - 1.0)
    smoothed = (1.0 - smooth) * inv_freq / 32.0 + smooth * inv_freq
    is_medium = ~(wavelen < cut_hi) & ~(wavelen > cut_lo)
    inv_freq = np.where(is_medium, smoothed.astype(np.float32), inv_freq).astype(np.float32)
    freqs = tpos[:, None].astype(np.float32) * inv_freq[None, :]
    return np.cos(freqs), np.sin(freqs)


def host_cos_fm(tpos):
    cos, sin = ref_rope_tables(tpos)
    cos_fm = np.concatenate([cos.T, cos.T], axis=0).astype(np.float32)
    sinneg_fm = np.concatenate([-sin.T, sin.T], axis=0).astype(np.float32)
    return np.ascontiguousarray(cos_fm), np.ascontiguousarray(sinneg_fm)



def _decl_in(nc, name, shape, dt):
    return nc.dram_tensor(name, list(shape), dt, kind="ExternalInput").ap()


def _decl_out(nc, name, shape, dt):
    return nc.dram_tensor(name, list(shape), dt, kind="ExternalOutput").ap()


def _decl_moe_wt(nc):
    return {"route_wT": _decl_in(nc, "route_wT", [D, 8], F32),
            "A_lhsT": _decl_in(nc, "A_lhsT", [D, 128], F32),
            "Bw_lhsT": _decl_in(nc, "Bw_lhsT", [128, D], F32)}


def _decl_gptq(nc):
    d = {}
    for m, N_, G_ in (("gate", INTER, 32), ("up", INTER, 32), ("down", D, 86)):
        K_ = D if m != "down" else INTER
        d[m + "_qwT"] = _decl_in(nc, m + "_qwT", [N_, K_ // 8], I32)
        d[m + "_sT"] = _decl_in(nc, m + "_sT", [N_, G_], F32)
        d[m + "_qz"] = _decl_in(nc, m + "_qz", [G_, N_ // 8], I32)
        d[m + "_bf"] = nc.dram_tensor(m + "_bf", [N_ // 128, 128, K_ // 128, 128], BF16, kind="Internal").ap()
    return d


def _emit_gptq_dequant_all(kb, bufs, gq, glists):
    for m, K_, N_ in (("gate", D, INTER), ("up", D, INTER), ("down", INTER, D)):
        phase_gptq_dequant_matrix(kb, bufs, gq[m + "_qwT"], gq[m + "_sT"],
                                  gq[m + "_qz"], gq[m + "_bf"], K_, N_, glists[m])


def build_part0():
    nc = bass.Bass("TRN2", target_bir_lowering=False)
    x0 = _decl_in(nc, "x0", [D, TPC], F32)
    x1 = _decl_out(nc, "x1", [D, TPC], F32)
    q0 = _decl_out(nc, "q0", [D, TPC], BF16)
    kvs0 = _decl_out(nc, "kvs0", [KVROWS, TPC], BF16)
    moe_wt = _decl_moe_wt(nc)
    qkv_wt = {"wqT": _decl_in(nc, "wqT", [NCH, 128, NCH, 128], F32),
              "wkT": _decl_in(nc, "wkT", [KVDIM // 128, 128, NCH, 128], F32),
              "wvT": _decl_in(nc, "wvT", [KVDIM // 128, 128, NCH, 128], F32)}
    consts = declare_consts(nc)
    with ExitStack() as ctx:
        kb = KB(nc, ctx)
        bufs = common_bufs(kb)
        emit_const_init(kb, bufs, consts)
        phase_ln(kb, bufs, x0)
        phase_moe(kb, bufs, moe_wt, x0, x1)
        phase_ln(kb, bufs, x1)
        phase_qkv(kb, bufs, qkv_wt, False, q0, kvs0)
        phase_qk_norms(kb, bufs, q0, kvs0)
    return nc


def build_part1(glists):
    nc = bass.Bass("TRN2", target_bir_lowering=False)
    x1 = _decl_in(nc, "x1", [D, TPC], F32)
    q0 = _decl_in(nc, "q0", [D, TPC], BF16)
    kv_g0 = _decl_in(nc, "kv_g0", [4 * KVROWS, TPC], BF16)
    x4 = _decl_out(nc, "x4", [D, TPC], F32)
    q1 = _decl_out(nc, "q1", [D, TPC], BF16)
    kvs1 = _decl_out(nc, "kvs1", [KVROWS, TPC], BF16)
    x2 = nc.dram_tensor("x2", [D, TPC], F32, kind="Internal").ap()
    x3 = nc.dram_tensor("x3", [D, TPC], F32, kind="Internal").ap()
    woT0 = _decl_in(nc, "woT0", [NCH, 128, NCH, 128], F32)
    moe_wt = _decl_moe_wt(nc)
    qkv_wt = {"wqT": _decl_in(nc, "wqT", [NCH, 128, NCH, 128], F32),
              "wkT": _decl_in(nc, "wkT", [KVDIM // 128, 128, NCH, 128], F32),
              "wvT": _decl_in(nc, "wvT", [KVDIM // 128, 128, NCH, 128], F32),
              "cos_fm": _decl_in(nc, "cos_fm", [128, TPC], F32),
              "sinneg_fm": _decl_in(nc, "sinneg_fm", [128, TPC], F32)}
    gq = _decl_gptq(nc)
    consts = declare_consts(nc)
    with ExitStack() as ctx:
        kb = KB(nc, ctx)
        bufs = common_bufs(kb)
        emit_const_init(kb, bufs, consts)
        phase_attn(kb, bufs, q0, kv_g0, None, False)
        phase_oproj(kb, bufs, woT0, x1, x2)
        phase_ln(kb, bufs, x2)
        phase_moe(kb, bufs, moe_wt, x2, x3)
        phase_ln(kb, bufs, x3)
        phase_gptq_ffn(kb, bufs, wg_fn, wu_fn, wd_fn, x3, x4, hh1_scr, cc2_wait=6)
        phase_ln(kb, bufs, x4)
        phase_qkv(kb, bufs, qkv_wt, True, q1, kvs1)
        phase_qk_norms(kb, bufs, q1, kvs1)
    return nc


def build_part2(glists):
    nc = bass.Bass("TRN2", target_bir_lowering=False)
    x4 = _decl_in(nc, "x4", [D, TPC], F32)
    q1 = _decl_in(nc, "q1", [D, TPC], BF16)
    kv_g1 = _decl_in(nc, "kv_g1", [4 * KVROWS, TPC], BF16)
    m01 = _decl_in(nc, "m01", [SEQ, TPC], BF16)
    xout = _decl_out(nc, "xout", [D, TPC], F32)
    x5 = nc.dram_tensor("x5", [D, TPC], F32, kind="Internal").ap()
    woT1 = _decl_in(nc, "woT1", [NCH, 128, NCH, 128], F32)
    gq = {}
    for m, G_ in (("gate", 32), ("up", 32), ("down", 86)):
        T = GATE_SLOTS if m != "down" else DOWN_SLOTS
        K_ = D if m != "down" else INTER
        N_ = INTER if m != "down" else D
        gq[m + "_qwT"] = _decl_in(nc, m + "_qwT", [T * 128, K_ // 8], I32)
        gq[m + "_sT"] = _decl_in(nc, m + "_sT", [T * 128, G_], F32)
        gq[m + "_qz"] = _decl_in(nc, m + "_qz", [G_, T * 16], I32)
    hh1_scr = nc.dram_tensor("hh1_scr", [NIT * 128, 512], BF16, kind="Internal").ap()
    deq_in_gu = nc.dram_tensor("deq_in_gu", [GU_BLOCK_ROWS, 128], BF16, kind="Internal")
    deq_out_gu = nc.dram_tensor("deq_out_gu", [8 * GU_BLOCK_ROWS, 128], BF16, kind="Internal", addr_space="Shared")
    deq_in_dn = nc.dram_tensor("deq_in_dn", [DN_BLOCK_ROWS, 128], BF16, kind="Internal")
    deq_out_dn = nc.dram_tensor("deq_out_dn", [8 * DN_BLOCK_ROWS, 128], BF16, kind="Internal", addr_space="Shared")
    consts = declare_consts(nc)
    with ExitStack() as ctx:
        kb = KB(nc, ctx)
        bufs = common_bufs(kb)
        emit_const_init(kb, bufs, consts)
        _emit_gptq_dequant_all(kb, bufs, gq, glists)
        phase_attn(kb, bufs, q1, kv_g1, m01, True)
        phase_oproj(kb, bufs, woT1, x4, x5)
        phase_ln(kb, bufs, x5)
        hh1_p = nc.dram_tensor("hh1_p", [NIT * 128, 512], BF16, kind="Internal").ap()
        phase_gptq_ffn(kb, bufs, gq["gate_bf"], gq["up_bf"], gq["down_bf"], x5, xout, hh1_p)
    return nc


def emit_kv_allgather(kb, kv_in, kv_g):
    groups = [[0, 1, 2, 3], [4, 5, 6, 7]]

    st_n = {ch: kb.cnt[ch] for ch in ("ST0", "ST2", "ST3")}

    def g_cc(g):
        for ch in sorted(st_n):
            kb.wait(g, ch, st_n[ch])
        kb.inc(g.collective_compute(
            "AllGather", ALU.bypass, replica_groups=groups,
            ins=[kv_in.ap().opt()], outs=[kv_g.ap().opt()]), "CC")
    kb.block.gpsimd(g_cc)
    cc_n = kb.cnt["CC"]

    def s_cc(s):
        kb.wait(s, "CC", cc_n)
    kb.block.sync(s_cc)
    kb.barrier()


def build_full(glists):
    nc = bass.Bass("TRN2", target_bir_lowering=False)
    x0 = _decl_in(nc, "x0", [D, TPC], F32)
    xout = _decl_out(nc, "xout", [D, TPC], F32)
    x1 = nc.dram_tensor("x1", [D, TPC], F32, kind="Internal").ap()
    x2 = nc.dram_tensor("x2", [D, TPC], F32, kind="Internal").ap()
    x3 = nc.dram_tensor("x3", [D, TPC], F32, kind="Internal").ap()
    x4 = nc.dram_tensor("x4", [D, TPC], F32, kind="Internal").ap()
    x5 = nc.dram_tensor("x5", [D, TPC], F32, kind="Internal").ap()
    q0 = nc.dram_tensor("q0", [D, TPC], BF16, kind="Internal").ap()
    q1 = nc.dram_tensor("q1", [D, TPC], BF16, kind="Internal").ap()
    kv_in0 = nc.dram_tensor("kv_in0", [KVROWS, TPC], BF16, kind="Internal")
    kv_g0 = nc.dram_tensor("kv_g0", [4 * KVROWS, TPC], BF16, kind="Internal")
    kv_in1 = nc.dram_tensor("kv_in1", [KVROWS, TPC], BF16, kind="Internal")
    kv_g1 = nc.dram_tensor("kv_g1", [4 * KVROWS, TPC], BF16, kind="Internal")
    m01 = _decl_in(nc, "m01", [SEQ, TPC], BF16)
    moe_wt = _decl_moe_wt(nc)
    a0 = {"wqT": _decl_in(nc, "a0_wqT", [NCH, 128, NCH, 128], F32),
          "wkT": _decl_in(nc, "a0_wkT", [KVDIM // 128, 128, NCH, 128], F32),
          "wvT": _decl_in(nc, "a0_wvT", [KVDIM // 128, 128, NCH, 128], F32)}
    woT0 = _decl_in(nc, "woT0", [NCH, 128, NCH, 128], F32)
    a1 = {"wqT": _decl_in(nc, "a1_wqT", [NCH, 128, NCH, 128], F32),
          "wkT": _decl_in(nc, "a1_wkT", [KVDIM // 128, 128, NCH, 128], F32),
          "wvT": _decl_in(nc, "a1_wvT", [KVDIM // 128, 128, NCH, 128], F32),
          "cos_fm": _decl_in(nc, "cos_fm", [128, TPC], F32),
          "sinneg_fm": _decl_in(nc, "sinneg_fm", [128, TPC], F32)}
    woT1 = _decl_in(nc, "woT1", [NCH, 128, NCH, 128], F32)
    gq = {}
    for m, G_ in (("gate", 32), ("up", 32), ("down", 86)):
        T = GATE_SLOTS if m != "down" else DOWN_SLOTS
        K_ = D if m != "down" else INTER
        N_ = INTER if m != "down" else D
        gq[m + "_qwT"] = _decl_in(nc, m + "_qwT", [T * 128, K_ // 8], I32)
        gq[m + "_sT"] = _decl_in(nc, m + "_sT", [T * 128, G_], F32)
        gq[m + "_qz"] = _decl_in(nc, m + "_qz", [G_, T * 16], I32)
    hh1_scr = nc.dram_tensor("hh1_scr", [NIT * 128, 512], BF16, kind="Internal").ap()
    deq_in_gu = nc.dram_tensor("deq_in_gu", [GU_BLOCK_ROWS, 128], BF16, kind="Internal")
    deq_out_gu = nc.dram_tensor("deq_out_gu", [8 * GU_BLOCK_ROWS, 128], BF16, kind="Internal", addr_space="Shared")
    deq_in_dn = nc.dram_tensor("deq_in_dn", [DN_BLOCK_ROWS, 128], BF16, kind="Internal")
    deq_out_dn = nc.dram_tensor("deq_out_dn", [8 * DN_BLOCK_ROWS, 128], BF16, kind="Internal", addr_space="Shared")
    consts = declare_consts(nc)
    with ExitStack() as ctx:
        kb = KB(nc, ctx)
        bufs = common_bufs(kb)
        emit_const_init(kb, bufs, consts)
        deq = DeqEmitter(kb, bufs)

        def deq_out4d(m):
            K_ = D if m != "down" else INTER
            T = GATE_SLOTS if m != "down" else DOWN_SLOTS
            trows = GATE_TILE_ROWS if m != "down" else DOWN_TILE_ROWS
            din = deq_in_dn.ap() if m == "down" else deq_in_gu.ap()
            sect = GATE_SLOTS * GATE_TILE_ROWS if m == "up" else 0
            return din[sect:sect + T * trows, :].rearrange(
                "(nt p c) n -> nt p c n", p=128, c=K_ // 128)
        for m in ("gate", "up"):
            deq.prep_matrix(m, gq[m + "_qwT"], gq[m + "_sT"], gq[m + "_qz"],
                            deq_out4d(m), D, GATE_SLOTS * 128, glists[m])
        deq.group_open(D, 32, [("gate", j) for j in range(GATE_SLOTS)]
                       + [("up", j) for j in range(GATE_SLOTS)])

        AG_CHUNKS = [("gate", 0, 6), ("gate", 6, 11), ("up", 0, 6), ("up", 6, 11),
                     ("down", 0, 2), ("down", 2, 4)]
        chunk_base = {}
        cb_gu = cb_dn = 0
        for ci, (m_, s0_, s1_) in enumerate(AG_CHUNKS):
            trows = DOWN_TILE_ROWS if m_ == "down" else GATE_TILE_ROWS
            rows = (s1_ - s0_) * trows
            if m_ == "down":
                chunk_base[ci] = cb_dn; cb_dn += 8 * rows
            else:
                chunk_base[ci] = cb_gu; cb_gu += 8 * rows
        assert cb_gu == 8 * GU_BLOCK_ROWS and cb_dn == 8 * DN_BLOCK_ROWS

        def emit_deq_ag(ci):
            m_, s0_, s1_ = AG_CHUNKS[ci]
            trows = DOWN_TILE_ROWS if m_ == "down" else GATE_TILE_ROWS
            rows = (s1_ - s0_) * trows
            if m_ == "down":
                in_sl = deq_in_dn.ap()[s0_ * trows:s1_ * trows, :]
                out_sl = deq_out_dn.ap()[chunk_base[ci]:chunk_base[ci] + 8 * rows, :]
            else:
                sect = GATE_SLOTS * GATE_TILE_ROWS if m_ == "up" else 0
                in_sl = deq_in_gu.ap()[sect + s0_ * trows:sect + s1_ * trows, :]
                out_sl = deq_out_gu.ap()[chunk_base[ci]:chunk_base[ci] + 8 * rows, :]

            st2_n, st3_n = kb.cnt["ST2"], kb.cnt["ST3"]

            def g_ag(g):
                kb.wait(g, "ST2", st2_n)
                kb.wait(g, "ST3", st3_n)
                kb.inc(g.collective_compute(
                    "AllGather", ALU.bypass, replica_groups=[list(range(8))],
                    ins=[in_sl.opt()], outs=[out_sl.opt()]), "CC2")
            kb.block.gpsimd(g_ag)
        dout_gu = deq_out_gu.ap()
        dout_dn = deq_out_dn.ap()

        def mk_wfn(m, K_):
            T = DOWN_SLOTS if m == "down" else GATE_SLOTS
            trows = DOWN_TILE_ROWS if m == "down" else GATE_TILE_ROWS

            def f(nt):
                r, j = nt // T, nt % T
                for ci, (m_, s0_, s1_) in enumerate(AG_CHUNKS):
                    if m_ == m and s0_ <= j < s1_:
                        base_row = chunk_base[ci] + r * (s1_ - s0_) * trows + (j - s0_) * trows
                        break
                dout = dout_dn if m == "down" else dout_gu
                return dout[base_row:base_row + 128 * (K_ // 128), :].rearrange(
                    "(p c) n -> p c n", p=128)
            return f
        wg_fn = mk_wfn("gate", D)
        wu_fn = mk_wfn("up", D)
        wd_fn = mk_wfn("down", INTER)

        phase_ln(kb, bufs, x0)
        deq.emit_tiles(4)
        phase_moe(kb, bufs, moe_wt, x0, x1)
        deq.emit_tiles(4)
        emit_deq_ag(0)
        phase_ln(kb, bufs, x1)
        deq.emit_tiles(3)
        emit_deq_ag(1)
        qkv_seams = [4, 4, 3]

        def qkv_seam():
            deq.emit_tiles(qkv_seams.pop(0))
        st_rec0 = []
        with ExitStack() as nsc0:
            pre0 = alloc_norm_bufs(kb, nsc0)
            phase_qkv(kb, bufs, a0, False, q0, kv_in0.ap(), seam_fn=qkv_seam, st_rec=st_rec0,
                      tail_fn=lambda: phase_qk_norms(kb, bufs, q0, kv_in0.ap(),
                                                     st_sched=st_rec0, pre=pre0))
        emit_deq_ag(2)
        emit_deq_ag(3)
        deq.group_close()
        deq.prep_matrix("down", gq["down_qwT"], gq["down_sT"], gq["down_qz"],
                        deq_out4d("down"), INTER, DOWN_SLOTS * 128, glists["down"])
        deq.group_open(INTER, 86, [("down", j) for j in range(DOWN_SLOTS)])
        deq.emit_tiles(4)
        deq.group_close()
        emit_kv_allgather(kb, kv_in0, kv_g0)
        emit_deq_ag(4)
        emit_deq_ag(5)
        phase_attn(kb, bufs, q0, kv_g0.ap(), None, False)
        phase_oproj(kb, bufs, woT0, x1, x2)
        phase_ln(kb, bufs, x2)
        phase_moe(kb, bufs, moe_wt, x2, x3)
        phase_ln(kb, bufs, x3)
        phase_gptq_ffn(kb, bufs, wg_fn, wu_fn, wd_fn, x3, x4, hh1_scr, cc2_wait=6)
        phase_ln(kb, bufs, x4)
        st_rec1 = []
        with ExitStack() as nsc1:
            pre1 = alloc_norm_bufs(kb, nsc1)
            phase_qkv(kb, bufs, a1, True, q1, kv_in1.ap(), st_rec=st_rec1,
                      tail_fn=lambda: phase_qk_norms(kb, bufs, q1, kv_in1.ap(),
                                                     st_sched=st_rec1, pre=pre1))
        emit_kv_allgather(kb, kv_in1, kv_g1)
        phase_attn(kb, bufs, q1, kv_g1.ap(), m01, True)
        phase_oproj(kb, bufs, woT1, x4, x5)
        phase_ln(kb, bufs, x5)
        phase_gptq_ffn(kb, bufs, wg_fn, wu_fn, wd_fn, x5, xout, hh1_scr)
    return nc


def kernel_fused(inputs):
    shared, a0, a1, moe, gq, glists, per_core = prep_host(inputs)
    gkey = tuple(tuple(glists[m]) for m in ("gate", "up", "down"))
    nc = _get_nc(("full", gkey), build_full, glists)
    maps = []
    for c in range(NCORES):
        m = {"x0": per_core[c]["x0"], "m01": per_core[c]["m01"],
             "cos_fm": per_core[c]["cos_fm"], "sinneg_fm": per_core[c]["sinneg_fm"],
             "woT0": a0["woT0"], "woT1": a1["woT1"], **moe, **shared}
        for mm in ("gate", "up", "down"):
            m[mm + "_qwT"] = per_core[c][mm + "_qwT"]
            m[mm + "_sT"] = per_core[c][mm + "_sT"]
            m[mm + "_qz"] = per_core[c][mm + "_qz"]
        for k in ("wqT", "wkT", "wvT"):
            m["a0_" + k] = a0[k]
            m["a1_" + k] = a1[k]
        maps.append(m)
    rr = run_bass_kernel_spmd(nc, maps, core_ids=list(range(NCORES))).results
    out = np.empty((NCORES * TPC, D), np.float32)
    for c in range(NCORES):
        nat = np.empty((D, TPC), np.float32)
        nat[PERM] = rr[c]["xout"]
        out[c * TPC:(c + 1) * TPC] = nat.T
    return out.reshape(2, SEQ, D)


def _glist_from(g_idx):
    gi = np.asarray(g_idx).reshape(-1, 128)
    assert (gi == gi[:, 0:1]).all(), "g_idx must be constant within 128-blocks"
    return [int(v) for v in gi[:, 0]]


def tile_wT(wT):
    Dd, N = wT.shape
    return np.ascontiguousarray(
        wT.reshape(Dd // 128, 128, N // 128, 128).transpose(2, 1, 0, 3))


def prep_host(inputs):
    asc = np.ascontiguousarray
    shared = dict(const_arrays())
    shared = {"const_" + k: v for k, v in shared.items()}

    def attn_w(L):
        return {"wqT": tile_wT(np.asarray(inputs[f"L{L}_attn_Wq"]).T[PERM]),
                "wkT": tile_wT(np.asarray(inputs[f"L{L}_attn_Wk"]).T[PERM]),
                "wvT": tile_wT(np.asarray(inputs[f"L{L}_attn_Wv"]).T[PERM]),
                "woT" + str(L): tile_wT(np.asarray(inputs[f"L{L}_attn_Wo"]).T[:, PERM])}

    a0 = attn_w(0)
    a1 = attn_w(1)
    moe = {"route_wT": asc(inputs["L0_ffn_route_w"].T[PERM]),
           "A_lhsT": asc(np.asarray(inputs["L0_ffn_A"]).reshape(128, D).T[PERM]),
           "Bw_lhsT": asc(np.asarray(inputs["L0_ffn_Bw"]).transpose(0, 2, 1).reshape(128, D)[:, PERM])}
    gq = {}
    glists = {}
    for m in ("gate", "up", "down"):
        P_N = PERM2 if m != "down" else PERM
        gq[m + "_qwT"] = asc(np.asarray(inputs[f"L1_ffn_{m}_qweight"]).T[P_N])
        gq[m + "_sT"] = asc(np.asarray(inputs[f"L1_ffn_{m}_scales"]).T[P_N])
        gq[m + "_qz"] = np.asarray(inputs[f"L1_ffn_{m}_qzeros"])
        glists[m] = _glist_from(inputs[f"L1_ffn_{m}_g_idx"])

    xf = np.asarray(inputs["L0_attn_x"]).reshape(NCORES * TPC, D)
    per_core = []
    for c in range(NCORES):
        p0 = (c % 4) * TPC
        cos_fm, sinneg_fm = host_cos_fm(np.arange(TPC) + p0)
        m01 = np.ascontiguousarray((np.arange(SEQ)[:, None] <= (p0 + np.arange(TPC))[None, :]).astype(ml_dtypes.bfloat16))
        pc = {
            "x0": asc(xf[c * TPC:(c + 1) * TPC].T[PERM]),
            "cos_fm": cos_fm, "sinneg_fm": sinneg_fm, "m01": m01,
        }
        for m in ("gate", "up", "down"):
            tiles = core_deq_tiles(c, m)
            for sfx in ("_qwT", "_sT"):
                full = gq[m + sfx]
                pc[m + sfx] = asc(full.reshape(full.shape[0] // 128, 128, -1)[tiles]
                                  .reshape(len(tiles) * 128, -1))
            qzf = gq[m + "_qz"]
            G_ = qzf.shape[0]
            pc[m + "_qz"] = asc(qzf.reshape(G_, -1, 16)[:, tiles].reshape(G_, -1))
        per_core.append(pc)
    return shared, a0, a1, moe, gq, glists, per_core


_BUILD_CACHE = {}


def _get_nc(key, builder, *args):
    if key not in _BUILD_CACHE:
        _BUILD_CACHE[key] = builder(*args)
    return _BUILD_CACHE[key]


def kernel(**inputs):
    return kernel_fused(inputs)


def kernel_unfused(**inputs):
    shared, a0, a1, moe, gq, glists, per_core = prep_host(inputs)
    gkey = tuple(tuple(glists[m]) for m in ("gate", "up", "down"))

    nc0 = _get_nc("p0", build_part0)
    maps0 = []
    for c in range(NCORES):
        m = {"x0": per_core[c]["x0"], **moe,
             "wqT": a0["wqT"], "wkT": a0["wkT"], "wvT": a0["wvT"], **shared}
        maps0.append(m)
    r0 = run_bass_kernel_spmd(nc0, maps0, core_ids=list(range(NCORES))).results

    def gather(rr, key):
        out = []
        for c in range(NCORES):
            grp = range(0, 4) if c < 4 else range(4, 8)
            out.append(np.concatenate([rr[g][key] for g in grp], axis=0))
        return out

    kvg0 = gather(r0, "kvs0")

    nc1 = _get_nc(("p1", gkey), build_part1, glists)
    maps1 = []
    for c in range(NCORES):
        m = {"x1": r0[c]["x1"], "q0": r0[c]["q0"], "kv_g0": kvg0[c],
             "woT0": a0["woT0"], **moe,
             "wqT": a1["wqT"], "wkT": a1["wkT"], "wvT": a1["wvT"],
             "cos_fm": per_core[c]["cos_fm"], "sinneg_fm": per_core[c]["sinneg_fm"],
             **gq, **shared}
        maps1.append(m)
    r1 = run_bass_kernel_spmd(nc1, maps1, core_ids=list(range(NCORES))).results
    kvg1 = gather(r1, "kvs1")

    nc2 = _get_nc(("p2", gkey), build_part2, glists)
    maps2 = []
    for c in range(NCORES):
        m = {"x4": r1[c]["x4"], "q1": r1[c]["q1"], "kv_g1": kvg1[c],
             "m01": per_core[c]["m01"], "woT1": a1["woT1"], **gq, **shared}
        maps2.append(m)
    r2 = run_bass_kernel_spmd(nc2, maps2, core_ids=list(range(NCORES))).results

    out = np.empty((NCORES * TPC, D), np.float32)
    for c in range(NCORES):
        nat = np.empty((D, TPC), np.float32)
        nat[PERM] = r2[c]["xout"]
        out[c * TPC:(c + 1) * TPC] = nat.T
    return out.reshape(2, SEQ, D)


def build_test_moe():
    nc = bass.Bass("TRN2", target_bir_lowering=False)
    x = nc.dram_tensor("x", [D, TPC], F32, kind="ExternalInput").ap()
    x1 = nc.dram_tensor("x1", [D, TPC], F32, kind="ExternalOutput").ap()
    wt = {k: nc.dram_tensor(k, shp, F32, kind="ExternalInput").ap()
          for k, shp in (("route_wT", [D, 8]), ("A_lhsT", [D, 128]), ("Bw_lhsT", [128, D]))}
    consts = declare_consts(nc)
    with ExitStack() as ctx:
        kb = KB(nc, ctx)
        bufs = common_bufs(kb)
        emit_const_init(kb, bufs, consts)
        phase_ln(kb, bufs, x)
        phase_moe(kb, bufs, wt, x, x1)
    return nc


if __name__ == "__main__":
    rng = np.random.default_rng(0)
    x = rng.standard_normal((D, TPC)).astype(np.float32)
    route_w = (rng.standard_normal((8, D)) / math.sqrt(D)).astype(np.float32)
    A = (rng.standard_normal((8, 16, D)) / math.sqrt(D)).astype(np.float32)
    Bw = (rng.standard_normal((8, D, 16)) * 0.02).astype(np.float32)

    wt_np = {
        "route_wT": np.ascontiguousarray(route_w.T),
        "A_lhsT": np.ascontiguousarray(A.reshape(128, D).T),
        "Bw_lhsT": np.ascontiguousarray(Bw.transpose(0, 2, 1).reshape(128, D)),
    }
    in_map = {"x": x, **wt_np, **const_arrays()}

    nc = build_test_moe()
    import os
    if os.environ.get("SIM"):
        from concourse import bass_interp
        sim = bass_interp.CoreSim(nc)
        for k, v in in_map.items():
            sim.tensor(k if k in ("x",) else ("const_" + k if k in const_arrays() else k))[:] = v
        sim.simulate()
        out = sim.tensor("x1").astype(np.float32)
    else:
        res = run_bass_kernel_spmd(nc, [{("const_" + k if k in const_arrays() else k): v
                                         for k, v in in_map.items()}], core_ids=[0])
        out = res.results[0]["x1"].astype(np.float32)

    xt = x.T
    mu = xt.mean(1, keepdims=True)
    ln = (xt - mu) / np.sqrt(xt.var(1, keepdims=True) + EPS)
    logits = ln @ route_w.T
    mx = logits.max(1, keepdims=True)
    rw = (logits == mx) * mx
    h = ln @ A.reshape(128, D).T
    wh = h * np.repeat(rw, 16, axis=1)
    o = wh @ Bw.transpose(0, 2, 1).reshape(128, D)
    ref = x + 0.5 * o.T
    rel = np.linalg.norm(out - ref) / np.linalg.norm(ref)
    print("MoE rel l2 err:", rel, "max abs:", np.abs(out - ref).max())
